# version 05
# baseline (speedup 1.0000x reference)
import os
import numpy as np
import ml_dtypes
from contextlib import ExitStack

import concourse.bass as bass
import concourse.mybir as mybir
from concourse.bass_utils import run_bass_kernel_spmd

F32 = mybir.dt.float32
F16 = mybir.dt.float16
BF16 = mybir.dt.bfloat16
I32 = mybir.dt.int32
U16 = mybir.dt.uint16
AF = mybir.ActivationFunctionType
ALU = mybir.AluOpType
AXX = mybir.AxisListType.X
NBF = ml_dtypes.bfloat16
f32 = np.float32

D = 2048
S = 4096
OWN = 2048
GU = 11264
II = 5632
NCORES = 8
INVN = 1.0 / D


def _ktlist(qc):
    return list(range(0, 4 * qc + 4)) + list(range(16, 32))


class Ctr:
    def __init__(self, sem):
        self.sem = sem
        self.n = 0


class Prog:
    def __init__(self, nc, stack):
        self.nc = nc
        self.stack = stack
        self.q = {e: [] for e in ("sync", "gpsimd", "tensor", "vector", "scalar")}
        self.ctr = {}
        for name in ("tS", "vS", "sS"):
            self.ctr[name] = Ctr(stack.enter_context(nc.semaphore(name)))
        self.guards = {}

    def _c(self, name):
        if name not in self.ctr:
            self.ctr[name] = Ctr(self.stack.enter_context(self.nc.semaphore(name)))
        return self.ctr[name]

    def opb(self, eng, fn, inc):
        v = self.op(eng, fn, inc=inc)
        self.wait(eng, inc, v)
        return v

    def op(self, eng, fn, inc=None, k=1):
        if inc is not None:
            c = self._c(inc)
            c.n += k
            sem = c.sem
            self.q[eng].append(lambda e, fn=fn, sem=sem, k=k: fn(e).then_inc(sem, k))
            return c.n
        self.q[eng].append(fn)
        return None

    def dma(self, eng, stream, outp, inp, **kw):
        return self.op(eng, lambda e, o=outp, i=inp, kw=kw: e.dma_start(out=o, in_=i, **kw),
                       inc=stream, k=16)

    def wait(self, eng, cname, v):
        if not v or v <= 0:
            return
        assert v <= self._c(cname).n, f"forward wait {cname} {v} > {self.ctr[cname].n}"
        sem = self.ctr[cname].sem
        self.q[eng].append(lambda e, sem=sem, v=v: e.wait_ge(sem, v))

    def waitall(self, eng, cname):
        self.wait(eng, cname, self._c(cname).n)

    def guard(self, eng, key):
        g = self.guards.get(key)
        if g is not None:
            self.wait(eng, g[0], g[1])

    def setg(self, key, cname, v):
        self.guards[key] = (cname, v)


def build_nc(limit=99):
    nc = bass.Bass("TRN2", target_bir_lowering=False)
    stack = ExitStack()

    def din(name, shape, dt):
        return nc.dram_tensor(name, list(shape), dt, kind="ExternalInput").ap()

    xin = din("xin", [S, D], F32)
    wqkv = din("wqkv", [D, 3 * D], F32)
    bqkv = din("bqkv", [1, 3 * D], F32)
    wout = din("wout", [D, D], F32)
    boutI = din("bout", [1, D], F32)
    pk_gu = din("pk_gu", [GU, D // 8], I32)
    pk_dn = din("pk_dn", [D, II // 8], I32)
    scT_gu = din("scT_gu", [GU, 16], F16)
    scT_dn = din("scT_dn", [D, 44], F16)
    ktab = din("ktab", [S, 1024], F32)
    qtab = din("qtab", [OWN, 1024], F32)
    umask = din("umask", [4, 128, 512], F32)
    bcolI = din("bcol", [128, 4 * 32], F32)
    onesr = din("onesr", [1, 128], F32)
    lmaskI = din("lmask", [128, 128], F32)

    out = nc.dram_tensor("out", [OWN, D], F32, kind="ExternalOutput").ap()

    d_ln1 = nc.dram_tensor("d_ln1", [S, D], BF16).ap()
    d_k = nc.dram_tensor("d_k", [S, D], BF16).ap()
    d_q = nc.dram_tensor("d_q", [OWN, D], BF16).ap()
    d_v = nc.dram_tensor("d_v", [S, D], BF16).ap()
    d_ctx = nc.dram_tensor("d_ctx", [OWN, D], BF16).ap()
    d_x2 = nc.dram_tensor("d_x2", [OWN, D], F32).ap()
    d_ln2 = nc.dram_tensor("d_ln2", [OWN, D], BF16).ap()
    d_wgu = nc.dram_tensor("d_wgu", [GU, D], BF16).ap()
    d_wdn = nc.dram_tensor("d_wdn", [D, II], BF16).ap()

    P = Prog(nc, stack)
    op, dma, wait, guard, setg = P.op, P.dma, P.wait, P.guard, P.setg
    opb, waitall = P.opb, P.waitall
    FINALS = {}

    class _Stop(Exception):
        pass

    def ckpt(n, pairs):
        FINALS["last"] = pairs
        if (n >= 40 and limit == n) or (n < 40 and limit <= n):
            raise _Stop()

    def sb(name, shape, dt):
        return stack.enter_context(nc.sbuf_tensor(name, list(shape), dt))

    AR1 = sb("AR1", [128, 33792], BF16)
    AR2 = sb("AR2", [128, 16384], BF16)
    AR3 = sb("AR3", [128, 16384], BF16)
    AR4 = sb("AR4", [128, 10240], F32)
    s_va = [sb(f"s_va{i}", [128, 32 * 129], BF16).ap() for i in range(2)]
    s_sil = sb("s_sil", [128, 512], F32)
    s_wd = [sb(f"s_wd{i}", [128, 512], BF16).ap() for i in range(2)]
    s_ctxb = [sb(f"s_ctx{i}", [128, 128], BF16).ap() for i in range(2)]
    s_cols = sb("s_cols", [128, 48], F32)
    s_rc = sb("s_rc", [128, 2], F32)
    c_bqg = sb("c_bqg", [1, 1024], BF16)
    c_bout = sb("c_bout", [1, D], BF16)
    c_lmask = sb("c_lmask", [128, 128], BF16)
    c_bcol = sb("c_bcol", [128, 128], F32)
    c_ones = sb("c_ones", [1, 128], BF16)
    c_scgu = sb("c_scgu", [128, 88 * 16], F32)
    c_scdn = sb("c_scdn", [128, 16 * 44], F32)

    A1, A2, A3, A4 = AR1.ap(), AR2.ap(), AR3.ap(), AR4.ap()

    psum = [stack.enter_context(nc.psum_tensor(f"ps{i}", [128, 1024], F32)).ap()
            for i in range(4)]

    def _phases():
        dma("gpsimd", "gC", c_ones[:], onesr)
        dma("gpsimd", "gC", c_bout[:], boutI)
        dma("gpsimd", "gC", c_lmask[:], lmaskI)
        dma("sync", "sC", c_bcol[:], bcolI)
        sg16 = A1.bitcast(F16)
        dma("sync", "sC", sg16[:, 0:88 * 16].rearrange("p (c g) -> p c g", g=16),
            scT_gu.rearrange("(c p) g -> p c g", p=128))
        v = dma("sync", "sC", sg16[:, 2048:2048 + 16 * 44].rearrange("p (c g) -> p c g", g=44),
                scT_dn.rearrange("(c p) g -> p c g", p=128))
        wait("vector", "sC", v)
        op("vector", lambda e: e.tensor_copy(c_scgu[:], sg16[:, 0:88 * 16]), inc="vS")
        scv = op("vector", lambda e: e.tensor_copy(c_scdn[:], sg16[:, 2048:2048 + 16 * 44]),
                 inc="vS")
        setg("AR1", "vS", scv)

        sx = [A4[:, 2048 * i:2048 * (i + 1)] for i in range(2)]
        sjunk = A4[:, 2048 * 2:2048 * 3]
        slo = [A2[:, 2048 * i:2048 * (i + 1)] for i in range(2)]

        def ln_stats(src, mcol, sdcol, sqbias=True, junk=None):
            pass

        p1_store = []
        for t in range(S // 128):
            b = t % 2
            guard("sync", f"p1x{b}")
            xl = dma("sync", f"x{b}", sx[b], xin[128 * t:128 * (t + 1), :])
            wait("vector", f"x{b}", xl)
            guard("vector", "p1cols")
            opb("vector", lambda e, b=b: e.reduce_sum(s_cols[:, 0:1], sx[b], axis=AXX),
                "vS")
            v = op("vector", lambda e: e.tensor_scalar_mul(
                s_cols[:, 1:2], s_cols[:, 0:1], -INVN), inc="vS")
            wait("scalar", "vS", v)
            opb("scalar", lambda e, b=b: e.activation(
                sjunk, sx[b], AF.Square, bias=s_cols[:, 1:2], accum_out=s_cols[:, 2:3]),
                "sS")
            v = op("scalar", lambda e: e.activation(
                s_cols[:, 3:4], s_cols[:, 2:3], AF.Sqrt,
                scale=f32(INVN), bias=f32(1e-5)), inc="sS")
            wait("vector", "sS", v)
            opb("vector", lambda e: e.reciprocal(s_cols[:, 4:5], s_cols[:, 3:4]), "vS")
            opb("vector", lambda e: e.tensor_tensor(
                s_cols[:, 5:6], s_cols[:, 1:2], s_cols[:, 4:5], ALU.mult), "vS")
            guard("vector", f"p1lo{b}")
            v = op("vector", lambda e, b=b: e.tensor_scalar(
                slo[b], sx[b], s_cols[:, 4:5], s_cols[:, 5:6],
                op0=ALU.mult, op1=ALU.add), inc="vS")
            setg(f"p1x{b}", "vS", v)
            setg("p1cols", "sS", P.ctr["sS"].n)
            wait("sync", "vS", v)
            p1_store.append(dma("sync", f"lo{b}", d_ln1[128 * t:128 * (t + 1), :], slo[b]))
            setg(f"p1lo{b}", f"lo{b}", p1_store[-1])
        ckpt(1, [("lo0", P.ctr["lo0"].n), ("lo1", P.ctr["lo1"].n)])

        lnq = A1[:, 0:16384]
        wg = A1[:, 16384:32768]
        qk = [A2[:, 1024 * i:1024 * (i + 1)] for i in range(2)]
        rts = [A4[:, 1024 * i:1024 * (i + 1)] for i in range(2)]
        rp = A4[:, 1024 * 2:1024 * 4]

        wait("gpsimd", "vS", scv)
        groups = [("k", D, 4, ktab, d_k, 0), ("k", D + 1024, 4, ktab, d_k, 1024),
                  ("q", 0, 2, qtab, d_q, 0), ("q", 1024, 2, qtab, d_q, 1024),
                  ("v", 2 * D, 4, None, d_v, 0), ("v", 2 * D + 1024, 4, None, d_v, 1024)]
        for gname, colbase, nq, rtab, dst, dcol in groups:
            guard("gpsimd", "wg")
            guard("gpsimd", "bqg")
            for c in range(16):
                wgv = dma("gpsimd", "wg", wg[:, 1024 * c:1024 * (c + 1)],
                          wqkv[128 * c:128 * (c + 1), colbase:colbase + 1024])
            bqv = dma("gpsimd", "wg", c_bqg[:], bqkv[0:1, colbase:colbase + 1024])
            for qtr in range(nq):
                guard("sync", "lnq")
                if gname == "k" and colbase == D and qtr == 0:
                    waitall("sync", "lo0")
                    waitall("sync", "lo1")
                for c in range(16):
                    lnv = dma("sync", "slnq", lnq[:, 1024 * c:1024 * (c + 1)],
                              d_ln1[1024 * qtr:1024 * (qtr + 1), 128 * c:128 * (c + 1)],
                              transpose=True)
                for tt in range(8):
                    at = 8 * qtr + tt
                    bb = at % 2
                    if rtab is not None:
                        guard("sync", f"rt{bb}")
                        rtv = dma("sync", f"rt{bb}", rts[bb], rtab[128 * at:128 * (at + 1), :])
                    guard_qk_done = False
                    for nh2 in range(2):
                        pidx = (2 * at + nh2) % 4
                        ps = psum[pidx][:, 0:512]
                        guard("tensor", f"ps{pidx}")
                        if tt == 0 and nh2 == 0:
                            wait("tensor", "wg", bqv)
                            wait("tensor", "slnq", lnv)
                            if gname == "k" and colbase == D and qtr == 0 and tt == 0:
                                waitall("tensor", "gC")
                        for c in range(16):
                            op("tensor", lambda e, ps=ps, c=c, tt=tt, nh2=nh2: e.matmul(
                                ps, lnq[:, 1024 * c + 128 * tt:1024 * c + 128 * (tt + 1)],
                                wg[:, 1024 * c + 512 * nh2:1024 * c + 512 * (nh2 + 1)],
                                start=(c == 0), stop=False))
                        pe_v = op("tensor", lambda e, ps=ps, nh2=nh2: e.matmul(
                            ps, c_ones[:], c_bqg[0:1, 512 * nh2:512 * (nh2 + 1)],
                            start=False, stop=True), inc="tS")
                        if rtab is None:
                            wait("scalar", "tS", pe_v)
                            if not guard_qk_done:
                                guard("scalar", f"qk{bb}")
                            ev = op("scalar", lambda e, ps=ps, bb=bb, nh2=nh2: e.activation(
                                qk[bb][:, 512 * nh2:512 * (nh2 + 1)], ps, AF.Copy),
                                inc="sS")
                            setg(f"ps{pidx}", "sS", ev)
                            ev_c = "sS"
                        else:
                            wait("vector", "tS", pe_v)
                            if nh2 == 0:
                                wait("vector", f"rt{bb}", rtv)
                            if not guard_qk_done:
                                guard("vector", f"qk{bb}")
                            ps3 = ps.rearrange("p (h x d) -> p h x d", h=4, x=2)
                            sc3 = rts[bb][:, 512:1024].rearrange(
                                "p (h x d) -> p h x d", h=4, x=2)
                            rp3 = rp[:, 512:1024].rearrange(
                                "p (h x d) -> p h x d", h=4, x=2)
                            op("vector", lambda e, ps=ps, bb=bb: e.tensor_tensor(
                                rp[:, 0:512], ps, rts[bb][:, 0:512], ALU.mult), inc="vS")
                            op("vector", lambda e, ps3=ps3, sc3=sc3, rp3=rp3: e.tensor_tensor(
                                rp3[:, :, 0, :], ps3[:, :, 1, :], sc3[:, :, 0, :], ALU.mult),
                                inc="vS")
                            v3b = op("vector", lambda e, ps3=ps3, sc3=sc3, rp3=rp3: e.tensor_tensor(
                                rp3[:, :, 1, :], ps3[:, :, 0, :], sc3[:, :, 1, :], ALU.mult),
                                inc="vS")
                            wait("vector", "vS", v3b)
                            ev = op("vector", lambda e, bb=bb, nh2=nh2: e.tensor_tensor(
                                qk[bb][:, 512 * nh2:512 * (nh2 + 1)],
                                rp[:, 0:512], rp[:, 512:1024], ALU.add), inc="vS")
                            wait("vector", "vS", ev)
                            setg(f"ps{pidx}", "vS", ev)
                            ev_c = "vS"
                        guard_qk_done = True
                    if rtab is not None:
                        setg(f"rt{bb}", "vS", ev)
                    wait("sync", ev_c, ev)
                    stv = dma("sync", f"qs{bb}", dst[128 * at:128 * (at + 1), dcol:dcol + 1024], qk[bb])
                    setg(f"qk{bb}", f"qs{bb}", stv)
                setg("lnq", "tS", P.ctr["tS"].n)
            setg("wg", "tS", P.ctr["tS"].n)
            setg("bqg", "tS", P.ctr["tS"].n)
        p2_tS = P.ctr["tS"].n
        ckpt(2, [("qs0", P.ctr["qs0"].n), ("qs1", P.ctr["qs1"].n)])
        setg("AR1", "tS", p2_tS)

        pko = [0, 704]
        mago, sgno, rawo, fixo, mg2o, wqo = 1408, 4224, 7040, 9856, 12672, 15488
        u16v = A1.bitcast(U16)
        i32v = A1.bitcast(I32)
        dq_units = [("gu", cc, 0) for cc in range(88)] + \
                   [("dn", cc, hf) for cc in range(16) for hf in range(2)]
        dq_state = {"i": 0, "spill": [], "first": True}

        def pump_dq():
            i = dq_state["i"]
            if i >= len(dq_units):
                return
            dq_state["i"] += 1
            which, cc, hf = dq_units[i]
            b = i % 2
            nu = 256 if which == "gu" else 352
            nn = nu * 8
            if dq_state["first"]:
                wait("gpsimd", "tS", p2_tS)
                guard("gpsimd", "AR1")
                dq_state["first"] = False
            guard("gpsimd", f"dqpk{b}")
            if which == "gu":
                src = pk_gu[128 * cc:128 * (cc + 1), :]
            else:
                src = pk_dn[128 * cc:128 * (cc + 1), 352 * hf:352 * (hf + 1)]
            ldv = dma("gpsimd", f"pk{b}", i32v[:, pko[b] // 2:pko[b] // 2 + nu], src)
            wait("vector", f"pk{b}", ldv)
            guard("vector", "dqbuf")
            u = u16v[:, pko[b]:pko[b] + 2 * nu]
            for j in range(4):
                sh = 6 - 4 * j
                if sh >= 0:
                    op("vector", lambda e, j=j, u=u, sh=sh, nn=nn: e.tensor_scalar(
                        u16v[:, mago + j:mago + nn:4], u, sh, 0x1C0,
                        op0=ALU.logical_shift_left, op1=ALU.bitwise_and), inc="vS")
                else:
                    op("vector", lambda e, j=j, u=u, sh=sh, nn=nn: e.tensor_scalar(
                        u16v[:, mago + j:mago + nn:4], u, -sh, 0x1C0,
                        op0=ALU.logical_shift_right, op1=ALU.bitwise_and), inc="vS")
                pv8 = op("vector", lambda e, j=j, u=u, nn=nn: e.tensor_scalar(
                    u16v[:, sgno + j:sgno + nn:4], u, 8 << (4 * j), 12 - 4 * j,
                    op0=ALU.bitwise_and, op1=ALU.logical_shift_left), inc="vS")
            wait("vector", "vS", pv8)
            v = op("vector", lambda e, nn=nn: e.tensor_single_scalar(
                u16v[:, rawo:rawo + nn], u16v[:, mago:mago + nn], 0x3F00, ALU.add),
                inc="vS")
            wait("scalar", "vS", v)
            v = op("scalar", lambda e, nn=nn: e.activation(
                A1[:, fixo:fixo + nn], A1[:, rawo:rawo + nn], AF.Relu,
                bias=f32(1.0), scale=f32(-1.0)), inc="sS")
            wait("vector", "sS", v)
            vsub = op("vector", lambda e, nn=nn: e.tensor_tensor(
                A1[:, mg2o:mg2o + nn], A1[:, rawo:rawo + nn],
                A1[:, fixo:fixo + nn], ALU.subtract), inc="vS")
            wait("vector", "vS", vsub)
            guard("vector", f"dqwq{b}")
            wqb = wqo + 2816 * b
            vor2 = op("vector", lambda e, nn=nn, wqb=wqb: e.tensor_tensor(
                u16v[:, wqb:wqb + nn], u16v[:, mg2o:mg2o + nn],
                u16v[:, sgno:sgno + nn], ALU.bitwise_or), inc="vS")
            wait("vector", "vS", vor2)
            if which == "gu":
                sc, scb = c_scgu, 16 * cc
            else:
                sc, scb = c_scdn, 44 * cc + 22 * hf
            ngr = nn // 128
            for g in range(ngr):
                v = op("vector", lambda e, g=g, wqb=wqb, sc=sc, scb=scb: e.tensor_scalar_mul(
                    A1[:, wqb + 128 * g:wqb + 128 * (g + 1)],
                    A1[:, wqb + 128 * g:wqb + 128 * (g + 1)],
                    sc[:, scb + g:scb + g + 1]), inc="vS" if g == ngr - 1 else None)
            setg(f"dqpk{b}", "vS", v)
            setg("dqbuf", "vS", v)
            wait("gpsimd", "vS", v)
            if which == "gu":
                dstp = d_wgu[128 * cc:128 * (cc + 1), :]
            else:
                dstp = d_wdn[128 * cc:128 * (cc + 1), 2816 * hf:2816 * (hf + 1)]
            sv = dma("gpsimd", f"wq{b}", dstp, A1[:, wqb:wqb + nn])
            dq_state["spill"].append(sv)
            setg(f"dqwq{b}", f"wq{b}", sv)

        kh = [A3[:, 4096 * i:4096 * (i + 1)] for i in range(2)]
        qh = [A3[:, 8192 + 2048 * i:8192 + 2048 * (i + 1)] for i in range(2)]
        um = A3[:, 12288:14336]
        pt = A2

        mz = None
        for i in range(2):
            mz = op("vector", lambda e, i=i: e.memset(
                s_va[i].rearrange("p (c v) -> p c v", c=32)[:, :, 128:129], 1.0),
                inc="vS")
        wait("gpsimd", "tS", p2_tS)
        umv = dma("gpsimd", "um", um.rearrange("p (a q) -> p a q", a=4),
                  umask.rearrange("a p q -> p a q"))
        waitall("scalar", "qs0")
        waitall("scalar", "qs1")
        waitall("scalar", "lo0")
        waitall("scalar", "lo1")

        for hh in range(16):
            b = hh % 2
            guard("sync", f"kh{b}")
            dma("sync", f"kq{b}", kh[b], d_k[:, 128 * hh:128 * (hh + 1)], transpose=True)
            kqv = dma("sync", f"kq{b}", qh[b], d_q[:, 128 * hh:128 * (hh + 1)], transpose=True)
            guard("gpsimd", f"kh{b}")
            vav = dma("gpsimd", f"va{b}", s_va[b].rearrange("p (c v) -> p c v", c=32)[:, :, 0:128],
                      d_v.rearrange("(c p) v -> p c v", p=128)[:, :, 128 * hh:128 * (hh + 1)])
            wait("tensor", f"kq{b}", kqv)
            wait("tensor", f"va{b}", vav)
            if hh == 0:
                wait("tensor", "vS", mz)
                wait("tensor", "um", umv)
            for qc in range(4):
                ktl = _ktlist(qc)
                exp_last = None
                for idx, kt in enumerate(ktl):
                    pidx = idx % 2
                    ps = psum[pidx][:, 0:512]
                    guard("tensor", f"ps{pidx}")
                    diag = 4 * qc <= kt < 4 * qc + 4
                    if diag:
                        op("tensor", lambda e, ps=ps, b=b, kt=kt, qc=qc: e.matmul(
                            ps, kh[b][:, 128 * kt:128 * (kt + 1)],
                            qh[b][:, 512 * qc:512 * (qc + 1)], start=True, stop=False))
                        so = kt - 4 * qc
                        pe_v = op("tensor", lambda e, ps=ps, so=so: e.matmul(
                            ps, c_lmask[:], um[:, 512 * so:512 * (so + 1)],
                            start=False, stop=True), inc="tS")
                    else:
                        pe_v = op("tensor", lambda e, ps=ps, b=b, kt=kt, qc=qc: e.matmul(
                            ps, kh[b][:, 128 * kt:128 * (kt + 1)],
                            qh[b][:, 512 * qc:512 * (qc + 1)], start=True, stop=True),
                            inc="tS")
                    wait("scalar", "tS", pe_v)
                    if idx == 0:
                        guard("scalar", "pt")
                    exp_last = op("scalar", lambda e, ps=ps, idx=idx, qc=qc: e.activation(
                        pt[:, 512 * idx:512 * (idx + 1)], ps, AF.Exp,
                        bias=c_bcol[:, 32 * qc + idx:32 * qc + idx + 1]), inc="sS")
                    setg(f"ps{pidx}", "sS", exp_last)
                wait("tensor", "sS", exp_last)
                for qt in range(4):
                    pvidx = 2 + qt % 2
                    pv = psum[pvidx][:, 0:129]
                    guard("tensor", f"ps{pvidx}")
                    pv_v = None
                    for idx, kt in enumerate(ktl):
                        last = (idx == len(ktl) - 1)
                        fn = (lambda e, pv=pv, idx=idx, qt=qt, kt=kt, b=b, last=last:
                              e.matmul(
                            pv, pt[:, 512 * idx + 128 * qt:512 * idx + 128 * (qt + 1)],
                            s_va[b][:, 129 * kt:129 * (kt + 1)],
                            start=(idx == 0), stop=last))
                        pv_v = op("tensor", fn, inc="tS" if last else None)
                    wait("vector", "tS", pv_v)
                    cb = (4 * qc + qt) % 2
                    guard("vector", f"cx{cb}")
                    opb("vector", lambda e, pv=pv: e.reciprocal(s_rc[:, 0:1], pv[:, 128:129]), "vS")
                    v = op("vector", lambda e, pv=pv, cb=cb: e.tensor_scalar_mul(
                        s_ctxb[cb][:], pv[:, 0:128], s_rc[:, 0:1]), inc="vS")
                    wait("vector", "vS", v)
                    setg(f"ps{pvidx}", "vS", v)
                    wait("sync", "vS", v)
                    r0 = 512 * qc + 128 * qt
                    cxv = dma("sync", f"cx{cb}", d_ctx[r0:r0 + 128, 128 * hh:128 * (hh + 1)],
                              s_ctxb[cb])
                    setg(f"cx{cb}", f"cx{cb}", cxv)
                setg("pt", "tS", P.ctr["tS"].n)
            setg(f"kh{b}", "tS", P.ctr["tS"].n)
            for _ in range(8):
                pump_dq()
        while dq_state["i"] < len(dq_units):
            pump_dq()
        p5_vS = P.ctr["vS"].n
        ckpt(3, [("wq0", P.ctr["wq0"].n), ("wq1", P.ctr["wq1"].n),
                 ("cx0", P.ctr["cx0"].n), ("cx1", P.ctr["cx1"].n)])

        cfm = A1[:, 0:16384]
        wo = A1[:, 16384:32768]
        aost = A2
        sx4 = [A4[:, 2048 * i:2048 * (i + 1)] for i in range(2)]
        sj4 = A4[:, 2048 * 2:2048 * 3]
        x2t = [A4[:, 2048 * (3 + i):2048 * (4 + i)] for i in range(2)]

        for th in range(2):
            guard("sync", "AR1cfm")
            if th == 0:
                waitall("sync", "wq0")
                waitall("sync", "wq1")
                wait("sync", "vS", p5_vS)
                waitall("sync", "cx0")
                waitall("sync", "cx1")
            for c in range(16):
                cfmv = dma("sync", "cfm", cfm[:, 1024 * c:1024 * (c + 1)],
                           d_ctx[1024 * th:1024 * (th + 1), 128 * c:128 * (c + 1)],
                           transpose=True)
            for oh in range(2):
                if th == 1 and oh == 0:
                    waitall("scalar", "xl20")
                    waitall("scalar", "xl21")
                    wait("scalar", "vS", P.ctr["vS"].n)
                guard("gpsimd", "wo")
                if th == 0 and oh == 0:
                    waitall("gpsimd", "wq0")
                    waitall("gpsimd", "wq1")
                    wait("gpsimd", "vS", p5_vS)
                for c in range(16):
                    wov = dma("gpsimd", "wo", wo[:, 1024 * c:1024 * (c + 1)],
                              wout[128 * c:128 * (c + 1), 1024 * oh:1024 * (oh + 1)])
                for t8 in range(8):
                    for oc2 in range(2):
                        pidx = (2 * t8 + oc2) % 4
                        ps = psum[pidx][:, 0:512]
                        guard("tensor", f"ps{pidx}")
                        if t8 == 0 and oc2 == 0:
                            wait("tensor", "cfm", cfmv)
                            wait("tensor", "wo", wov)
                        for c in range(16):
                            op("tensor", lambda e, ps=ps, c=c, t8=t8, oh=oh, oc2=oc2: e.matmul(
                                ps, cfm[:, 1024 * c + 128 * t8:1024 * c + 128 * (t8 + 1)],
                                wo[:, 1024 * c + 512 * oc2:1024 * c + 512 * (oc2 + 1)],
                                start=(c == 0), stop=False))
                        pe_v = op("tensor", lambda e, ps=ps, oh=oh, oc2=oc2: e.matmul(
                            ps, c_ones[:], c_bout[0:1, 1024 * oh + 512 * oc2:
                                                  1024 * oh + 512 * (oc2 + 1)],
                            start=False, stop=True), inc="tS")
                        wait("scalar", "tS", pe_v)
                        if th == 0 and oh == 0 and t8 == 0 and oc2 == 0:
                            guard("scalar", "pt")
                        ev = op("scalar", lambda e, ps=ps, t8=t8, oh=oh, oc2=oc2: e.activation(
                            aost[:, 2048 * t8 + 1024 * oh + 512 * oc2:
                                 2048 * t8 + 1024 * oh + 512 * (oc2 + 1)], ps, AF.Copy),
                            inc="sS")
                        setg(f"ps{pidx}", "sS", ev)
                setg("wo", "tS", P.ctr["tS"].n)
            aost_ready = ev
            wait("vector", "sS", aost_ready)
            for t8 in range(8):
                t = 8 * th + t8
                b2 = t8 % 2
                ao = aost[:, 2048 * t8:2048 * (t8 + 1)]
                guard("sync", f"p4x{b2}")
                xl = dma("sync", f"x4{b2}", sx4[b2], xin[128 * t:128 * (t + 1), :])
                guard("vector", "p4cols")
                opb("vector", lambda e, ao=ao: e.reduce_sum(s_cols[:, 0:1], ao, axis=AXX),
                    "vS")
                v = op("vector", lambda e: e.tensor_scalar_mul(
                    s_cols[:, 1:2], s_cols[:, 0:1], -INVN), inc="vS")
                wait("scalar", "vS", v)
                opb("scalar", lambda e, ao=ao: e.activation(
                    sj4, ao, AF.Square, bias=s_cols[:, 1:2], accum_out=s_cols[:, 2:3]),
                    "sS")
                v = op("scalar", lambda e: e.activation(
                    s_cols[:, 3:4], s_cols[:, 2:3], AF.Sqrt,
                    scale=f32(INVN), bias=f32(1e-5)), inc="sS")
                wait("vector", "sS", v)
                opb("vector", lambda e: e.reciprocal(s_cols[:, 4:5], s_cols[:, 3:4]), "vS")
                opb("vector", lambda e: e.tensor_tensor(
                    s_cols[:, 5:6], s_cols[:, 1:2], s_cols[:, 4:5], ALU.mult), "vS")
                v = op("vector", lambda e, ao=ao: e.tensor_scalar(
                    sj4, ao, s_cols[:, 4:5], s_cols[:, 5:6], op0=ALU.mult, op1=ALU.add),
                    inc="vS")
                wait("vector", "vS", v)
                wait("vector", f"x4{b2}", xl)
                guard("vector", f"p4x2{b2}")
                v = op("vector", lambda e, b2=b2: e.tensor_tensor(
                    x2t[b2], sx4[b2], sj4, ALU.add), inc="vS")
                wait("vector", "vS", v)
                v = opb("vector", lambda e, b2=b2: e.reduce_sum(
                    s_cols[:, 6:7], x2t[b2], axis=AXX), "vS")
                setg(f"p4x{b2}", "vS", v)
                v = op("vector", lambda e: e.tensor_scalar_mul(
                    s_cols[:, 7:8], s_cols[:, 6:7], -INVN), inc="vS")
                wait("scalar", "vS", v)
                opb("scalar", lambda e, b2=b2: e.activation(
                    sj4, x2t[b2], AF.Square, bias=s_cols[:, 7:8], accum_out=s_cols[:, 8:9]),
                    "sS")
                v = op("scalar", lambda e: e.activation(
                    s_cols[:, 9:10], s_cols[:, 8:9], AF.Sqrt,
                    scale=f32(INVN), bias=f32(1e-5)), inc="sS")
                wait("vector", "sS", v)
                opb("vector", lambda e: e.reciprocal(s_cols[:, 10:11], s_cols[:, 9:10]),
                    "vS")
                opb("vector", lambda e: e.tensor_tensor(
                    s_cols[:, 11:12], s_cols[:, 7:8], s_cols[:, 10:11], ALU.mult), "vS")
                v = op("vector", lambda e, ao=ao, b2=b2: e.tensor_scalar(
                    ao, x2t[b2], s_cols[:, 10:11], s_cols[:, 11:12],
                    op0=ALU.mult, op1=ALU.add), inc="vS")
                setg("p4cols", "vS", v)
                wait("sync", "vS", v)
                x2v = dma("sync", f"xl2{b2}", d_x2[128 * t:128 * (t + 1), :], x2t[b2])
                l2v = dma("sync", f"xl2{b2}", d_ln2[128 * t:128 * (t + 1), :], ao)
                setg(f"p4x2{b2}", f"xl2{b2}", l2v)
            setg("AR1cfm", "tS", P.ctr["tS"].n)
        ckpt(4, [("xl20", P.ctr["xl20"].n), ("xl21", P.ctr["xl21"].n)])

        l2fm = A1[:, 0:8192]
        sh_ = A1[:, 8192:8192 + 44 * 512]
        gs = [A2[:, 8192 * i:8192 * (i + 1)] for i in range(2)]
        us = [A3[:, 8192 * i:8192 * (i + 1)] for i in range(2)]
        sy = A3[:, 0:8192]
        x6 = [A4[:, 2048 * i:2048 * (i + 1)] for i in range(2)]
        j6 = A4[:, 2048 * 2:2048 * 3]
        ot = [A4[:, 2048 * (3 + i):2048 * (4 + i)] for i in range(2)]
        out_st = []

        for T in range(4):
            guard("sync", "l2fm")
            if T == 0:
                waitall("sync", "xl20")
                waitall("sync", "xl21")
                wait("sync", "tS", P.ctr["tS"].n)
            for c in range(16):
                l2v = dma("sync", "l2fm", l2fm[:, 512 * c:512 * (c + 1)],
                          d_ln2[512 * T:512 * (T + 1), 128 * c:128 * (c + 1)],
                          transpose=True)
            for ib in range(11):
                sbb = ib % 2
                guard("sync", f"gs{sbb}")
                if ib < 2:
                    guard("sync", "sy")
                for c in range(16):
                    gv = dma("sync", f"gsl{sbb}", gs[sbb][:, 512 * c:512 * (c + 1)],
                             d_wgu[512 * ib:512 * (ib + 1), 128 * c:128 * (c + 1)],
                             transpose=True)
                    gv = dma("sync", f"gsl{sbb}", us[sbb][:, 512 * c:512 * (c + 1)],
                             d_wgu[II + 512 * ib:II + 512 * (ib + 1),
                                   128 * c:128 * (c + 1)], transpose=True)
                for i4 in range(4):
                    i = 4 * ib + i4
                    gp = psum[i4 % 2][:, 0:512]
                    up = psum[2 + i4 % 2][:, 0:512]
                    guard("tensor", f"ps{i4 % 2}")
                    guard("tensor", f"ps{2 + i4 % 2}")
                    if i4 == 0:
                        wait("tensor", f"gsl{sbb}", gv)
                        if ib == 0:
                            wait("tensor", "l2fm", l2v)
                            if T == 0:
                                waitall("tensor", "wq0")
                                waitall("tensor", "wq1")
                    for c in range(16):
                        op("tensor", lambda e, gp=gp, c=c, i4=i4, sbb=sbb: e.matmul(
                            gp, gs[sbb][:, 512 * c + 128 * i4:512 * c + 128 * (i4 + 1)],
                            l2fm[:, 512 * c:512 * (c + 1)], start=(c == 0), stop=(c == 15)))
                    pe_v = None
                    for c in range(16):
                        pe_v = op("tensor", lambda e, up=up, c=c, i4=i4, sbb=sbb: e.matmul(
                            up, us[sbb][:, 512 * c + 128 * i4:512 * c + 128 * (i4 + 1)],
                            l2fm[:, 512 * c:512 * (c + 1)], start=(c == 0), stop=(c == 15)),
                            inc="tS" if c == 15 else None)
                    wait("scalar", "tS", pe_v)
                    guard("scalar", "sil")
                    sv = op("scalar", lambda e, gp=gp: e.activation(s_sil[:], gp, AF.Silu),
                            inc="sS")
                    setg(f"ps{i4 % 2}", "sS", sv)
                    wait("vector", "sS", sv)
                    guard("vector", "shh")
                    v = op("vector", lambda e, up=up, i=i: e.tensor_tensor(
                        sh_[:, 512 * i:512 * (i + 1)], s_sil[:], up, ALU.mult), inc="vS")
                    setg(f"ps{2 + i4 % 2}", "vS", v)
                    setg("sil", "vS", v)
                setg(f"gs{sbb}", "tS", P.ctr["tS"].n)
                if T == 0 and ib == 0:
                    ckpt(41, [("vS", v)])
            gate_tS = P.ctr["tS"].n
            if T == 0:
                ckpt(42, [("vS", v)])
            for fq in range(4):
                for i in range(44):
                    wb = i % 2
                    guard("sync", f"wd{wb}")
                    wdv = dma("sync", f"wdl{wb}", s_wd[wb],
                              d_wdn[512 * fq:512 * (fq + 1), 128 * i:128 * (i + 1)],
                              transpose=True)
                    wait("tensor", f"wdl{wb}", wdv)
                    if i == 0:
                        for sub in range(4):
                            guard("tensor", f"ps{sub}")
                    for sub in range(4):
                        wd_v = op("tensor", lambda e, sub=sub, i=i, wb=wb: e.matmul(
                            psum[sub][:, 0:512],
                            sh_[:, 512 * i + 128 * sub:512 * i + 128 * (sub + 1)],
                            s_wd[wb], start=(i == 0), stop=(i == 43)),
                            inc="tS" if sub == 3 else None)
                    setg(f"wd{wb}", "tS", wd_v)
                dn_v = wd_v
                if T == 0 and fq == 0 and limit == 44:
                    op("vector", lambda e, dn_v=dn_v: e.engine_nop())
                    wait("vector", "tS", dn_v)
                    v44 = op("vector", lambda e: e.engine_nop(), inc="vS")
                    ckpt(44, [("vS", v44)])
                wait("vector", "tS", dn_v)
                wait("scalar", "tS", dn_v)
                if T == 0 and fq == 0:
                    pass
                for sub in range(4):
                    ps = psum[sub][:, 0:512]
                    rv = opb("vector", lambda e, ps=ps, fq=fq, sub=sub: e.reduce_sum(
                        s_cols[:, 16 + 4 * fq + sub:17 + 4 * fq + sub], ps,
                        axis=AXX), "vS")
                    if limit == 45 and T == 0 and fq == 0 and sub == 3:
                        ckpt(45, [("vS", rv)])
                    if T == 0 and fq == 0 and sub == 0:
                        guard("vector", "sy")
                        wait("vector", "tS", gate_tS)
                    v = op("vector", lambda e, ps=ps, sub=sub, fq=fq: e.tensor_copy(
                        sy[:, 2048 * sub + 512 * fq:2048 * sub + 512 * (fq + 1)], ps),
                        inc="vS")
                    wait("vector", "vS", v)
                    setg(f"ps{sub}", "vS", v)
                    guard("scalar", "sil")
                    wait("scalar", "vS", v)
                    sqv = op("scalar", lambda e, fq=fq, sub=sub: e.activation(
                        s_sil[:], sy[:, 2048 * sub + 512 * fq:2048 * sub + 512 * (fq + 1)],
                        AF.Square,
                        accum_out=s_cols[:, 32 + 4 * fq + sub:33 + 4 * fq + sub]),
                        inc="sS")
                    setg("sil", "sS", sqv)
                    if limit == 46 and T == 0 and fq == 0 and sub == 3:
                        ckpt(46, [("vS", v)])
                if T == 0 and fq == 0:
                    ckpt(43, [("vS", v)])
            for sub in range(4):
                t = 4 * T + sub
                b6 = sub % 2
                guard("sync", f"p6x{b6}")
                xl = dma("sync", f"x6{b6}", x6[b6], d_x2[128 * t:128 * (t + 1), :])
                opb("vector", lambda e, sub=sub: e.tensor_tensor(
                    s_cols[:, 1:2], s_cols[:, 16 + sub:17 + sub],
                    s_cols[:, 20 + sub:21 + sub], ALU.add), "vS")
                opb("vector", lambda e, sub=sub: e.tensor_tensor(
                    s_cols[:, 2:3], s_cols[:, 24 + sub:25 + sub],
                    s_cols[:, 28 + sub:29 + sub], ALU.add), "vS")
                opb("vector", lambda e: e.tensor_tensor(
                    s_cols[:, 3:4], s_cols[:, 1:2], s_cols[:, 2:3], ALU.add), "vS")
                wait("vector", "sS", P.ctr["sS"].n)
                opb("vector", lambda e, sub=sub: e.tensor_tensor(
                    s_cols[:, 1:2], s_cols[:, 32 + sub:33 + sub],
                    s_cols[:, 36 + sub:37 + sub], ALU.add), "vS")
                opb("vector", lambda e, sub=sub: e.tensor_tensor(
                    s_cols[:, 2:3], s_cols[:, 40 + sub:41 + sub],
                    s_cols[:, 44 + sub:45 + sub], ALU.add), "vS")
                opb("vector", lambda e: e.tensor_tensor(
                    s_cols[:, 5:6], s_cols[:, 1:2], s_cols[:, 2:3], ALU.add), "vS")
                opb("vector", lambda e: e.tensor_scalar_mul(
                    s_cols[:, 8:9], s_cols[:, 3:4], INVN), "vS")
                opb("vector", lambda e: e.tensor_scalar_mul(
                    s_cols[:, 9:10], s_cols[:, 5:6], INVN), "vS")
                opb("vector", lambda e: e.tensor_tensor(
                    s_cols[:, 10:11], s_cols[:, 8:9], s_cols[:, 8:9], ALU.mult), "vS")
                opb("vector", lambda e: e.tensor_tensor(
                    s_cols[:, 11:12],
                    s_cols[:, 9:10], s_cols[:, 10:11], ALU.subtract), "vS")
                opb("vector", lambda e: e.tensor_scalar(
                    s_cols[:, 12:13], s_cols[:, 9:10], 1e-5, 1e-11,
                    op0=ALU.mult, op1=ALU.add), "vS")
                v = op("vector", lambda e: e.tensor_tensor(
                    s_cols[:, 13:14], s_cols[:, 11:12], s_cols[:, 12:13], ALU.add),
                    inc="vS")
                wait("scalar", "vS", v)
                v = op("scalar", lambda e: e.activation(
                    s_cols[:, 14:15], s_cols[:, 13:14], AF.Sqrt), inc="sS")
                wait("vector", "sS", v)
                opb("vector", lambda e: e.reciprocal(s_cols[:, 15:16], s_cols[:, 14:15]),
                    "vS")
                opb("vector", lambda e: e.tensor_tensor(
                    s_rc[:, 1:2], s_cols[:, 8:9], s_cols[:, 15:16], ALU.mult), "vS")
                opb("vector", lambda e: e.tensor_scalar_mul(
                    s_cols[:, 12:13], s_rc[:, 1:2], -1.0), "vS")
                wait("vector", f"x6{b6}", xl)
                guard("vector", f"p6o{b6}")
                v = op("vector", lambda e, sub=sub: e.tensor_scalar(
                    j6, sy[:, 2048 * sub:2048 * (sub + 1)],
                    s_cols[:, 15:16], s_cols[:, 12:13], op0=ALU.mult, op1=ALU.add),
                    inc="vS")
                wait("vector", "vS", v)
                v = op("vector", lambda e, b6=b6: e.tensor_tensor(
                    ot[b6], x6[b6], j6, ALU.add), inc="vS")
                setg(f"p6x{b6}", "vS", v)
                wait("sync", "vS", v)
                out_st.append(dma("sync", f"ob{b6}", out[128 * t:128 * (t + 1), :], ot[b6]))
                setg(f"p6o{b6}", f"ob{b6}", out_st[-1])
            setg("l2fm", "tS", P.ctr["tS"].n)
            setg("sy", "vS", P.ctr["vS"].n)
            ckpt(5 + T, [("ob0", P.ctr["ob0"].n), ("ob1", P.ctr["ob1"].n)])
        FINALS["full"] = [("ob0", P.ctr["ob0"].n), ("ob1", P.ctr["ob1"].n)]


    try:
        _phases()
    except _Stop:
        pass

    fpairs = FINALS.get("full", FINALS.get("last"))
    if os.environ.get("MK_DUMP"):
        dmp = {"ln1": (d_ln1, [S, D], BF16), "k": (d_k, [S, D], BF16),
               "q": (d_q, [OWN, D], BF16), "v": (d_v, [S, D], BF16),
               "ctx": (d_ctx, [OWN, D], BF16), "x2": (d_x2, [OWN, D], F32),
               "ln2": (d_ln2, [OWN, D], BF16), "wgu": (d_wgu, [GU, D], BF16),
               "wdn": (d_wdn, [D, II], BF16)}
        for cn, vv in fpairs:
            wait("sync", cn, vv)
        for nm in os.environ["MK_DUMP"].split(","):
            ap_, shp, dt = dmp[nm]
            o = nc.dram_tensor("dump_" + nm, shp, dt, kind="ExternalOutput").ap()
            dma("sync", "dmp", o, ap_)
        fpairs = [("dmp", P.ctr["dmp"].n)]
    _fsems = [(P.ctr[cn].sem, vv) for cn, vv in fpairs]
    with nc.Block() as block:
        @block.sync
        def _(e):
            for fn in P.q["sync"]:
                fn(e)
            for sem_, vv_ in _fsems:
                e.wait_ge(sem_, vv_)

        @block.gpsimd
        def _(e):
            for fn in P.q["gpsimd"]:
                fn(e)

        @block.tensor
        def _(e):
            for fn in P.q["tensor"]:
                fn(e)

        @block.vector
        def _(e):
            for fn in P.q["vector"]:
                fn(e)

        @block.scalar
        def _(e):
            for fn in P.q["scalar"]:
                fn(e)

    stack.close()
    return nc


def _host_consts(parity):
    hd = 128
    inv_freq = 1.0 / (10000.0 ** (np.arange(0, hd, 2, dtype=np.float64) / hd))
    t = np.arange(S, dtype=np.float64)
    fr = np.outer(t, inv_freq)
    cos = np.cos(fr).astype(np.float32)
    sin = np.sin(fr).astype(np.float32)
    sc = np.float32(1.0 / np.sqrt(hd))
    ccat = np.concatenate([cos, cos], 1)
    scat = np.concatenate([-sin, sin], 1)
    ccat4 = np.tile(ccat, (1, 4))
    scat4 = np.tile(scat, (1, 4))
    own = np.arange(parity * OWN, parity * OWN + OWN)
    other = np.arange((1 - parity) * OWN, (1 - parity) * OWN + OWN)
    perm = np.concatenate([own, other])
    ktab = np.concatenate([ccat4[perm] * sc, scat4[perm] * sc], 1)
    qtab = np.concatenate([ccat4[own], scat4[own]], 1)
    um = np.zeros((4, 128, 512), np.float32)
    kp = np.arange(128)[:, None]
    qj = np.arange(512)[None, :]
    for s_ in range(4):
        um[s_] = (s_ * 128 + kp > qj).astype(np.float32)
    bcol = np.full((4 * 32, 128), -10030.0, np.float32)
    for qc in range(4):
        for idx, kt in enumerate(_ktlist(qc)):
            if kt >= 16 and parity == 0:
                bcol[32 * qc + idx] = -10030.0
            else:
                bcol[32 * qc + idx] = -30.0
    bcol = np.ascontiguousarray(bcol.T)
    return dict(ktab=np.ascontiguousarray(ktab), qtab=np.ascontiguousarray(qtab),
                umask=um, bcol=bcol, perm=perm)


def kernel(L0_attn_x, L0_attn_Wqkv, L0_attn_bqkv, L0_attn_Wout, L0_attn_bout,
           L0_ffn_gate_up_packed, L0_ffn_gate_up_scales,
           L0_ffn_down_packed, L0_ffn_down_scales):
    x = np.ascontiguousarray(np.asarray(L0_attn_x, dtype=np.float32))
    wqkv = np.ascontiguousarray(np.asarray(L0_attn_Wqkv, dtype=np.float32))
    bqkv = np.ascontiguousarray(np.asarray(L0_attn_bqkv, dtype=np.float32)[None, :])
    wout = np.ascontiguousarray(np.asarray(L0_attn_Wout, dtype=np.float32))
    bout = np.ascontiguousarray(np.asarray(L0_attn_bout, dtype=np.float32)[None, :])
    pk_gu = np.ascontiguousarray(np.asarray(L0_ffn_gate_up_packed, dtype=np.int32))
    pk_dn = np.ascontiguousarray(np.asarray(L0_ffn_down_packed, dtype=np.int32))
    scT_gu = np.ascontiguousarray(np.asarray(L0_ffn_gate_up_scales).T)
    scT_dn = np.ascontiguousarray(np.asarray(L0_ffn_down_scales).T)
    onesr = np.ones((1, 128), np.float32)
    lmask = (-10000.0 * np.eye(128)).astype(np.float32)

    consts = [_host_consts(0), _host_consts(1)]
    nc = build_nc()
    in_maps = []
    for c in range(NCORES):
        B, par = c // 2, c % 2
        cst = consts[par]
        xp = np.ascontiguousarray(x[B][cst["perm"]])
        in_maps.append(dict(
            xin=xp, wqkv=wqkv, bqkv=bqkv, wout=wout, bout=bout,
            pk_gu=pk_gu, pk_dn=pk_dn, scT_gu=scT_gu, scT_dn=scT_dn,
            ktab=cst["ktab"], qtab=cst["qtab"],
            umask=np.ascontiguousarray(cst["umask"]),
            bcol=np.ascontiguousarray(cst["bcol"]),
            onesr=onesr, lmask=lmask,
        ))
    res = run_bass_kernel_spmd(nc, in_maps, core_ids=list(range(NCORES)))
    outa = np.empty((4, S, D), np.float32)
    for c in range(NCORES):
        B, par = c // 2, c % 2
        outa[B, par * OWN:(par + 1) * OWN] = res.results[c]["out"]
    return outa




# version 08
# speedup vs baseline: 1.0904x; 1.0904x over previous
import os
import numpy as np
import ml_dtypes
from contextlib import ExitStack

import concourse.bass as bass
import concourse.mybir as mybir
from concourse.bass_utils import run_bass_kernel_spmd

F32 = mybir.dt.float32
F16 = mybir.dt.float16
BF16 = mybir.dt.bfloat16
I32 = mybir.dt.int32
U16 = mybir.dt.uint16
AF = mybir.ActivationFunctionType
ALU = mybir.AluOpType
AXX = mybir.AxisListType.X
NBF = ml_dtypes.bfloat16
f32 = np.float32

D = 2048
S = 4096
OWN = 2048
GU = 11264
II = 5632
NCORES = 8
INVN = 1.0 / D


def _ktlist(qc):
    return list(range(0, 4 * qc + 4)) + list(range(16, 32))


class Ctr:
    def __init__(self, sem):
        self.sem = sem
        self.n = 0


class Prog:
    def __init__(self, nc, stack):
        self.nc = nc
        self.stack = stack
        self.q = {e: [] for e in ("sync", "gpsimd", "tensor", "vector", "scalar")}
        self.ctr = {}
        for name in ("tS", "vS", "sS"):
            self.ctr[name] = Ctr(stack.enter_context(nc.semaphore(name)))
        self.guards = {}

    def _c(self, name):
        if name not in self.ctr:
            self.ctr[name] = Ctr(self.stack.enter_context(self.nc.semaphore(name)))
        return self.ctr[name]

    def opb(self, eng, fn, inc):
        v = self.op(eng, fn, inc=inc)
        self.wait(eng, inc, v)
        return v

    def op(self, eng, fn, inc=None, k=1):
        if inc is not None:
            c = self._c(inc)
            c.n += k
            sem = c.sem
            self.q[eng].append(lambda e, fn=fn, sem=sem, k=k: fn(e).then_inc(sem, k))
            return c.n
        self.q[eng].append(fn)
        return None

    def dma(self, eng, stream, outp, inp, **kw):
        return self.op(eng, lambda e, o=outp, i=inp, kw=kw: e.dma_start(out=o, in_=i, **kw),
                       inc=stream, k=16)

    def wait(self, eng, cname, v):
        if not v or v <= 0:
            return
        assert v <= self._c(cname).n, f"forward wait {cname} {v} > {self.ctr[cname].n}"
        sem = self.ctr[cname].sem
        self.q[eng].append(lambda e, sem=sem, v=v: e.wait_ge(sem, v))

    def waitall(self, eng, cname):
        self.wait(eng, cname, self._c(cname).n)

    def guard(self, eng, key):
        g = self.guards.get(key)
        if g is not None:
            self.wait(eng, g[0], g[1])

    def setg(self, key, cname, v):
        self.guards[key] = (cname, v)


def build_nc(limit=99):
    nc = bass.Bass("TRN2", target_bir_lowering=False)
    stack = ExitStack()

    def din(name, shape, dt):
        return nc.dram_tensor(name, list(shape), dt, kind="ExternalInput").ap()

    xin = din("xin", [S, D], F32)
    wqkv = din("wqkv", [D, 3 * D], F32)
    bqkv = din("bqkv", [1, 3 * D], F32)
    wout = din("wout", [D, D], F32)
    boutI = din("bout", [1, D], F32)
    pk_gu = din("pk_gu", [GU, D // 8], I32)
    pk_dn = din("pk_dn", [D, II // 8], I32)
    scT_gu = din("scT_gu", [GU, 16], F16)
    scT_dn = din("scT_dn", [D, 44], F16)
    ktab = din("ktab", [S, 1024], F32)
    qtab = din("qtab", [OWN, 1024], F32)
    umask = din("umask", [4, 128, 512], F32)
    bcolI = din("bcol", [128, 4 * 32], F32)
    onesr = din("onesr", [1, 128], F32)
    lmaskI = din("lmask", [128, 128], F32)

    out = nc.dram_tensor("out", [OWN, D], F32, kind="ExternalOutput").ap()

    d_ln1 = nc.dram_tensor("d_ln1", [S, D], BF16).ap()
    d_k = nc.dram_tensor("d_k", [S, D], BF16).ap()
    d_q = nc.dram_tensor("d_q", [OWN, D], BF16).ap()
    d_v = nc.dram_tensor("d_v", [S, D], BF16).ap()
    d_ctx = nc.dram_tensor("d_ctx", [OWN, D], BF16).ap()
    d_x2 = nc.dram_tensor("d_x2", [OWN, D], F32).ap()
    d_ln2 = nc.dram_tensor("d_ln2", [OWN, D], BF16).ap()
    d_wgu = nc.dram_tensor("d_wgu", [GU, D], BF16).ap()
    d_wdn = nc.dram_tensor("d_wdn", [D, II], BF16).ap()

    P = Prog(nc, stack)
    op, dma, wait, guard, setg = P.op, P.dma, P.wait, P.guard, P.setg
    opb, waitall = P.opb, P.waitall
    FINALS = {}

    class _Stop(Exception):
        pass

    def ckpt(n, pairs):
        FINALS["last"] = pairs
        if (n >= 40 and limit == n) or (n < 40 and limit <= n):
            raise _Stop()

    def sb(name, shape, dt):
        return stack.enter_context(nc.sbuf_tensor(name, list(shape), dt))

    AR1 = sb("AR1", [128, 33280], BF16)
    AR2 = sb("AR2", [128, 16384], BF16)
    AR3 = sb("AR3", [128, 16384], BF16)
    AR4 = sb("AR4", [128, 10240], F32)
    s_va = [sb(f"s_va{i}", [128, 32 * 129], BF16).ap() for i in range(2)]
    s_sil = sb("s_sil", [128, 512], F32)
    s_wd = [sb(f"s_wd{i}", [128, 512], BF16).ap() for i in range(4)]
    s_ctxb = [sb(f"s_ctx{i}", [128, 128], BF16).ap() for i in range(2)]
    s_cols = sb("s_cols", [128, 48], F32)
    s_rc = sb("s_rc", [128, 2], F32)
    c_bqg = sb("c_bqg", [1, 1024], BF16)
    c_bout = sb("c_bout", [1, D], BF16)
    c_lmask = sb("c_lmask", [128, 128], BF16)
    c_bcol = sb("c_bcol", [128, 128], F32)
    c_ones = sb("c_ones", [1, 128], BF16)
    c_scgu = sb("c_scgu", [128, 88 * 16], F32)
    c_scdn = sb("c_scdn", [128, 16 * 44], F32)

    A1, A2, A3, A4 = AR1.ap(), AR2.ap(), AR3.ap(), AR4.ap()

    psum = [stack.enter_context(nc.psum_tensor(f"ps{i}", [128, 1024], F32)).ap()
            for i in range(4)]

    def _phases():
        dma("gpsimd", "gC", c_ones[:], onesr)
        dma("gpsimd", "gC", c_bout[:], boutI)
        dma("gpsimd", "gC", c_lmask[:], lmaskI)
        dma("sync", "sC", c_bcol[:], bcolI)
        sg16 = A1.bitcast(F16)
        dma("sync", "sC", sg16[:, 0:88 * 16].rearrange("p (c g) -> p c g", g=16),
            scT_gu.rearrange("(c p) g -> p c g", p=128))
        v = dma("sync", "sC", sg16[:, 2048:2048 + 16 * 44].rearrange("p (c g) -> p c g", g=44),
                scT_dn.rearrange("(c p) g -> p c g", p=128))
        wait("vector", "sC", v)
        op("vector", lambda e: e.tensor_copy(c_scgu[:], sg16[:, 0:88 * 16]), inc="vS")
        scv = op("vector", lambda e: e.tensor_copy(c_scdn[:], sg16[:, 2048:2048 + 16 * 44]),
                 inc="vS")
        setg("AR1", "vS", scv)

        pko = [0, 704]
        mago, sgno, rawo, fixo, mg2o, wqo = 1408, 4224, 7040, 9856, 12672, 15488
        u16v = A1.bitcast(U16)
        i32v = A1.bitcast(I32)
        dq_units = [("gu", cc, 0) for cc in range(88)] + \
                   [("dn", cc, hf) for cc in range(16) for hf in range(2)]
        dq_state = {"i": 0, "spill": [], "first": True}

        def pump_dq():
            i = dq_state["i"]
            if i >= len(dq_units):
                return
            dq_state["i"] += 1
            which, cc, hf = dq_units[i]
            b = i % 2
            nu = 256 if which == "gu" else 352
            nn = nu * 8
            if dq_state["first"]:
                wait("gpsimd", "vS", scv)
                dq_state["first"] = False
            if dq_state.get("p3_resume"):
                wait("gpsimd", "tS", dq_state["p3_resume"])
                dq_state["p3_resume"] = None
            guard("gpsimd", f"dqpk{b}")
            if which == "gu":
                src = pk_gu[128 * cc:128 * (cc + 1), :]
            else:
                src = pk_dn[128 * cc:128 * (cc + 1), 352 * hf:352 * (hf + 1)]
            ldv = dma("gpsimd", f"pk{b}", i32v[:, pko[b] // 2:pko[b] // 2 + nu], src)
            wait("vector", f"pk{b}", ldv)
            guard("vector", "dqbuf")
            u = u16v[:, pko[b]:pko[b] + 2 * nu]
            for j in range(4):
                sh = 6 - 4 * j
                if sh >= 0:
                    op("vector", lambda e, j=j, u=u, sh=sh, nn=nn: e.tensor_scalar(
                        u16v[:, mago + j:mago + nn:4], u, sh, 0x1C0,
                        op0=ALU.logical_shift_left, op1=ALU.bitwise_and), inc="vS")
                else:
                    op("vector", lambda e, j=j, u=u, sh=sh, nn=nn: e.tensor_scalar(
                        u16v[:, mago + j:mago + nn:4], u, -sh, 0x1C0,
                        op0=ALU.logical_shift_right, op1=ALU.bitwise_and), inc="vS")
                pv8 = op("vector", lambda e, j=j, u=u, nn=nn: e.tensor_scalar(
                    u16v[:, sgno + j:sgno + nn:4], u, 8 << (4 * j), 12 - 4 * j,
                    op0=ALU.bitwise_and, op1=ALU.logical_shift_left), inc="vS")
            wait("vector", "vS", pv8)
            v = op("vector", lambda e, nn=nn: e.tensor_single_scalar(
                u16v[:, rawo:rawo + nn], u16v[:, mago:mago + nn], 0x3F00, ALU.add),
                inc="vS")
            wait("scalar", "vS", v)
            v = op("scalar", lambda e, nn=nn: e.activation(
                A1[:, fixo:fixo + nn], A1[:, rawo:rawo + nn], AF.Relu,
                bias=f32(1.0), scale=f32(-1.0)), inc="sS")
            wait("vector", "sS", v)
            vsub = op("vector", lambda e, nn=nn: e.tensor_tensor(
                A1[:, mg2o:mg2o + nn], A1[:, rawo:rawo + nn],
                A1[:, fixo:fixo + nn], ALU.subtract), inc="vS")
            wait("vector", "vS", vsub)
            guard("vector", f"dqwq{b}")
            wqb = wqo + 2816 * b
            vor2 = op("vector", lambda e, nn=nn, wqb=wqb: e.tensor_tensor(
                u16v[:, wqb:wqb + nn], u16v[:, mg2o:mg2o + nn],
                u16v[:, sgno:sgno + nn], ALU.bitwise_or), inc="vS")
            wait("vector", "vS", vor2)
            if which == "gu":
                sc, scb = c_scgu, 16 * cc
            else:
                sc, scb = c_scdn, 44 * cc + 22 * hf
            ngr = nn // 128
            for g in range(ngr):
                v = op("vector", lambda e, g=g, wqb=wqb, sc=sc, scb=scb: e.tensor_scalar_mul(
                    A1[:, wqb + 128 * g:wqb + 128 * (g + 1)],
                    A1[:, wqb + 128 * g:wqb + 128 * (g + 1)],
                    sc[:, scb + g:scb + g + 1]), inc="vS" if g == ngr - 1 else None)
            setg(f"dqpk{b}", "vS", v)
            setg("dqbuf", "vS", v)
            wait("gpsimd", "vS", v)
            if which == "gu":
                dstp = d_wgu[128 * cc:128 * (cc + 1), :]
            else:
                dstp = d_wdn[128 * cc:128 * (cc + 1), 2816 * hf:2816 * (hf + 1)]
            sv = dma("gpsimd", f"wq{b}", dstp, A1[:, wqb:wqb + nn])
            dq_state["spill"].append(sv)
            setg(f"dqwq{b}", f"wq{b}", sv)

        sx = [A4[:, 2048 * i:2048 * (i + 1)] for i in range(2)]
        sjunk = A4[:, 2048 * 2:2048 * 3]
        slo = [A2[:, 2048 * i:2048 * (i + 1)] for i in range(2)]

        def ln_stats(src, mcol, sdcol, sqbias=True, junk=None):
            pass

        p1_store = []
        for t in range(S // 128):
            b = t % 2
            guard("sync", f"p1x{b}")
            xl = dma("sync", f"x{b}", sx[b], xin[128 * t:128 * (t + 1), :])
            wait("vector", f"x{b}", xl)
            guard("vector", "p1cols")
            opb("vector", lambda e, b=b: e.reduce_sum(s_cols[:, 0:1], sx[b], axis=AXX),
                "vS")
            v = op("vector", lambda e: e.tensor_scalar_mul(
                s_cols[:, 1:2], s_cols[:, 0:1], -INVN), inc="vS")
            wait("scalar", "vS", v)
            opb("scalar", lambda e, b=b: e.activation(
                sjunk, sx[b], AF.Square, bias=s_cols[:, 1:2], accum_out=s_cols[:, 2:3]),
                "sS")
            v = op("scalar", lambda e: e.activation(
                s_cols[:, 3:4], s_cols[:, 2:3], AF.Sqrt,
                scale=f32(INVN), bias=f32(1e-5)), inc="sS")
            wait("vector", "sS", v)
            opb("vector", lambda e: e.reciprocal(s_cols[:, 4:5], s_cols[:, 3:4]), "vS")
            opb("vector", lambda e: e.tensor_tensor(
                s_cols[:, 5:6], s_cols[:, 1:2], s_cols[:, 4:5], ALU.mult), "vS")
            guard("vector", f"p1lo{b}")
            v = op("vector", lambda e, b=b: e.tensor_scalar(
                slo[b], sx[b], s_cols[:, 4:5], s_cols[:, 5:6],
                op0=ALU.mult, op1=ALU.add), inc="vS")
            setg(f"p1x{b}", "vS", v)
            setg("p1cols", "sS", P.ctr["sS"].n)
            wait("sync", "vS", v)
            p1_store.append(dma("sync", f"lo{b}", d_ln1[128 * t:128 * (t + 1), :], slo[b]))
            setg(f"p1lo{b}", f"lo{b}", p1_store[-1])
            pump_dq()
        ckpt(1, [("lo0", P.ctr["lo0"].n), ("lo1", P.ctr["lo1"].n)])

        lnq = A1[:, 0:16384]
        wg = A1[:, 16384:32768]
        qk = [A2[:, 1024 * i:1024 * (i + 1)] for i in range(2)]
        rts = [A4[:, 1024 * i:1024 * (i + 1)] for i in range(2)]
        rp = A4[:, 1024 * 2:1024 * 4]

        waitall("gpsimd", "wq0")
        waitall("gpsimd", "wq1")
        wait("gpsimd", "vS", P.ctr["vS"].n)
        waitall("sync", "wq0")
        waitall("sync", "wq1")
        wait("sync", "vS", P.ctr["vS"].n)
        groups = [("k", D, 4, ktab, d_k, 0), ("k", D + 1024, 4, ktab, d_k, 1024),
                  ("q", 0, 2, qtab, d_q, 0), ("q", 1024, 2, qtab, d_q, 1024),
                  ("v", 2 * D, 4, None, d_v, 0), ("v", 2 * D + 1024, 4, None, d_v, 1024)]
        for gname, colbase, nq, rtab, dst, dcol in groups:
            guard("gpsimd", "wg")
            guard("gpsimd", "bqg")
            for c in range(16):
                wgv = dma("gpsimd", "wg", wg[:, 1024 * c:1024 * (c + 1)],
                          wqkv[128 * c:128 * (c + 1), colbase:colbase + 1024])
            bqv = dma("gpsimd", "wg", c_bqg[:], bqkv[0:1, colbase:colbase + 1024])
            for qtr in range(nq):
                guard("sync", "lnq")
                if gname == "k" and colbase == D and qtr == 0:
                    waitall("sync", "lo0")
                    waitall("sync", "lo1")
                for c in range(16):
                    lnv = dma("sync", "slnq", lnq[:, 1024 * c:1024 * (c + 1)],
                              d_ln1[1024 * qtr:1024 * (qtr + 1), 128 * c:128 * (c + 1)],
                              transpose=True)
                for tt in range(8):
                    at = 8 * qtr + tt
                    bb = at % 2
                    if rtab is not None:
                        guard("sync", f"rt{bb}")
                        rtv = dma("sync", f"rt{bb}", rts[bb], rtab[128 * at:128 * (at + 1), :])
                    guard_qk_done = False
                    for nh2 in range(2):
                        pidx = (2 * at + nh2) % 4
                        ps = psum[pidx][:, 0:512]
                        guard("tensor", f"ps{pidx}")
                        if tt == 0 and nh2 == 0:
                            wait("tensor", "wg", bqv)
                            wait("tensor", "slnq", lnv)
                            if gname == "k" and colbase == D and qtr == 0 and tt == 0:
                                waitall("tensor", "gC")
                        for c in range(16):
                            op("tensor", lambda e, ps=ps, c=c, tt=tt, nh2=nh2: e.matmul(
                                ps, lnq[:, 1024 * c + 128 * tt:1024 * c + 128 * (tt + 1)],
                                wg[:, 1024 * c + 512 * nh2:1024 * c + 512 * (nh2 + 1)],
                                start=(c == 0), stop=False))
                        pe_v = op("tensor", lambda e, ps=ps, nh2=nh2: e.matmul(
                            ps, c_ones[:], c_bqg[0:1, 512 * nh2:512 * (nh2 + 1)],
                            start=False, stop=True), inc="tS")
                        if rtab is None:
                            wait("scalar", "tS", pe_v)
                            if not guard_qk_done:
                                guard("scalar", f"qk{bb}")
                            ev = op("scalar", lambda e, ps=ps, bb=bb, nh2=nh2: e.activation(
                                qk[bb][:, 512 * nh2:512 * (nh2 + 1)], ps, AF.Copy),
                                inc="sS")
                            setg(f"ps{pidx}", "sS", ev)
                            ev_c = "sS"
                        else:
                            wait("vector", "tS", pe_v)
                            if nh2 == 0:
                                wait("vector", f"rt{bb}", rtv)
                            if not guard_qk_done:
                                guard("vector", f"qk{bb}")
                            ps3 = ps.rearrange("p (h x d) -> p h x d", h=4, x=2)
                            sc3 = rts[bb][:, 512:1024].rearrange(
                                "p (h x d) -> p h x d", h=4, x=2)
                            rp3 = rp[:, 512:1024].rearrange(
                                "p (h x d) -> p h x d", h=4, x=2)
                            op("vector", lambda e, ps=ps, bb=bb: e.tensor_tensor(
                                rp[:, 0:512], ps, rts[bb][:, 0:512], ALU.mult), inc="vS")
                            op("vector", lambda e, ps3=ps3, sc3=sc3, rp3=rp3: e.tensor_tensor(
                                rp3[:, :, 0, :], ps3[:, :, 1, :], sc3[:, :, 0, :], ALU.mult),
                                inc="vS")
                            v3b = op("vector", lambda e, ps3=ps3, sc3=sc3, rp3=rp3: e.tensor_tensor(
                                rp3[:, :, 1, :], ps3[:, :, 0, :], sc3[:, :, 1, :], ALU.mult),
                                inc="vS")
                            wait("vector", "vS", v3b)
                            ev = op("vector", lambda e, bb=bb, nh2=nh2: e.tensor_tensor(
                                qk[bb][:, 512 * nh2:512 * (nh2 + 1)],
                                rp[:, 0:512], rp[:, 512:1024], ALU.add), inc="vS")
                            wait("vector", "vS", ev)
                            setg(f"ps{pidx}", "vS", ev)
                            ev_c = "vS"
                        guard_qk_done = True
                    if rtab is not None:
                        setg(f"rt{bb}", "vS", ev)
                    wait("sync", ev_c, ev)
                    stv = dma("sync", f"qs{bb}", dst[128 * at:128 * (at + 1), dcol:dcol + 1024], qk[bb])
                    setg(f"qk{bb}", f"qs{bb}", stv)
                setg("lnq", "tS", P.ctr["tS"].n)
            setg("wg", "tS", P.ctr["tS"].n)
            setg("bqg", "tS", P.ctr["tS"].n)
        p2_tS = P.ctr["tS"].n
        dq_state["p3_resume"] = p2_tS
        ckpt(2, [("qs0", P.ctr["qs0"].n), ("qs1", P.ctr["qs1"].n)])
        setg("AR1", "tS", p2_tS)

        kh = [A3[:, 4096 * i:4096 * (i + 1)] for i in range(2)]
        qh = [A3[:, 8192 + 2048 * i:8192 + 2048 * (i + 1)] for i in range(2)]
        um = A3[:, 12288:14336]
        pt = A2

        mz = None
        for i in range(2):
            mz = op("vector", lambda e, i=i: e.memset(
                s_va[i].rearrange("p (c v) -> p c v", c=32)[:, :, 128:129], 1.0),
                inc="vS")
        wait("gpsimd", "tS", p2_tS)
        umv = dma("gpsimd", "um", um.rearrange("p (a q) -> p a q", a=4),
                  umask.rearrange("a p q -> p a q"))
        waitall("scalar", "qs0")
        waitall("scalar", "qs1")
        waitall("scalar", "lo0")
        waitall("scalar", "lo1")

        for hh in range(16):
            b = hh % 2
            guard("sync", f"kh{b}")
            dma("sync", f"kq{b}", kh[b], d_k[:, 128 * hh:128 * (hh + 1)], transpose=True)
            kqv = dma("sync", f"kq{b}", qh[b], d_q[:, 128 * hh:128 * (hh + 1)], transpose=True)
            guard("gpsimd", f"kh{b}")
            vav = dma("gpsimd", f"va{b}", s_va[b].rearrange("p (c v) -> p c v", c=32)[:, :, 0:128],
                      d_v.rearrange("(c p) v -> p c v", p=128)[:, :, 128 * hh:128 * (hh + 1)])
            wait("tensor", f"kq{b}", kqv)
            wait("tensor", f"va{b}", vav)
            if hh == 0:
                wait("tensor", "vS", mz)
                wait("tensor", "um", umv)
            if hh == 0:
                wait("tensor", "vS", P.ctr["vS"].n)
                wait("tensor", "sS", P.ctr["sS"].n)
            for qc in range(4):
                ktl = _ktlist(qc)
                exp_last = None
                for idx, kt in enumerate(ktl):
                    pidx = idx % 4
                    ps = psum[pidx % 2][:, 512 * (pidx // 2):512 * (pidx // 2) + 512]
                    guard("tensor", f"sc{pidx}")
                    diag = 4 * qc <= kt < 4 * qc + 4
                    if diag:
                        op("tensor", lambda e, ps=ps, b=b, kt=kt, qc=qc: e.matmul(
                            ps, kh[b][:, 128 * kt:128 * (kt + 1)],
                            qh[b][:, 512 * qc:512 * (qc + 1)], start=True, stop=False))
                        so = kt - 4 * qc
                        pe_v = op("tensor", lambda e, ps=ps, so=so: e.matmul(
                            ps, c_lmask[:], um[:, 512 * so:512 * (so + 1)],
                            start=False, stop=True), inc="tS")
                    else:
                        pe_v = op("tensor", lambda e, ps=ps, b=b, kt=kt, qc=qc: e.matmul(
                            ps, kh[b][:, 128 * kt:128 * (kt + 1)],
                            qh[b][:, 512 * qc:512 * (qc + 1)], start=True, stop=True),
                            inc="tS")
                    wait("scalar", "tS", pe_v)
                    if idx == 0:
                        guard("scalar", "pt")
                    exp_last = op("scalar", lambda e, ps=ps, idx=idx, qc=qc: e.activation(
                        pt[:, 512 * idx:512 * (idx + 1)], ps, AF.Exp,
                        bias=c_bcol[:, 32 * qc + idx:32 * qc + idx + 1]), inc="sS")
                    setg(f"sc{pidx}", "sS", exp_last)
                wait("tensor", "sS", exp_last)
                for qt in range(4):
                    pv = psum[2 + qt % 2][:, 512 * (qt // 2):512 * (qt // 2) + 129]
                    guard("tensor", f"pv{qt}")
                    pv_v = None
                    for idx, kt in enumerate(ktl):
                        last = (idx == len(ktl) - 1)
                        fn = (lambda e, pv=pv, idx=idx, qt=qt, kt=kt, b=b, last=last:
                              e.matmul(
                            pv, pt[:, 512 * idx + 128 * qt:512 * idx + 128 * (qt + 1)],
                            s_va[b][:, 129 * kt:129 * (kt + 1)],
                            start=(idx == 0), stop=last))
                        pv_v = op("tensor", fn, inc="tS" if last else None)
                    wait("vector", "tS", pv_v)
                    cb = (4 * qc + qt) % 2
                    guard("vector", f"cx{cb}")
                    opb("vector", lambda e, pv=pv: e.reciprocal(s_rc[:, 0:1], pv[:, 128:129]), "vS")
                    v = op("vector", lambda e, pv=pv, cb=cb: e.tensor_scalar_mul(
                        s_ctxb[cb][:], pv[:, 0:128], s_rc[:, 0:1]), inc="vS")
                    wait("vector", "vS", v)
                    setg(f"pv{qt}", "vS", v)
                    wait("sync", "vS", v)
                    r0 = 512 * qc + 128 * qt
                    cxv = dma("sync", f"cx{cb}", d_ctx[r0:r0 + 128, 128 * hh:128 * (hh + 1)],
                              s_ctxb[cb])
                    setg(f"cx{cb}", f"cx{cb}", cxv)
                setg("pt", "tS", P.ctr["tS"].n)
            setg(f"kh{b}", "tS", P.ctr["tS"].n)
            for _ in range(8):
                pump_dq()
        while dq_state["i"] < len(dq_units):
            pump_dq()
        p5_vS = P.ctr["vS"].n
        ckpt(3, [("wq0", P.ctr["wq0"].n), ("wq1", P.ctr["wq1"].n),
                 ("cx0", P.ctr["cx0"].n), ("cx1", P.ctr["cx1"].n)])

        cfm = A1[:, 0:16384]
        wo = A1[:, 16384:32768]
        aost = A2
        sx4 = [A4[:, 2048 * i:2048 * (i + 1)] for i in range(2)]
        sj4 = A4[:, 2048 * 2:2048 * 3]
        x2t = [A4[:, 2048 * (3 + i):2048 * (4 + i)] for i in range(2)]

        for th in range(2):
            guard("sync", "AR1cfm")
            if th == 0:
                waitall("sync", "wq0")
                waitall("sync", "wq1")
                wait("sync", "vS", p5_vS)
                waitall("sync", "cx0")
                waitall("sync", "cx1")
            for c in range(16):
                cfmv = dma("sync", "cfm", cfm[:, 1024 * c:1024 * (c + 1)],
                           d_ctx[1024 * th:1024 * (th + 1), 128 * c:128 * (c + 1)],
                           transpose=True)
            for oh in range(2):
                if th == 1 and oh == 0:
                    waitall("scalar", "xl20")
                    waitall("scalar", "xl21")
                    wait("scalar", "vS", P.ctr["vS"].n)
                guard("gpsimd", "wo")
                if th == 0 and oh == 0:
                    waitall("gpsimd", "wq0")
                    waitall("gpsimd", "wq1")
                    wait("gpsimd", "vS", p5_vS)
                for c in range(16):
                    wov = dma("gpsimd", "wo", wo[:, 1024 * c:1024 * (c + 1)],
                              wout[128 * c:128 * (c + 1), 1024 * oh:1024 * (oh + 1)])
                for t8 in range(8):
                    for oc2 in range(2):
                        pidx = (2 * t8 + oc2) % 4
                        ps = psum[pidx][:, 0:512]
                        guard("tensor", f"ps{pidx}")
                        if t8 == 0 and oc2 == 0:
                            if th == 0 and oh == 0:
                                wait("tensor", "vS", P.ctr["vS"].n)
                                wait("tensor", "sS", P.ctr["sS"].n)
                            wait("tensor", "cfm", cfmv)
                            wait("tensor", "wo", wov)
                        for c in range(16):
                            op("tensor", lambda e, ps=ps, c=c, t8=t8, oh=oh, oc2=oc2: e.matmul(
                                ps, cfm[:, 1024 * c + 128 * t8:1024 * c + 128 * (t8 + 1)],
                                wo[:, 1024 * c + 512 * oc2:1024 * c + 512 * (oc2 + 1)],
                                start=(c == 0), stop=False))
                        pe_v = op("tensor", lambda e, ps=ps, oh=oh, oc2=oc2: e.matmul(
                            ps, c_ones[:], c_bout[0:1, 1024 * oh + 512 * oc2:
                                                  1024 * oh + 512 * (oc2 + 1)],
                            start=False, stop=True), inc="tS")
                        wait("scalar", "tS", pe_v)
                        if th == 0 and oh == 0 and t8 == 0 and oc2 == 0:
                            guard("scalar", "pt")
                        ev = op("scalar", lambda e, ps=ps, t8=t8, oh=oh, oc2=oc2: e.activation(
                            aost[:, 2048 * t8 + 1024 * oh + 512 * oc2:
                                 2048 * t8 + 1024 * oh + 512 * (oc2 + 1)], ps, AF.Copy),
                            inc="sS")
                        setg(f"ps{pidx}", "sS", ev)
                setg("wo", "tS", P.ctr["tS"].n)
            aost_ready = ev
            wait("vector", "sS", aost_ready)
            for t8 in range(8):
                t = 8 * th + t8
                b2 = t8 % 2
                ao = aost[:, 2048 * t8:2048 * (t8 + 1)]
                guard("sync", f"p4x{b2}")
                xl = dma("sync", f"x4{b2}", sx4[b2], xin[128 * t:128 * (t + 1), :])
                guard("vector", "p4cols")
                opb("vector", lambda e, ao=ao: e.reduce_sum(s_cols[:, 0:1], ao, axis=AXX),
                    "vS")
                v = op("vector", lambda e: e.tensor_scalar_mul(
                    s_cols[:, 1:2], s_cols[:, 0:1], -INVN), inc="vS")
                wait("scalar", "vS", v)
                opb("scalar", lambda e, ao=ao: e.activation(
                    sj4, ao, AF.Square, bias=s_cols[:, 1:2], accum_out=s_cols[:, 2:3]),
                    "sS")
                v = op("scalar", lambda e: e.activation(
                    s_cols[:, 3:4], s_cols[:, 2:3], AF.Sqrt,
                    scale=f32(INVN), bias=f32(1e-5)), inc="sS")
                wait("vector", "sS", v)
                opb("vector", lambda e: e.reciprocal(s_cols[:, 4:5], s_cols[:, 3:4]), "vS")
                opb("vector", lambda e: e.tensor_tensor(
                    s_cols[:, 5:6], s_cols[:, 1:2], s_cols[:, 4:5], ALU.mult), "vS")
                v = op("vector", lambda e, ao=ao: e.tensor_scalar(
                    sj4, ao, s_cols[:, 4:5], s_cols[:, 5:6], op0=ALU.mult, op1=ALU.add),
                    inc="vS")
                wait("vector", "vS", v)
                wait("vector", f"x4{b2}", xl)
                guard("vector", f"p4x2{b2}")
                v = op("vector", lambda e, b2=b2: e.tensor_tensor(
                    x2t[b2], sx4[b2], sj4, ALU.add), inc="vS")
                wait("vector", "vS", v)
                v = opb("vector", lambda e, b2=b2: e.reduce_sum(
                    s_cols[:, 6:7], x2t[b2], axis=AXX), "vS")
                setg(f"p4x{b2}", "vS", v)
                v = op("vector", lambda e: e.tensor_scalar_mul(
                    s_cols[:, 7:8], s_cols[:, 6:7], -INVN), inc="vS")
                wait("scalar", "vS", v)
                opb("scalar", lambda e, b2=b2: e.activation(
                    sj4, x2t[b2], AF.Square, bias=s_cols[:, 7:8], accum_out=s_cols[:, 8:9]),
                    "sS")
                v = op("scalar", lambda e: e.activation(
                    s_cols[:, 9:10], s_cols[:, 8:9], AF.Sqrt,
                    scale=f32(INVN), bias=f32(1e-5)), inc="sS")
                wait("vector", "sS", v)
                opb("vector", lambda e: e.reciprocal(s_cols[:, 10:11], s_cols[:, 9:10]),
                    "vS")
                opb("vector", lambda e: e.tensor_tensor(
                    s_cols[:, 11:12], s_cols[:, 7:8], s_cols[:, 10:11], ALU.mult), "vS")
                v = op("vector", lambda e, ao=ao, b2=b2: e.tensor_scalar(
                    ao, x2t[b2], s_cols[:, 10:11], s_cols[:, 11:12],
                    op0=ALU.mult, op1=ALU.add), inc="vS")
                setg("p4cols", "vS", v)
                wait("sync", "vS", v)
                x2v = dma("sync", f"xl2{b2}", d_x2[128 * t:128 * (t + 1), :], x2t[b2])
                l2v = dma("sync", f"xl2{b2}", d_ln2[128 * t:128 * (t + 1), :], ao)
                setg(f"p4x2{b2}", f"xl2{b2}", l2v)
            setg("AR1cfm", "tS", P.ctr["tS"].n)
        ckpt(4, [("xl20", P.ctr["xl20"].n), ("xl21", P.ctr["xl21"].n)])

        l2fm = A1[:, 0:8192]
        sh_ = A1[:, 8192:8192 + 44 * 512]
        gs = [A2[:, 8192 * i:8192 * (i + 1)] for i in range(2)]
        us = [A3[:, 8192 * i:8192 * (i + 1)] for i in range(2)]
        sy = A3[:, 0:8192]
        x6 = [A4[:, 2048 * i:2048 * (i + 1)] for i in range(2)]
        j6 = A4[:, 2048 * 2:2048 * 3]
        ot = [A4[:, 2048 * (3 + i):2048 * (4 + i)] for i in range(2)]
        out_st = []

        for T in range(4):
            guard("sync", "l2fm")
            if T == 0:
                waitall("sync", "xl20")
                waitall("sync", "xl21")
                wait("sync", "tS", P.ctr["tS"].n)
            for c in range(16):
                l2v = dma("sync", "l2fm", l2fm[:, 512 * c:512 * (c + 1)],
                          d_ln2[512 * T:512 * (T + 1), 128 * c:128 * (c + 1)],
                          transpose=True)
            for ib in range(11):
                sbb = ib % 2
                guard("sync", f"gs{sbb}")
                guard("scalar", f"gs{sbb}")
                if ib < 2:
                    guard("sync", "sy")
                    guard("scalar", "sy")
                gv2 = None
                _dual = False
                for c in range(16):
                    q1 = "sync" if (c % 2 == 0 or not _dual) else "scalar"
                    st1 = f"gsl{sbb}" if (c % 2 == 0 or not _dual) else f"gsg{sbb}"
                    gv = dma(q1, st1, gs[sbb][:, 512 * c:512 * (c + 1)],
                             d_wgu[512 * ib:512 * (ib + 1), 128 * c:128 * (c + 1)],
                             transpose=True)
                    if c % 2 == 0:
                        gv2 = dma(q1, st1, us[sbb][:, 512 * c:512 * (c + 1)],
                                  d_wgu[II + 512 * ib:II + 512 * (ib + 1),
                                        128 * c:128 * (c + 1)], transpose=True)
                    else:
                        gv = dma(q1, st1, us[sbb][:, 512 * c:512 * (c + 1)],
                                 d_wgu[II + 512 * ib:II + 512 * (ib + 1),
                                       128 * c:128 * (c + 1)], transpose=True)
                for i4 in range(4):
                    i = 4 * ib + i4
                    gp = psum[i4 % 2][:, 0:512]
                    up = psum[2 + i4 % 2][:, 0:512]
                    guard("tensor", f"ps{i4 % 2}")
                    guard("tensor", f"ps{2 + i4 % 2}")
                    if i4 == 0:
                        waitall("tensor", f"gsl{sbb}")
                        waitall("tensor", f"gsg{sbb}")
                        if ib == 0:
                            wait("tensor", "l2fm", l2v)
                            if T == 0:
                                waitall("tensor", "wq0")
                                waitall("tensor", "wq1")
                    for c in range(16):
                        op("tensor", lambda e, gp=gp, c=c, i4=i4, sbb=sbb: e.matmul(
                            gp, gs[sbb][:, 512 * c + 128 * i4:512 * c + 128 * (i4 + 1)],
                            l2fm[:, 512 * c:512 * (c + 1)], start=(c == 0), stop=(c == 15)))
                    pe_v = None
                    for c in range(16):
                        pe_v = op("tensor", lambda e, up=up, c=c, i4=i4, sbb=sbb: e.matmul(
                            up, us[sbb][:, 512 * c + 128 * i4:512 * c + 128 * (i4 + 1)],
                            l2fm[:, 512 * c:512 * (c + 1)], start=(c == 0), stop=(c == 15)),
                            inc="tS" if c == 15 else None)
                    wait("scalar", "tS", pe_v)
                    guard("scalar", "sil")
                    sv = op("scalar", lambda e, gp=gp: e.activation(s_sil[:], gp, AF.Silu),
                            inc="sS")
                    setg(f"ps{i4 % 2}", "sS", sv)
                    wait("vector", "sS", sv)
                    guard("vector", "shh")
                    v = op("vector", lambda e, up=up, i=i: e.tensor_tensor(
                        sh_[:, 512 * i:512 * (i + 1)], s_sil[:], up, ALU.mult), inc="vS")
                    setg(f"ps{2 + i4 % 2}", "vS", v)
                    setg("sil", "vS", v)
                setg(f"gs{sbb}", "tS", P.ctr["tS"].n)
                if T == 0 and ib == 0:
                    ckpt(41, [("vS", v)])
            gate_tS = P.ctr["tS"].n
            if T == 0:
                ckpt(42, [("vS", v)])
            stg = [x6[1], j6]
            for fq in range(4):
                for qr in range(4):
                    sl = qr % 2
                    slab = A2[:, 5632 * sl:5632 * sl + 5632]
                    guard("sync", f"dsl{sl}")
                    guard("scalar", f"dsl{sl}")
                    if fq == 0 and qr < 2:
                        guard("sync", "sy")
                        guard("scalar", "sy")
                        guard("sync", "gs0")
                        guard("sync", "gs1")
                        guard("scalar", "gs0")
                        guard("scalar", "gs1")
                    g1 = g2 = 0
                    for ii in range(11):
                        i = 11 * qr + ii
                        _dual = False
                        q1 = "sync" if (ii % 2 == 0 or not _dual) else "scalar"
                        st1 = f"dwl{sl}" if (ii % 2 == 0 or not _dual) else f"dwg{sl}"
                        gv = dma(q1, st1, slab[:, 512 * ii:512 * (ii + 1)],
                                 d_wdn[512 * fq:512 * (fq + 1), 128 * i:128 * (i + 1)],
                                 transpose=True)
                        if ii % 2 == 0:
                            g1 = gv
                        else:
                            g2 = gv
                    waitall("tensor", f"dwl{sl}")
                    waitall("tensor", f"dwg{sl}")
                    for sub in range(4):
                        guard("tensor", f"ps{sub}")
                    for ii in range(11):
                        i = 11 * qr + ii
                        for sub in range(4):
                            wd_v = op("tensor", lambda e, sub=sub, i=i, ii=ii, slab=slab: e.matmul(
                                psum[sub][:, 0:512],
                                sh_[:, 512 * i + 128 * sub:512 * i + 128 * (sub + 1)],
                                slab[:, 512 * ii:512 * (ii + 1)],
                                start=(ii == 0), stop=(ii == 10)),
                                inc="tS" if (ii == 10 and sub == 3) else None)
                    setg(f"dsl{sl}", "tS", wd_v)
                    wait("vector", "tS", wd_v)
                    for sub in range(4):
                        ps = psum[sub][:, 0:512]
                        if qr == 3:
                            rv = opb("vector", lambda e, ps=ps, fq=fq, sub=sub: e.reduce_sum(
                                s_cols[:, 16 + 4 * fq + sub:17 + 4 * fq + sub], ps,
                                axis=AXX), "vS")
                        if qr == 0:
                            v = op("vector", lambda e, ps=ps, sub=sub: e.tensor_copy(
                                stg[0][:, 512 * sub:512 * (sub + 1)], ps), inc="vS")
                        elif qr < 3:
                            v = op("vector", lambda e, ps=ps, sub=sub, qr=qr: e.tensor_tensor(
                                stg[qr % 2][:, 512 * sub:512 * (sub + 1)],
                                stg[1 - qr % 2][:, 512 * sub:512 * (sub + 1)],
                                ps, ALU.add), inc="vS")
                        else:
                            v = op("vector", lambda e, ps=ps, sub=sub, fq=fq: e.tensor_tensor(
                                sy[:, 2048 * sub + 512 * fq:2048 * sub + 512 * (fq + 1)],
                                stg[0][:, 512 * sub:512 * (sub + 1)],
                                ps, ALU.add), inc="vS")
                        wait("vector", "vS", v)
                        setg(f"ps{sub}", "vS", v)
                    if T == 0 and fq == 0 and qr == 0:
                        pass
                for sub in range(4):
                    ysl = sy[:, 2048 * sub + 512 * fq:2048 * sub + 512 * (fq + 1)]
                    opb("vector", lambda e, ysl=ysl, fq=fq, sub=sub: e.reduce_sum(
                        s_cols[:, 16 + 4 * fq + sub:17 + 4 * fq + sub], ysl,
                        axis=AXX), "vS")
                    guard("scalar", "sil")
                    wait("scalar", "vS", P.ctr["vS"].n)
                    sqv = op("scalar", lambda e, ysl=ysl, fq=fq, sub=sub: e.activation(
                        s_sil[:], ysl, AF.Square,
                        accum_out=s_cols[:, 32 + 4 * fq + sub:33 + 4 * fq + sub]),
                        inc="sS")
                    setg("sil", "sS", sqv)
                if T == 0 and fq == 0:
                    ckpt(43, [("vS", P.ctr["vS"].n)])
            setg("gs0", "tS", P.ctr["tS"].n)
            setg("gs1", "tS", P.ctr["tS"].n)
            for sub in range(4):
                t = 4 * T + sub
                b6 = 0
                guard("sync", "p6x0")
                xl = dma("sync", "x60", x6[0], d_x2[128 * t:128 * (t + 1), :])
                opb("vector", lambda e, sub=sub: e.tensor_tensor(
                    s_cols[:, 1:2], s_cols[:, 16 + sub:17 + sub],
                    s_cols[:, 20 + sub:21 + sub], ALU.add), "vS")
                opb("vector", lambda e, sub=sub: e.tensor_tensor(
                    s_cols[:, 2:3], s_cols[:, 24 + sub:25 + sub],
                    s_cols[:, 28 + sub:29 + sub], ALU.add), "vS")
                opb("vector", lambda e: e.tensor_tensor(
                    s_cols[:, 3:4], s_cols[:, 1:2], s_cols[:, 2:3], ALU.add), "vS")
                wait("vector", "sS", P.ctr["sS"].n)
                opb("vector", lambda e, sub=sub: e.tensor_tensor(
                    s_cols[:, 1:2], s_cols[:, 32 + sub:33 + sub],
                    s_cols[:, 36 + sub:37 + sub], ALU.add), "vS")
                opb("vector", lambda e, sub=sub: e.tensor_tensor(
                    s_cols[:, 2:3], s_cols[:, 40 + sub:41 + sub],
                    s_cols[:, 44 + sub:45 + sub], ALU.add), "vS")
                opb("vector", lambda e: e.tensor_tensor(
                    s_cols[:, 5:6], s_cols[:, 1:2], s_cols[:, 2:3], ALU.add), "vS")
                opb("vector", lambda e: e.tensor_scalar_mul(
                    s_cols[:, 8:9], s_cols[:, 3:4], INVN), "vS")
                opb("vector", lambda e: e.tensor_scalar_mul(
                    s_cols[:, 9:10], s_cols[:, 5:6], INVN), "vS")
                opb("vector", lambda e: e.tensor_tensor(
                    s_cols[:, 10:11], s_cols[:, 8:9], s_cols[:, 8:9], ALU.mult), "vS")
                opb("vector", lambda e: e.tensor_tensor(
                    s_cols[:, 11:12],
                    s_cols[:, 9:10], s_cols[:, 10:11], ALU.subtract), "vS")
                opb("vector", lambda e: e.tensor_scalar(
                    s_cols[:, 12:13], s_cols[:, 9:10], 1e-5, 1e-11,
                    op0=ALU.mult, op1=ALU.add), "vS")
                v = op("vector", lambda e: e.tensor_tensor(
                    s_cols[:, 13:14], s_cols[:, 11:12], s_cols[:, 12:13], ALU.add),
                    inc="vS")
                wait("scalar", "vS", v)
                v = op("scalar", lambda e: e.activation(
                    s_cols[:, 14:15], s_cols[:, 13:14], AF.Sqrt), inc="sS")
                wait("vector", "sS", v)
                opb("vector", lambda e: e.reciprocal(s_cols[:, 15:16], s_cols[:, 14:15]),
                    "vS")
                opb("vector", lambda e: e.tensor_tensor(
                    s_rc[:, 1:2], s_cols[:, 8:9], s_cols[:, 15:16], ALU.mult), "vS")
                opb("vector", lambda e: e.tensor_scalar_mul(
                    s_cols[:, 12:13], s_rc[:, 1:2], -1.0), "vS")
                wait("vector", "x60", xl)
                ob = sub % 2
                guard("vector", f"p6o{ob}")
                v = op("vector", lambda e, sub=sub: e.tensor_scalar(
                    j6, sy[:, 2048 * sub:2048 * (sub + 1)],
                    s_cols[:, 15:16], s_cols[:, 12:13], op0=ALU.mult, op1=ALU.add),
                    inc="vS")
                wait("vector", "vS", v)
                v = op("vector", lambda e, ob=ob: e.tensor_tensor(
                    ot[ob], x6[0], j6, ALU.add), inc="vS")
                setg("p6x0", "vS", v)
                wait("sync", "vS", v)
                out_st.append(dma("sync", f"ob{ob}", out[128 * t:128 * (t + 1), :], ot[ob]))
                setg(f"p6o{ob}", f"ob{ob}", out_st[-1])
            setg("l2fm", "tS", P.ctr["tS"].n)
            setg("sy", "vS", P.ctr["vS"].n)
            ckpt(5 + T, [("ob0", P.ctr["ob0"].n), ("ob1", P.ctr["ob1"].n)])
        FINALS["full"] = [("ob0", P.ctr["ob0"].n), ("ob1", P.ctr["ob1"].n)]


    try:
        _phases()
    except _Stop:
        pass

    fpairs = FINALS.get("full", FINALS.get("last"))
    if os.environ.get("MK_DUMP"):
        dmp = {"ln1": (d_ln1, [S, D], BF16), "k": (d_k, [S, D], BF16),
               "q": (d_q, [OWN, D], BF16), "v": (d_v, [S, D], BF16),
               "ctx": (d_ctx, [OWN, D], BF16), "x2": (d_x2, [OWN, D], F32),
               "ln2": (d_ln2, [OWN, D], BF16), "wgu": (d_wgu, [GU, D], BF16),
               "wdn": (d_wdn, [D, II], BF16)}
        for cn, vv in fpairs:
            wait("sync", cn, vv)
        for nm in os.environ["MK_DUMP"].split(","):
            ap_, shp, dt = dmp[nm]
            o = nc.dram_tensor("dump_" + nm, shp, dt, kind="ExternalOutput").ap()
            dma("sync", "dmp", o, ap_)
        fpairs = [("dmp", P.ctr["dmp"].n)]
    _fsems = [(P.ctr[cn].sem, vv) for cn, vv in fpairs]
    with nc.Block() as block:
        @block.sync
        def _(e):
            for fn in P.q["sync"]:
                fn(e)
            for sem_, vv_ in _fsems:
                e.wait_ge(sem_, vv_)

        @block.gpsimd
        def _(e):
            for fn in P.q["gpsimd"]:
                fn(e)

        @block.tensor
        def _(e):
            for fn in P.q["tensor"]:
                fn(e)

        @block.vector
        def _(e):
            for fn in P.q["vector"]:
                fn(e)

        @block.scalar
        def _(e):
            for fn in P.q["scalar"]:
                fn(e)

    stack.close()
    return nc


def _host_consts(parity):
    hd = 128
    inv_freq = 1.0 / (10000.0 ** (np.arange(0, hd, 2, dtype=np.float64) / hd))
    t = np.arange(S, dtype=np.float64)
    fr = np.outer(t, inv_freq)
    cos = np.cos(fr).astype(np.float32)
    sin = np.sin(fr).astype(np.float32)
    sc = np.float32(1.0 / np.sqrt(hd))
    ccat = np.concatenate([cos, cos], 1)
    scat = np.concatenate([-sin, sin], 1)
    ccat4 = np.tile(ccat, (1, 4))
    scat4 = np.tile(scat, (1, 4))
    own = np.arange(parity * OWN, parity * OWN + OWN)
    other = np.arange((1 - parity) * OWN, (1 - parity) * OWN + OWN)
    perm = np.concatenate([own, other])
    ktab = np.concatenate([ccat4[perm] * sc, scat4[perm] * sc], 1)
    qtab = np.concatenate([ccat4[own], scat4[own]], 1)
    um = np.zeros((4, 128, 512), np.float32)
    kp = np.arange(128)[:, None]
    qj = np.arange(512)[None, :]
    for s_ in range(4):
        um[s_] = (s_ * 128 + kp > qj).astype(np.float32)
    bcol = np.full((4 * 32, 128), -10030.0, np.float32)
    for qc in range(4):
        for idx, kt in enumerate(_ktlist(qc)):
            if kt >= 16 and parity == 0:
                bcol[32 * qc + idx] = -10030.0
            else:
                bcol[32 * qc + idx] = -30.0
    bcol = np.ascontiguousarray(bcol.T)
    return dict(ktab=np.ascontiguousarray(ktab), qtab=np.ascontiguousarray(qtab),
                umask=um, bcol=bcol, perm=perm)


def kernel(L0_attn_x, L0_attn_Wqkv, L0_attn_bqkv, L0_attn_Wout, L0_attn_bout,
           L0_ffn_gate_up_packed, L0_ffn_gate_up_scales,
           L0_ffn_down_packed, L0_ffn_down_scales):
    x = np.ascontiguousarray(np.asarray(L0_attn_x, dtype=np.float32))
    wqkv = np.ascontiguousarray(np.asarray(L0_attn_Wqkv, dtype=np.float32))
    bqkv = np.ascontiguousarray(np.asarray(L0_attn_bqkv, dtype=np.float32)[None, :])
    wout = np.ascontiguousarray(np.asarray(L0_attn_Wout, dtype=np.float32))
    bout = np.ascontiguousarray(np.asarray(L0_attn_bout, dtype=np.float32)[None, :])
    pk_gu = np.ascontiguousarray(np.asarray(L0_ffn_gate_up_packed, dtype=np.int32))
    pk_dn = np.ascontiguousarray(np.asarray(L0_ffn_down_packed, dtype=np.int32))
    scT_gu = np.ascontiguousarray(np.asarray(L0_ffn_gate_up_scales).T)
    scT_dn = np.ascontiguousarray(np.asarray(L0_ffn_down_scales).T)
    onesr = np.ones((1, 128), np.float32)
    lmask = (-10000.0 * np.eye(128)).astype(np.float32)

    consts = [_host_consts(0), _host_consts(1)]
    nc = build_nc()
    in_maps = []
    for c in range(NCORES):
        B, par = c // 2, c % 2
        cst = consts[par]
        xp = np.ascontiguousarray(x[B][cst["perm"]])
        in_maps.append(dict(
            xin=xp, wqkv=wqkv, bqkv=bqkv, wout=wout, bout=bout,
            pk_gu=pk_gu, pk_dn=pk_dn, scT_gu=scT_gu, scT_dn=scT_dn,
            ktab=cst["ktab"], qtab=cst["qtab"],
            umask=np.ascontiguousarray(cst["umask"]),
            bcol=np.ascontiguousarray(cst["bcol"]),
            onesr=onesr, lmask=lmask,
        ))
    res = run_bass_kernel_spmd(nc, in_maps, core_ids=list(range(NCORES)))
    outa = np.empty((4, S, D), np.float32)
    for c in range(NCORES):
        B, par = c // 2, c % 2
        outa[B, par * OWN:(par + 1) * OWN] = res.results[c]["out"]
    return outa




# version 09 (final)
# speedup vs baseline: 1.0923x; 1.0018x over previous
import os
import numpy as np
import ml_dtypes
from contextlib import ExitStack

import concourse.bass as bass
import concourse.mybir as mybir
from concourse.bass_utils import run_bass_kernel_spmd

F32 = mybir.dt.float32
F16 = mybir.dt.float16
BF16 = mybir.dt.bfloat16
I32 = mybir.dt.int32
U16 = mybir.dt.uint16
AF = mybir.ActivationFunctionType
ALU = mybir.AluOpType
AXX = mybir.AxisListType.X
NBF = ml_dtypes.bfloat16
f32 = np.float32

D = 2048
S = 4096
OWN = 2048
GU = 11264
II = 5632
NCORES = 8
INVN = 1.0 / D


def _ktlist(qc):
    return list(range(0, 4 * qc + 4)) + list(range(16, 32))


class Ctr:
    def __init__(self, sem):
        self.sem = sem
        self.n = 0


class Prog:
    def __init__(self, nc, stack):
        self.nc = nc
        self.stack = stack
        self.q = {e: [] for e in ("sync", "gpsimd", "tensor", "vector", "scalar")}
        self.ctr = {}
        for name in ("tS", "vS", "sS"):
            self.ctr[name] = Ctr(stack.enter_context(nc.semaphore(name)))
        self.guards = {}

    def _c(self, name):
        if name not in self.ctr:
            self.ctr[name] = Ctr(self.stack.enter_context(self.nc.semaphore(name)))
        return self.ctr[name]

    def opb(self, eng, fn, inc):
        v = self.op(eng, fn, inc=inc)
        self.wait(eng, inc, v)
        return v

    def op(self, eng, fn, inc=None, k=1):
        if inc is not None:
            c = self._c(inc)
            c.n += k
            sem = c.sem
            self.q[eng].append(lambda e, fn=fn, sem=sem, k=k: fn(e).then_inc(sem, k))
            return c.n
        self.q[eng].append(fn)
        return None

    def dma(self, eng, stream, outp, inp, **kw):
        return self.op(eng, lambda e, o=outp, i=inp, kw=kw: e.dma_start(out=o, in_=i, **kw),
                       inc=stream, k=16)

    def wait(self, eng, cname, v):
        if not v or v <= 0:
            return
        assert v <= self._c(cname).n, f"forward wait {cname} {v} > {self.ctr[cname].n}"
        sem = self.ctr[cname].sem
        self.q[eng].append(lambda e, sem=sem, v=v: e.wait_ge(sem, v))

    def waitall(self, eng, cname):
        self.wait(eng, cname, self._c(cname).n)

    def guard(self, eng, key):
        g = self.guards.get(key)
        if g is not None:
            self.wait(eng, g[0], g[1])

    def setg(self, key, cname, v):
        self.guards[key] = (cname, v)


def build_nc(limit=99):
    nc = bass.Bass("TRN2", target_bir_lowering=False)
    stack = ExitStack()

    def din(name, shape, dt):
        return nc.dram_tensor(name, list(shape), dt, kind="ExternalInput").ap()

    xin = din("xin", [S, D], F32)
    wqkv = din("wqkv", [D, 3 * D], F32)
    bqkv = din("bqkv", [1, 3 * D], F32)
    wout = din("wout", [D, D], F32)
    boutI = din("bout", [1, D], F32)
    pk_gu = din("pk_gu", [GU, D // 8], I32)
    pk_dn = din("pk_dn", [D, II // 8], I32)
    scT_gu = din("scT_gu", [GU, 16], F16)
    scT_dn = din("scT_dn", [D, 44], F16)
    ktab = din("ktab", [S, 1024], F32)
    qtab = din("qtab", [OWN, 1024], F32)
    umask = din("umask", [4, 128, 512], F32)
    bcolI = din("bcol", [128, 4 * 32], F32)
    onesr = din("onesr", [1, 128], F32)
    lmaskI = din("lmask", [128, 128], F32)

    out = nc.dram_tensor("out", [OWN, D], F32, kind="ExternalOutput").ap()

    d_ln1 = nc.dram_tensor("d_ln1", [S, D], BF16).ap()
    d_k = nc.dram_tensor("d_k", [S, D], BF16).ap()
    d_q = nc.dram_tensor("d_q", [OWN, D], BF16).ap()
    d_v = nc.dram_tensor("d_v", [S, D], BF16).ap()
    d_ctx = nc.dram_tensor("d_ctx", [OWN, D], BF16).ap()
    d_x2 = nc.dram_tensor("d_x2", [OWN, D], F32).ap()
    d_ln2 = nc.dram_tensor("d_ln2", [OWN, D], BF16).ap()
    d_wgu = nc.dram_tensor("d_wgu", [GU, D], BF16).ap()
    d_wdn = nc.dram_tensor("d_wdn", [D, II], BF16).ap()

    P = Prog(nc, stack)
    op, dma, wait, guard, setg = P.op, P.dma, P.wait, P.guard, P.setg
    opb, waitall = P.opb, P.waitall
    FINALS = {}

    class _Stop(Exception):
        pass

    def ckpt(n, pairs):
        FINALS["last"] = pairs
        if (n >= 40 and limit == n) or (n < 40 and limit <= n):
            raise _Stop()

    def sb(name, shape, dt):
        return stack.enter_context(nc.sbuf_tensor(name, list(shape), dt))

    AR1 = sb("AR1", [128, 32768], BF16)
    AR2 = sb("AR2", [128, 16384], BF16)
    AR3 = sb("AR3", [128, 16384], BF16)
    AR4 = sb("AR4", [128, 10240], F32)
    s_va = [sb(f"s_va{i}", [128, 32 * 129], BF16).ap() for i in range(2)]
    s_sil = sb("s_sil", [128, 512], F32)
    s_wd = [sb(f"s_wd{i}", [128, 512], BF16).ap() for i in range(4)]
    s_ctxs = [sb(f"s_ctxs{i}", [128, 512], BF16).ap() for i in range(2)]
    s_rc2 = sb("s_rc2", [128, 4], F32)
    s_cols = sb("s_cols", [128, 48], F32)
    s_rc = sb("s_rc", [128, 2], F32)
    c_bqg = sb("c_bqg", [1, 1024], BF16)
    c_bout = sb("c_bout", [1, D], BF16)
    c_lmask = sb("c_lmask", [128, 128], BF16)
    c_bcol = sb("c_bcol", [128, 128], F32)
    c_ones = sb("c_ones", [1, 128], BF16)
    c_scgu = sb("c_scgu", [128, 88 * 16], F32)
    c_scdn = sb("c_scdn", [128, 16 * 44], F32)

    A1, A2, A3, A4 = AR1.ap(), AR2.ap(), AR3.ap(), AR4.ap()

    psum = [stack.enter_context(nc.psum_tensor(f"ps{i}", [128, 1024], F32)).ap()
            for i in range(4)]

    def _phases():
        dma("gpsimd", "gC", c_ones[:], onesr)
        dma("gpsimd", "gC", c_bout[:], boutI)
        dma("gpsimd", "gC", c_lmask[:], lmaskI)
        dma("sync", "sC", c_bcol[:], bcolI)
        sg16 = A1.bitcast(F16)
        dma("sync", "sC", sg16[:, 0:88 * 16].rearrange("p (c g) -> p c g", g=16),
            scT_gu.rearrange("(c p) g -> p c g", p=128))
        v = dma("sync", "sC", sg16[:, 2048:2048 + 16 * 44].rearrange("p (c g) -> p c g", g=44),
                scT_dn.rearrange("(c p) g -> p c g", p=128))
        wait("vector", "sC", v)
        op("vector", lambda e: e.tensor_copy(c_scgu[:], sg16[:, 0:88 * 16]), inc="vS")
        scv = op("vector", lambda e: e.tensor_copy(c_scdn[:], sg16[:, 2048:2048 + 16 * 44]),
                 inc="vS")
        setg("AR1", "vS", scv)

        pko = [0, 704]
        mago, sgno, rawo, fixo, mg2o, wqo = 1408, 4224, 7040, 9856, 12672, 15488
        u16v = A1.bitcast(U16)
        i32v = A1.bitcast(I32)
        dq_units = [("gu", cc, 0) for cc in range(88)] + \
                   [("dn", cc, hf) for cc in range(16) for hf in range(2)]
        dq_state = {"i": 0, "spill": [], "first": True}

        def pump_dq():
            i = dq_state["i"]
            if i >= len(dq_units):
                return
            dq_state["i"] += 1
            which, cc, hf = dq_units[i]
            b = i % 2
            nu = 256 if which == "gu" else 352
            nn = nu * 8
            if dq_state["first"]:
                wait("gpsimd", "vS", scv)
                dq_state["first"] = False
            if dq_state.get("p3_resume"):
                wait("gpsimd", "tS", dq_state["p3_resume"])
                dq_state["p3_resume"] = None
            guard("gpsimd", f"dqpk{b}")
            if which == "gu":
                src = pk_gu[128 * cc:128 * (cc + 1), :]
            else:
                src = pk_dn[128 * cc:128 * (cc + 1), 352 * hf:352 * (hf + 1)]
            ldv = dma("gpsimd", f"pk{b}", i32v[:, pko[b] // 2:pko[b] // 2 + nu], src)
            wait("vector", f"pk{b}", ldv)
            guard("vector", "dqbuf")
            u = u16v[:, pko[b]:pko[b] + 2 * nu]
            for j in range(4):
                sh = 6 - 4 * j
                if sh >= 0:
                    op("vector", lambda e, j=j, u=u, sh=sh, nn=nn: e.tensor_scalar(
                        u16v[:, mago + j:mago + nn:4], u, sh, 0x1C0,
                        op0=ALU.logical_shift_left, op1=ALU.bitwise_and), inc="vS")
                else:
                    op("vector", lambda e, j=j, u=u, sh=sh, nn=nn: e.tensor_scalar(
                        u16v[:, mago + j:mago + nn:4], u, -sh, 0x1C0,
                        op0=ALU.logical_shift_right, op1=ALU.bitwise_and), inc="vS")
                pv8 = op("vector", lambda e, j=j, u=u, nn=nn: e.tensor_scalar(
                    u16v[:, sgno + j:sgno + nn:4], u, 8 << (4 * j), 12 - 4 * j,
                    op0=ALU.bitwise_and, op1=ALU.logical_shift_left), inc="vS")
            wait("vector", "vS", pv8)
            v = op("vector", lambda e, nn=nn: e.tensor_single_scalar(
                u16v[:, rawo:rawo + nn], u16v[:, mago:mago + nn], 0x3F00, ALU.add),
                inc="vS")
            wait("scalar", "vS", v)
            v = op("scalar", lambda e, nn=nn: e.activation(
                A1[:, fixo:fixo + nn], A1[:, rawo:rawo + nn], AF.Relu,
                bias=f32(1.0), scale=f32(-1.0)), inc="sS")
            wait("vector", "sS", v)
            vsub = op("vector", lambda e, nn=nn: e.tensor_tensor(
                A1[:, mg2o:mg2o + nn], A1[:, rawo:rawo + nn],
                A1[:, fixo:fixo + nn], ALU.subtract), inc="vS")
            wait("vector", "vS", vsub)
            guard("vector", f"dqwq{b}")
            wqb = wqo + 2816 * b
            vor2 = op("vector", lambda e, nn=nn, wqb=wqb: e.tensor_tensor(
                u16v[:, wqb:wqb + nn], u16v[:, mg2o:mg2o + nn],
                u16v[:, sgno:sgno + nn], ALU.bitwise_or), inc="vS")
            wait("vector", "vS", vor2)
            if which == "gu":
                sc, scb = c_scgu, 16 * cc
            else:
                sc, scb = c_scdn, 44 * cc + 22 * hf
            ngr = nn // 128
            for g in range(ngr):
                v = op("vector", lambda e, g=g, wqb=wqb, sc=sc, scb=scb: e.tensor_scalar_mul(
                    A1[:, wqb + 128 * g:wqb + 128 * (g + 1)],
                    A1[:, wqb + 128 * g:wqb + 128 * (g + 1)],
                    sc[:, scb + g:scb + g + 1]), inc="vS" if g == ngr - 1 else None)
            setg(f"dqpk{b}", "vS", v)
            setg("dqbuf", "vS", v)
            wait("gpsimd", "vS", v)
            if which == "gu":
                dstp = d_wgu[128 * cc:128 * (cc + 1), :]
            else:
                dstp = d_wdn[128 * cc:128 * (cc + 1), 2816 * hf:2816 * (hf + 1)]
            sv = dma("gpsimd", f"wq{b}", dstp, A1[:, wqb:wqb + nn])
            dq_state["spill"].append(sv)
            setg(f"dqwq{b}", f"wq{b}", sv)

        sx = [A4[:, 2048 * i:2048 * (i + 1)] for i in range(2)]
        sjunk = A4[:, 2048 * 2:2048 * 3]
        slo = [A2[:, 2048 * i:2048 * (i + 1)] for i in range(2)]

        def ln_stats(src, mcol, sdcol, sqbias=True, junk=None):
            pass

        p1_store = []
        for t in range(S // 128):
            b = t % 2
            guard("sync", f"p1x{b}")
            xl = dma("sync", f"x{b}", sx[b], xin[128 * t:128 * (t + 1), :])
            wait("vector", f"x{b}", xl)
            guard("vector", "p1cols")
            opb("vector", lambda e, b=b: e.reduce_sum(s_cols[:, 0:1], sx[b], axis=AXX),
                "vS")
            v = op("vector", lambda e: e.tensor_scalar_mul(
                s_cols[:, 1:2], s_cols[:, 0:1], -INVN), inc="vS")
            wait("scalar", "vS", v)
            opb("scalar", lambda e, b=b: e.activation(
                sjunk, sx[b], AF.Square, bias=s_cols[:, 1:2], accum_out=s_cols[:, 2:3]),
                "sS")
            v = op("scalar", lambda e: e.activation(
                s_cols[:, 3:4], s_cols[:, 2:3], AF.Sqrt,
                scale=f32(INVN), bias=f32(1e-5)), inc="sS")
            wait("vector", "sS", v)
            opb("vector", lambda e: e.reciprocal(s_cols[:, 4:5], s_cols[:, 3:4]), "vS")
            opb("vector", lambda e: e.tensor_tensor(
                s_cols[:, 5:6], s_cols[:, 1:2], s_cols[:, 4:5], ALU.mult), "vS")
            guard("vector", f"p1lo{b}")
            v = op("vector", lambda e, b=b: e.tensor_scalar(
                slo[b], sx[b], s_cols[:, 4:5], s_cols[:, 5:6],
                op0=ALU.mult, op1=ALU.add), inc="vS")
            setg(f"p1x{b}", "vS", v)
            setg("p1cols", "sS", P.ctr["sS"].n)
            wait("sync", "vS", v)
            p1_store.append(dma("sync", f"lo{b}", d_ln1[128 * t:128 * (t + 1), :], slo[b]))
            setg(f"p1lo{b}", f"lo{b}", p1_store[-1])
            pump_dq()
        ckpt(1, [("lo0", P.ctr["lo0"].n), ("lo1", P.ctr["lo1"].n)])

        lnq = A1[:, 0:16384]
        wg = A1[:, 16384:32768]
        qk = [A2[:, 1024 * i:1024 * (i + 1)] for i in range(2)]
        rts = [A4[:, 1024 * i:1024 * (i + 1)] for i in range(2)]
        rp = A4[:, 1024 * 2:1024 * 4]

        waitall("gpsimd", "wq0")
        waitall("gpsimd", "wq1")
        wait("gpsimd", "vS", P.ctr["vS"].n)
        waitall("sync", "wq0")
        waitall("sync", "wq1")
        wait("sync", "vS", P.ctr["vS"].n)
        groups = [("k", D, 4, ktab, d_k, 0), ("k", D + 1024, 4, ktab, d_k, 1024),
                  ("q", 0, 2, qtab, d_q, 0), ("q", 1024, 2, qtab, d_q, 1024),
                  ("v", 2 * D, 4, None, d_v, 0), ("v", 2 * D + 1024, 4, None, d_v, 1024)]
        for gname, colbase, nq, rtab, dst, dcol in groups:
            guard("gpsimd", "wg")
            guard("gpsimd", "bqg")
            for c in range(16):
                wgv = dma("gpsimd", "wg", wg[:, 1024 * c:1024 * (c + 1)],
                          wqkv[128 * c:128 * (c + 1), colbase:colbase + 1024])
            bqv = dma("gpsimd", "wg", c_bqg[:], bqkv[0:1, colbase:colbase + 1024])
            for qtr in range(nq):
                guard("sync", "lnq")
                if gname == "k" and colbase == D and qtr == 0:
                    waitall("sync", "lo0")
                    waitall("sync", "lo1")
                for c in range(16):
                    lnv = dma("sync", "slnq", lnq[:, 1024 * c:1024 * (c + 1)],
                              d_ln1[1024 * qtr:1024 * (qtr + 1), 128 * c:128 * (c + 1)],
                              transpose=True)
                for tt in range(8):
                    at = 8 * qtr + tt
                    bb = at % 2
                    if rtab is not None:
                        guard("sync", f"rt{bb}")
                        rtv = dma("sync", f"rt{bb}", rts[bb], rtab[128 * at:128 * (at + 1), :])
                    guard_qk_done = False
                    for nh2 in range(2):
                        pidx = (2 * at + nh2) % 4
                        ps = psum[pidx][:, 0:512]
                        guard("tensor", f"ps{pidx}")
                        if tt == 0 and nh2 == 0:
                            wait("tensor", "wg", bqv)
                            wait("tensor", "slnq", lnv)
                            if gname == "k" and colbase == D and qtr == 0 and tt == 0:
                                waitall("tensor", "gC")
                        for c in range(16):
                            op("tensor", lambda e, ps=ps, c=c, tt=tt, nh2=nh2: e.matmul(
                                ps, lnq[:, 1024 * c + 128 * tt:1024 * c + 128 * (tt + 1)],
                                wg[:, 1024 * c + 512 * nh2:1024 * c + 512 * (nh2 + 1)],
                                start=(c == 0), stop=False))
                        pe_v = op("tensor", lambda e, ps=ps, nh2=nh2: e.matmul(
                            ps, c_ones[:], c_bqg[0:1, 512 * nh2:512 * (nh2 + 1)],
                            start=False, stop=True), inc="tS")
                        if rtab is None:
                            wait("scalar", "tS", pe_v)
                            if not guard_qk_done:
                                guard("scalar", f"qk{bb}")
                            ev = op("scalar", lambda e, ps=ps, bb=bb, nh2=nh2: e.activation(
                                qk[bb][:, 512 * nh2:512 * (nh2 + 1)], ps, AF.Copy),
                                inc="sS")
                            setg(f"ps{pidx}", "sS", ev)
                            ev_c = "sS"
                        else:
                            wait("vector", "tS", pe_v)
                            if nh2 == 0:
                                wait("vector", f"rt{bb}", rtv)
                            if not guard_qk_done:
                                guard("vector", f"qk{bb}")
                            ps3 = ps.rearrange("p (h x d) -> p h x d", h=4, x=2)
                            sc3 = rts[bb][:, 512:1024].rearrange(
                                "p (h x d) -> p h x d", h=4, x=2)
                            rp3 = rp[:, 512:1024].rearrange(
                                "p (h x d) -> p h x d", h=4, x=2)
                            op("vector", lambda e, ps=ps, bb=bb: e.tensor_tensor(
                                rp[:, 0:512], ps, rts[bb][:, 0:512], ALU.mult), inc="vS")
                            op("vector", lambda e, ps3=ps3, sc3=sc3, rp3=rp3: e.tensor_tensor(
                                rp3[:, :, 0, :], ps3[:, :, 1, :], sc3[:, :, 0, :], ALU.mult),
                                inc="vS")
                            v3b = op("vector", lambda e, ps3=ps3, sc3=sc3, rp3=rp3: e.tensor_tensor(
                                rp3[:, :, 1, :], ps3[:, :, 0, :], sc3[:, :, 1, :], ALU.mult),
                                inc="vS")
                            wait("vector", "vS", v3b)
                            ev = op("vector", lambda e, bb=bb, nh2=nh2: e.tensor_tensor(
                                qk[bb][:, 512 * nh2:512 * (nh2 + 1)],
                                rp[:, 0:512], rp[:, 512:1024], ALU.add), inc="vS")
                            wait("vector", "vS", ev)
                            setg(f"ps{pidx}", "vS", ev)
                            ev_c = "vS"
                        guard_qk_done = True
                    if rtab is not None:
                        setg(f"rt{bb}", "vS", ev)
                    wait("sync", ev_c, ev)
                    stv = dma("sync", f"qs{bb}", dst[128 * at:128 * (at + 1), dcol:dcol + 1024], qk[bb])
                    setg(f"qk{bb}", f"qs{bb}", stv)
                setg("lnq", "tS", P.ctr["tS"].n)
            setg("wg", "tS", P.ctr["tS"].n)
            setg("bqg", "tS", P.ctr["tS"].n)
        p2_tS = P.ctr["tS"].n
        dq_state["p3_resume"] = p2_tS
        ckpt(2, [("qs0", P.ctr["qs0"].n), ("qs1", P.ctr["qs1"].n)])
        setg("AR1", "tS", p2_tS)

        kh = [A3[:, 4096 * i:4096 * (i + 1)] for i in range(2)]
        qh = [A3[:, 8192 + 2048 * i:8192 + 2048 * (i + 1)] for i in range(2)]
        um = A3[:, 12288:14336]
        pt = A2

        mz = None
        for i in range(2):
            mz = op("vector", lambda e, i=i: e.memset(
                s_va[i].rearrange("p (c v) -> p c v", c=32)[:, :, 128:129], 1.0),
                inc="vS")
        wait("gpsimd", "tS", p2_tS)
        umv = dma("gpsimd", "um", um.rearrange("p (a q) -> p a q", a=4),
                  umask.rearrange("a p q -> p a q"))
        waitall("scalar", "qs0")
        waitall("scalar", "qs1")
        waitall("scalar", "lo0")
        waitall("scalar", "lo1")

        for hh in range(16):
            b = hh % 2
            guard("sync", f"kh{b}")
            dma("sync", f"kq{b}", kh[b], d_k[:, 128 * hh:128 * (hh + 1)], transpose=True)
            kqv = dma("sync", f"kq{b}", qh[b], d_q[:, 128 * hh:128 * (hh + 1)], transpose=True)
            guard("gpsimd", f"kh{b}")
            vav = dma("gpsimd", f"va{b}", s_va[b].rearrange("p (c v) -> p c v", c=32)[:, :, 0:128],
                      d_v.rearrange("(c p) v -> p c v", p=128)[:, :, 128 * hh:128 * (hh + 1)])
            wait("tensor", f"kq{b}", kqv)
            wait("tensor", f"va{b}", vav)
            if hh == 0:
                wait("tensor", "vS", mz)
                wait("tensor", "um", umv)
            if hh == 0:
                wait("tensor", "vS", P.ctr["vS"].n)
                wait("tensor", "sS", P.ctr["sS"].n)
            for qc in range(4):
                ktl = _ktlist(qc)
                exp_last = None
                for idx, kt in enumerate(ktl):
                    pidx = idx % 4
                    ps = psum[pidx % 2][:, 512 * (pidx // 2):512 * (pidx // 2) + 512]
                    guard("tensor", f"sc{pidx}")
                    diag = 4 * qc <= kt < 4 * qc + 4
                    if diag:
                        op("tensor", lambda e, ps=ps, b=b, kt=kt, qc=qc: e.matmul(
                            ps, kh[b][:, 128 * kt:128 * (kt + 1)],
                            qh[b][:, 512 * qc:512 * (qc + 1)], start=True, stop=False))
                        so = kt - 4 * qc
                        pe_v = op("tensor", lambda e, ps=ps, so=so: e.matmul(
                            ps, c_lmask[:], um[:, 512 * so:512 * (so + 1)],
                            start=False, stop=True), inc="tS")
                    else:
                        pe_v = op("tensor", lambda e, ps=ps, b=b, kt=kt, qc=qc: e.matmul(
                            ps, kh[b][:, 128 * kt:128 * (kt + 1)],
                            qh[b][:, 512 * qc:512 * (qc + 1)], start=True, stop=True),
                            inc="tS")
                    wait("scalar", "tS", pe_v)
                    if idx == 0:
                        guard("scalar", "pt")
                    exp_last = op("scalar", lambda e, ps=ps, idx=idx, qc=qc: e.activation(
                        pt[:, 512 * idx:512 * (idx + 1)], ps, AF.Exp,
                        bias=c_bcol[:, 32 * qc + idx:32 * qc + idx + 1]), inc="sS")
                    setg(f"sc{pidx}", "sS", exp_last)
                wait("tensor", "sS", exp_last)
                pv_vs = []
                for qt in range(4):
                    pv = psum[2 + qt % 2][:, 512 * (qt // 2):512 * (qt // 2) + 129]
                    guard("tensor", f"pv{qt}")
                    pv_v = None
                    for idx, kt in enumerate(ktl):
                        last = (idx == len(ktl) - 1)
                        fn = (lambda e, pv=pv, idx=idx, qt=qt, kt=kt, b=b, last=last:
                              e.matmul(
                            pv, pt[:, 512 * idx + 128 * qt:512 * idx + 128 * (qt + 1)],
                            s_va[b][:, 129 * kt:129 * (kt + 1)],
                            start=(idx == 0), stop=last))
                        pv_v = op("tensor", fn, inc="tS" if last else None)
                    pv_vs.append(pv_v)
                cb = qc % 2
                guard("vector", f"cx{cb}")
                rc_v = None
                for qt in range(4):
                    pv = psum[2 + qt % 2][:, 512 * (qt // 2):512 * (qt // 2) + 129]
                    wait("vector", "tS", pv_vs[qt])
                    rc_v = op("vector", lambda e, pv=pv, qt=qt: e.reciprocal(
                        s_rc2[:, qt:qt + 1], pv[:, 128:129]),
                        inc="vS" if qt == 3 else None)
                wait("vector", "vS", rc_v)
                ts_v = None
                for qt in range(4):
                    pv = psum[2 + qt % 2][:, 512 * (qt // 2):512 * (qt // 2) + 129]
                    ts_v = op("vector", lambda e, pv=pv, qt=qt, cb=cb: e.tensor_scalar_mul(
                        s_ctxs[cb][:, 128 * qt:128 * (qt + 1)], pv[:, 0:128],
                        s_rc2[:, qt:qt + 1]), inc="vS")
                    setg(f"pv{qt}", "vS", ts_v)
                wait("vector", "vS", ts_v)
                wait("sync", "vS", ts_v)
                r0 = 512 * qc
                cxv = dma("sync", f"cx{cb}",
                          d_ctx[r0:r0 + 512, 128 * hh:128 * (hh + 1)].rearrange(
                              "(qt p) c -> p qt c", p=128),
                          s_ctxs[cb].rearrange("p (qt c) -> p qt c", qt=4))
                setg(f"cx{cb}", f"cx{cb}", cxv)
                setg("pt", "tS", P.ctr["tS"].n)
            setg(f"kh{b}", "tS", P.ctr["tS"].n)
            for _ in range(8):
                pump_dq()
        while dq_state["i"] < len(dq_units):
            pump_dq()
        p5_vS = P.ctr["vS"].n
        ckpt(3, [("wq0", P.ctr["wq0"].n), ("wq1", P.ctr["wq1"].n),
                 ("cx0", P.ctr["cx0"].n), ("cx1", P.ctr["cx1"].n)])

        cfm = A1[:, 0:16384]
        wo = A1[:, 16384:32768]
        aost = A2
        sx4 = [A4[:, 2048 * i:2048 * (i + 1)] for i in range(2)]
        sj4 = A4[:, 2048 * 2:2048 * 3]
        x2t = [A4[:, 2048 * (3 + i):2048 * (4 + i)] for i in range(2)]

        for th in range(2):
            guard("sync", "AR1cfm")
            if th == 0:
                waitall("sync", "wq0")
                waitall("sync", "wq1")
                wait("sync", "vS", p5_vS)
                waitall("sync", "cx0")
                waitall("sync", "cx1")
            for c in range(16):
                cfmv = dma("sync", "cfm", cfm[:, 1024 * c:1024 * (c + 1)],
                           d_ctx[1024 * th:1024 * (th + 1), 128 * c:128 * (c + 1)],
                           transpose=True)
            for oh in range(2):
                if th == 1 and oh == 0:
                    waitall("scalar", "xl20")
                    waitall("scalar", "xl21")
                    wait("scalar", "vS", P.ctr["vS"].n)
                guard("gpsimd", "wo")
                if th == 0 and oh == 0:
                    waitall("gpsimd", "wq0")
                    waitall("gpsimd", "wq1")
                    wait("gpsimd", "vS", p5_vS)
                for c in range(16):
                    wov = dma("gpsimd", "wo", wo[:, 1024 * c:1024 * (c + 1)],
                              wout[128 * c:128 * (c + 1), 1024 * oh:1024 * (oh + 1)])
                for t8 in range(8):
                    for oc2 in range(2):
                        pidx = (2 * t8 + oc2) % 4
                        ps = psum[pidx][:, 0:512]
                        guard("tensor", f"ps{pidx}")
                        if t8 == 0 and oc2 == 0:
                            if th == 0 and oh == 0:
                                wait("tensor", "vS", P.ctr["vS"].n)
                                wait("tensor", "sS", P.ctr["sS"].n)
                            wait("tensor", "cfm", cfmv)
                            wait("tensor", "wo", wov)
                        for c in range(16):
                            op("tensor", lambda e, ps=ps, c=c, t8=t8, oh=oh, oc2=oc2: e.matmul(
                                ps, cfm[:, 1024 * c + 128 * t8:1024 * c + 128 * (t8 + 1)],
                                wo[:, 1024 * c + 512 * oc2:1024 * c + 512 * (oc2 + 1)],
                                start=(c == 0), stop=False))
                        pe_v = op("tensor", lambda e, ps=ps, oh=oh, oc2=oc2: e.matmul(
                            ps, c_ones[:], c_bout[0:1, 1024 * oh + 512 * oc2:
                                                  1024 * oh + 512 * (oc2 + 1)],
                            start=False, stop=True), inc="tS")
                        wait("scalar", "tS", pe_v)
                        if th == 0 and oh == 0 and t8 == 0 and oc2 == 0:
                            guard("scalar", "pt")
                        ev = op("scalar", lambda e, ps=ps, t8=t8, oh=oh, oc2=oc2: e.activation(
                            aost[:, 2048 * t8 + 1024 * oh + 512 * oc2:
                                 2048 * t8 + 1024 * oh + 512 * (oc2 + 1)], ps, AF.Copy),
                            inc="sS")
                        setg(f"ps{pidx}", "sS", ev)
                setg("wo", "tS", P.ctr["tS"].n)
            aost_ready = ev
            wait("vector", "sS", aost_ready)
            for t8 in range(8):
                t = 8 * th + t8
                b2 = t8 % 2
                ao = aost[:, 2048 * t8:2048 * (t8 + 1)]
                guard("gpsimd", f"p4x{b2}")
                xl = dma("gpsimd", f"x4{b2}", sx4[b2], xin[128 * t:128 * (t + 1), :])
                guard("vector", "p4cols")
                opb("vector", lambda e, ao=ao: e.reduce_sum(s_cols[:, 0:1], ao, axis=AXX),
                    "vS")
                v = op("vector", lambda e: e.tensor_scalar_mul(
                    s_cols[:, 1:2], s_cols[:, 0:1], -INVN), inc="vS")
                wait("scalar", "vS", v)
                opb("scalar", lambda e, ao=ao: e.activation(
                    sj4, ao, AF.Square, bias=s_cols[:, 1:2], accum_out=s_cols[:, 2:3]),
                    "sS")
                v = op("scalar", lambda e: e.activation(
                    s_cols[:, 3:4], s_cols[:, 2:3], AF.Sqrt,
                    scale=f32(INVN), bias=f32(1e-5)), inc="sS")
                wait("vector", "sS", v)
                opb("vector", lambda e: e.reciprocal(s_cols[:, 4:5], s_cols[:, 3:4]), "vS")
                opb("vector", lambda e: e.tensor_tensor(
                    s_cols[:, 5:6], s_cols[:, 1:2], s_cols[:, 4:5], ALU.mult), "vS")
                v = op("vector", lambda e, ao=ao: e.tensor_scalar(
                    sj4, ao, s_cols[:, 4:5], s_cols[:, 5:6], op0=ALU.mult, op1=ALU.add),
                    inc="vS")
                wait("vector", "vS", v)
                wait("vector", f"x4{b2}", xl)
                guard("vector", f"p4x2{b2}")
                v = op("vector", lambda e, b2=b2: e.tensor_tensor(
                    x2t[b2], sx4[b2], sj4, ALU.add), inc="vS")
                wait("vector", "vS", v)
                v = opb("vector", lambda e, b2=b2: e.reduce_sum(
                    s_cols[:, 6:7], x2t[b2], axis=AXX), "vS")
                setg(f"p4x{b2}", "vS", v)
                v = op("vector", lambda e: e.tensor_scalar_mul(
                    s_cols[:, 7:8], s_cols[:, 6:7], -INVN), inc="vS")
                wait("scalar", "vS", v)
                opb("scalar", lambda e, b2=b2: e.activation(
                    sj4, x2t[b2], AF.Square, bias=s_cols[:, 7:8], accum_out=s_cols[:, 8:9]),
                    "sS")
                v = op("scalar", lambda e: e.activation(
                    s_cols[:, 9:10], s_cols[:, 8:9], AF.Sqrt,
                    scale=f32(INVN), bias=f32(1e-5)), inc="sS")
                wait("vector", "sS", v)
                opb("vector", lambda e: e.reciprocal(s_cols[:, 10:11], s_cols[:, 9:10]),
                    "vS")
                opb("vector", lambda e: e.tensor_tensor(
                    s_cols[:, 11:12], s_cols[:, 7:8], s_cols[:, 10:11], ALU.mult), "vS")
                v = op("vector", lambda e, ao=ao, b2=b2: e.tensor_scalar(
                    ao, x2t[b2], s_cols[:, 10:11], s_cols[:, 11:12],
                    op0=ALU.mult, op1=ALU.add), inc="vS")
                setg("p4cols", "vS", v)
                wait("gpsimd", "vS", v)
                x2v = dma("gpsimd", f"xl2{b2}", d_x2[128 * t:128 * (t + 1), :], x2t[b2])
                l2v = dma("gpsimd", f"xl2{b2}", d_ln2[128 * t:128 * (t + 1), :], ao)
                setg(f"p4x2{b2}", f"xl2{b2}", l2v)
            setg("AR1cfm", "tS", P.ctr["tS"].n)
        ckpt(4, [("xl20", P.ctr["xl20"].n), ("xl21", P.ctr["xl21"].n)])

        l2fm = A1[:, 0:8192]
        sh_ = A1[:, 8192:8192 + 44 * 512]
        gs = [A2[:, 8192 * i:8192 * (i + 1)] for i in range(2)]
        us = [A3[:, 8192 * i:8192 * (i + 1)] for i in range(2)]
        sy = A3[:, 0:8192]
        x6 = [A4[:, 2048 * i:2048 * (i + 1)] for i in range(2)]
        j6 = A4[:, 2048 * 2:2048 * 3]
        ot = [A4[:, 2048 * (3 + i):2048 * (4 + i)] for i in range(2)]
        out_st = []

        for T in range(4):
            guard("sync", "l2fm")
            if T == 0:
                waitall("sync", "xl20")
                waitall("sync", "xl21")
                wait("sync", "tS", P.ctr["tS"].n)
            for c in range(16):
                l2v = dma("sync", "l2fm", l2fm[:, 512 * c:512 * (c + 1)],
                          d_ln2[512 * T:512 * (T + 1), 128 * c:128 * (c + 1)],
                          transpose=True)
            for ib in range(11):
                sbb = ib % 2
                guard("sync", f"gs{sbb}")
                guard("scalar", f"gs{sbb}")
                if ib < 2:
                    guard("sync", "sy")
                    guard("scalar", "sy")
                gv2 = None
                _dual = False
                for c in range(16):
                    q1 = "sync" if (c % 2 == 0 or not _dual) else "scalar"
                    st1 = f"gsl{sbb}" if (c % 2 == 0 or not _dual) else f"gsg{sbb}"
                    gv = dma(q1, st1, gs[sbb][:, 512 * c:512 * (c + 1)],
                             d_wgu[512 * ib:512 * (ib + 1), 128 * c:128 * (c + 1)],
                             transpose=True)
                    if c % 2 == 0:
                        gv2 = dma(q1, st1, us[sbb][:, 512 * c:512 * (c + 1)],
                                  d_wgu[II + 512 * ib:II + 512 * (ib + 1),
                                        128 * c:128 * (c + 1)], transpose=True)
                    else:
                        gv = dma(q1, st1, us[sbb][:, 512 * c:512 * (c + 1)],
                                 d_wgu[II + 512 * ib:II + 512 * (ib + 1),
                                       128 * c:128 * (c + 1)], transpose=True)
                for i4 in range(4):
                    i = 4 * ib + i4
                    gp = psum[i4 % 2][:, 0:512]
                    up = psum[2 + i4 % 2][:, 0:512]
                    guard("tensor", f"ps{i4 % 2}")
                    guard("tensor", f"ps{2 + i4 % 2}")
                    if i4 == 0:
                        waitall("tensor", f"gsl{sbb}")
                        waitall("tensor", f"gsg{sbb}")
                        if ib == 0:
                            wait("tensor", "l2fm", l2v)
                            if T == 0:
                                waitall("tensor", "wq0")
                                waitall("tensor", "wq1")
                    for c in range(16):
                        op("tensor", lambda e, gp=gp, c=c, i4=i4, sbb=sbb: e.matmul(
                            gp, gs[sbb][:, 512 * c + 128 * i4:512 * c + 128 * (i4 + 1)],
                            l2fm[:, 512 * c:512 * (c + 1)], start=(c == 0), stop=(c == 15)))
                    pe_v = None
                    for c in range(16):
                        pe_v = op("tensor", lambda e, up=up, c=c, i4=i4, sbb=sbb: e.matmul(
                            up, us[sbb][:, 512 * c + 128 * i4:512 * c + 128 * (i4 + 1)],
                            l2fm[:, 512 * c:512 * (c + 1)], start=(c == 0), stop=(c == 15)),
                            inc="tS" if c == 15 else None)
                    wait("scalar", "tS", pe_v)
                    guard("scalar", "sil")
                    sv = op("scalar", lambda e, gp=gp: e.activation(s_sil[:], gp, AF.Silu),
                            inc="sS")
                    setg(f"ps{i4 % 2}", "sS", sv)
                    wait("vector", "sS", sv)
                    guard("vector", "shh")
                    v = op("vector", lambda e, up=up, i=i: e.tensor_tensor(
                        sh_[:, 512 * i:512 * (i + 1)], s_sil[:], up, ALU.mult), inc="vS")
                    setg(f"ps{2 + i4 % 2}", "vS", v)
                    setg("sil", "vS", v)
                setg(f"gs{sbb}", "tS", P.ctr["tS"].n)
                if T == 0 and ib == 0:
                    ckpt(41, [("vS", v)])
            gate_tS = P.ctr["tS"].n
            if T == 0:
                ckpt(42, [("vS", v)])
            stg = [x6[1], j6]
            for fq in range(4):
                for qr in range(4):
                    sl = qr % 2
                    slab = A2[:, 5632 * sl:5632 * sl + 5632]
                    guard("sync", f"dsl{sl}")
                    guard("scalar", f"dsl{sl}")
                    if fq == 0 and qr < 2:
                        guard("sync", "sy")
                        guard("scalar", "sy")
                        guard("sync", "gs0")
                        guard("sync", "gs1")
                        guard("scalar", "gs0")
                        guard("scalar", "gs1")
                    g1 = g2 = 0
                    for ii in range(11):
                        i = 11 * qr + ii
                        _dual = False
                        q1 = "sync" if (ii % 2 == 0 or not _dual) else "scalar"
                        st1 = f"dwl{sl}" if (ii % 2 == 0 or not _dual) else f"dwg{sl}"
                        gv = dma(q1, st1, slab[:, 512 * ii:512 * (ii + 1)],
                                 d_wdn[512 * fq:512 * (fq + 1), 128 * i:128 * (i + 1)],
                                 transpose=True)
                        if ii % 2 == 0:
                            g1 = gv
                        else:
                            g2 = gv
                    waitall("tensor", f"dwl{sl}")
                    waitall("tensor", f"dwg{sl}")
                    for sub in range(4):
                        guard("tensor", f"ps{sub}")
                    for ii in range(11):
                        i = 11 * qr + ii
                        for sub in range(4):
                            wd_v = op("tensor", lambda e, sub=sub, i=i, ii=ii, slab=slab: e.matmul(
                                psum[sub][:, 0:512],
                                sh_[:, 512 * i + 128 * sub:512 * i + 128 * (sub + 1)],
                                slab[:, 512 * ii:512 * (ii + 1)],
                                start=(ii == 0), stop=(ii == 10)),
                                inc="tS" if (ii == 10 and sub == 3) else None)
                    setg(f"dsl{sl}", "tS", wd_v)
                    wait("vector", "tS", wd_v)
                    for sub in range(4):
                        ps = psum[sub][:, 0:512]
                        if qr == 3:
                            rv = opb("vector", lambda e, ps=ps, fq=fq, sub=sub: e.reduce_sum(
                                s_cols[:, 16 + 4 * fq + sub:17 + 4 * fq + sub], ps,
                                axis=AXX), "vS")
                        if qr == 0:
                            v = op("vector", lambda e, ps=ps, sub=sub: e.tensor_copy(
                                stg[0][:, 512 * sub:512 * (sub + 1)], ps), inc="vS")
                        elif qr < 3:
                            v = op("vector", lambda e, ps=ps, sub=sub, qr=qr: e.tensor_tensor(
                                stg[qr % 2][:, 512 * sub:512 * (sub + 1)],
                                stg[1 - qr % 2][:, 512 * sub:512 * (sub + 1)],
                                ps, ALU.add), inc="vS")
                        else:
                            v = op("vector", lambda e, ps=ps, sub=sub, fq=fq: e.tensor_tensor(
                                sy[:, 2048 * sub + 512 * fq:2048 * sub + 512 * (fq + 1)],
                                stg[0][:, 512 * sub:512 * (sub + 1)],
                                ps, ALU.add), inc="vS")
                        wait("vector", "vS", v)
                        setg(f"ps{sub}", "vS", v)
                    if T == 0 and fq == 0 and qr == 0:
                        pass
                for sub in range(4):
                    ysl = sy[:, 2048 * sub + 512 * fq:2048 * sub + 512 * (fq + 1)]
                    opb("vector", lambda e, ysl=ysl, fq=fq, sub=sub: e.reduce_sum(
                        s_cols[:, 16 + 4 * fq + sub:17 + 4 * fq + sub], ysl,
                        axis=AXX), "vS")
                    guard("scalar", "sil")
                    wait("scalar", "vS", P.ctr["vS"].n)
                    sqv = op("scalar", lambda e, ysl=ysl, fq=fq, sub=sub: e.activation(
                        s_sil[:], ysl, AF.Square,
                        accum_out=s_cols[:, 32 + 4 * fq + sub:33 + 4 * fq + sub]),
                        inc="sS")
                    setg("sil", "sS", sqv)
                if T == 0 and fq == 0:
                    ckpt(43, [("vS", P.ctr["vS"].n)])
            setg("gs0", "tS", P.ctr["tS"].n)
            setg("gs1", "tS", P.ctr["tS"].n)
            for sub in range(4):
                t = 4 * T + sub
                b6 = 0
                guard("gpsimd", "p6x0")
                xl = dma("gpsimd", "x60", x6[0], d_x2[128 * t:128 * (t + 1), :])
                opb("vector", lambda e, sub=sub: e.tensor_tensor(
                    s_cols[:, 1:2], s_cols[:, 16 + sub:17 + sub],
                    s_cols[:, 20 + sub:21 + sub], ALU.add), "vS")
                opb("vector", lambda e, sub=sub: e.tensor_tensor(
                    s_cols[:, 2:3], s_cols[:, 24 + sub:25 + sub],
                    s_cols[:, 28 + sub:29 + sub], ALU.add), "vS")
                opb("vector", lambda e: e.tensor_tensor(
                    s_cols[:, 3:4], s_cols[:, 1:2], s_cols[:, 2:3], ALU.add), "vS")
                wait("vector", "sS", P.ctr["sS"].n)
                opb("vector", lambda e, sub=sub: e.tensor_tensor(
                    s_cols[:, 1:2], s_cols[:, 32 + sub:33 + sub],
                    s_cols[:, 36 + sub:37 + sub], ALU.add), "vS")
                opb("vector", lambda e, sub=sub: e.tensor_tensor(
                    s_cols[:, 2:3], s_cols[:, 40 + sub:41 + sub],
                    s_cols[:, 44 + sub:45 + sub], ALU.add), "vS")
                opb("vector", lambda e: e.tensor_tensor(
                    s_cols[:, 5:6], s_cols[:, 1:2], s_cols[:, 2:3], ALU.add), "vS")
                opb("vector", lambda e: e.tensor_scalar_mul(
                    s_cols[:, 8:9], s_cols[:, 3:4], INVN), "vS")
                opb("vector", lambda e: e.tensor_scalar_mul(
                    s_cols[:, 9:10], s_cols[:, 5:6], INVN), "vS")
                opb("vector", lambda e: e.tensor_tensor(
                    s_cols[:, 10:11], s_cols[:, 8:9], s_cols[:, 8:9], ALU.mult), "vS")
                opb("vector", lambda e: e.tensor_tensor(
                    s_cols[:, 11:12],
                    s_cols[:, 9:10], s_cols[:, 10:11], ALU.subtract), "vS")
                opb("vector", lambda e: e.tensor_scalar(
                    s_cols[:, 12:13], s_cols[:, 9:10], 1e-5, 1e-11,
                    op0=ALU.mult, op1=ALU.add), "vS")
                v = op("vector", lambda e: e.tensor_tensor(
                    s_cols[:, 13:14], s_cols[:, 11:12], s_cols[:, 12:13], ALU.add),
                    inc="vS")
                wait("scalar", "vS", v)
                v = op("scalar", lambda e: e.activation(
                    s_cols[:, 14:15], s_cols[:, 13:14], AF.Sqrt), inc="sS")
                wait("vector", "sS", v)
                opb("vector", lambda e: e.reciprocal(s_cols[:, 15:16], s_cols[:, 14:15]),
                    "vS")
                opb("vector", lambda e: e.tensor_tensor(
                    s_rc[:, 1:2], s_cols[:, 8:9], s_cols[:, 15:16], ALU.mult), "vS")
                opb("vector", lambda e: e.tensor_scalar_mul(
                    s_cols[:, 12:13], s_rc[:, 1:2], -1.0), "vS")
                wait("vector", "x60", xl)
                ob = sub % 2
                guard("vector", f"p6o{ob}")
                v = op("vector", lambda e, sub=sub: e.tensor_scalar(
                    j6, sy[:, 2048 * sub:2048 * (sub + 1)],
                    s_cols[:, 15:16], s_cols[:, 12:13], op0=ALU.mult, op1=ALU.add),
                    inc="vS")
                wait("vector", "vS", v)
                v = op("vector", lambda e, ob=ob: e.tensor_tensor(
                    ot[ob], x6[0], j6, ALU.add), inc="vS")
                setg("p6x0", "vS", v)
                wait("gpsimd", "vS", v)
                out_st.append(dma("gpsimd", f"ob{ob}", out[128 * t:128 * (t + 1), :], ot[ob]))
                setg(f"p6o{ob}", f"ob{ob}", out_st[-1])
            setg("l2fm", "tS", P.ctr["tS"].n)
            setg("sy", "vS", P.ctr["vS"].n)
            ckpt(5 + T, [("ob0", P.ctr["ob0"].n), ("ob1", P.ctr["ob1"].n)])
        FINALS["full"] = [("ob0", P.ctr["ob0"].n), ("ob1", P.ctr["ob1"].n)]


    try:
        _phases()
    except _Stop:
        pass

    fpairs = FINALS.get("full", FINALS.get("last"))
    if os.environ.get("MK_DUMP"):
        dmp = {"ln1": (d_ln1, [S, D], BF16), "k": (d_k, [S, D], BF16),
               "q": (d_q, [OWN, D], BF16), "v": (d_v, [S, D], BF16),
               "ctx": (d_ctx, [OWN, D], BF16), "x2": (d_x2, [OWN, D], F32),
               "ln2": (d_ln2, [OWN, D], BF16), "wgu": (d_wgu, [GU, D], BF16),
               "wdn": (d_wdn, [D, II], BF16)}
        for cn, vv in fpairs:
            wait("sync", cn, vv)
        for nm in os.environ["MK_DUMP"].split(","):
            ap_, shp, dt = dmp[nm]
            o = nc.dram_tensor("dump_" + nm, shp, dt, kind="ExternalOutput").ap()
            dma("sync", "dmp", o, ap_)
        fpairs = [("dmp", P.ctr["dmp"].n)]
    _fsems = [(P.ctr[cn].sem, vv) for cn, vv in fpairs]
    with nc.Block() as block:
        @block.sync
        def _(e):
            for fn in P.q["sync"]:
                fn(e)
            for sem_, vv_ in _fsems:
                e.wait_ge(sem_, vv_)

        @block.gpsimd
        def _(e):
            for fn in P.q["gpsimd"]:
                fn(e)

        @block.tensor
        def _(e):
            for fn in P.q["tensor"]:
                fn(e)

        @block.vector
        def _(e):
            for fn in P.q["vector"]:
                fn(e)

        @block.scalar
        def _(e):
            for fn in P.q["scalar"]:
                fn(e)

    stack.close()
    return nc


def _host_consts(parity):
    hd = 128
    inv_freq = 1.0 / (10000.0 ** (np.arange(0, hd, 2, dtype=np.float64) / hd))
    t = np.arange(S, dtype=np.float64)
    fr = np.outer(t, inv_freq)
    cos = np.cos(fr).astype(np.float32)
    sin = np.sin(fr).astype(np.float32)
    sc = np.float32(1.0 / np.sqrt(hd))
    ccat = np.concatenate([cos, cos], 1)
    scat = np.concatenate([-sin, sin], 1)
    ccat4 = np.tile(ccat, (1, 4))
    scat4 = np.tile(scat, (1, 4))
    own = np.arange(parity * OWN, parity * OWN + OWN)
    other = np.arange((1 - parity) * OWN, (1 - parity) * OWN + OWN)
    perm = np.concatenate([own, other])
    ktab = np.concatenate([ccat4[perm] * sc, scat4[perm] * sc], 1)
    qtab = np.concatenate([ccat4[own], scat4[own]], 1)
    um = np.zeros((4, 128, 512), np.float32)
    kp = np.arange(128)[:, None]
    qj = np.arange(512)[None, :]
    for s_ in range(4):
        um[s_] = (s_ * 128 + kp > qj).astype(np.float32)
    bcol = np.full((4 * 32, 128), -10030.0, np.float32)
    for qc in range(4):
        for idx, kt in enumerate(_ktlist(qc)):
            if kt >= 16 and parity == 0:
                bcol[32 * qc + idx] = -10030.0
            else:
                bcol[32 * qc + idx] = -30.0
    bcol = np.ascontiguousarray(bcol.T)
    return dict(ktab=np.ascontiguousarray(ktab), qtab=np.ascontiguousarray(qtab),
                umask=um, bcol=bcol, perm=perm)


def kernel(L0_attn_x, L0_attn_Wqkv, L0_attn_bqkv, L0_attn_Wout, L0_attn_bout,
           L0_ffn_gate_up_packed, L0_ffn_gate_up_scales,
           L0_ffn_down_packed, L0_ffn_down_scales):
    x = np.ascontiguousarray(np.asarray(L0_attn_x, dtype=np.float32))
    wqkv = np.ascontiguousarray(np.asarray(L0_attn_Wqkv, dtype=np.float32))
    bqkv = np.ascontiguousarray(np.asarray(L0_attn_bqkv, dtype=np.float32)[None, :])
    wout = np.ascontiguousarray(np.asarray(L0_attn_Wout, dtype=np.float32))
    bout = np.ascontiguousarray(np.asarray(L0_attn_bout, dtype=np.float32)[None, :])
    pk_gu = np.ascontiguousarray(np.asarray(L0_ffn_gate_up_packed, dtype=np.int32))
    pk_dn = np.ascontiguousarray(np.asarray(L0_ffn_down_packed, dtype=np.int32))
    scT_gu = np.ascontiguousarray(np.asarray(L0_ffn_gate_up_scales).T)
    scT_dn = np.ascontiguousarray(np.asarray(L0_ffn_down_scales).T)
    onesr = np.ones((1, 128), np.float32)
    lmask = (-10000.0 * np.eye(128)).astype(np.float32)

    consts = [_host_consts(0), _host_consts(1)]
    nc = build_nc()
    in_maps = []
    for c in range(NCORES):
        B, par = c // 2, c % 2
        cst = consts[par]
        xp = np.ascontiguousarray(x[B][cst["perm"]])
        in_maps.append(dict(
            xin=xp, wqkv=wqkv, bqkv=bqkv, wout=wout, bout=bout,
            pk_gu=pk_gu, pk_dn=pk_dn, scT_gu=scT_gu, scT_dn=scT_dn,
            ktab=cst["ktab"], qtab=cst["qtab"],
            umask=np.ascontiguousarray(cst["umask"]),
            bcol=np.ascontiguousarray(cst["bcol"]),
            onesr=onesr, lmask=lmask,
        ))
    res = run_bass_kernel_spmd(nc, in_maps, core_ids=list(range(NCORES)))
    outa = np.empty((4, S, D), np.float32)
    for c in range(NCORES):
        B, par = c // 2, c % 2
        outa[B, par * OWN:(par + 1) * OWN] = res.results[c]["out"]
    return outa




# version 10
# speedup vs baseline: 1.1186x; 1.0240x over previous
import os
import numpy as np
import ml_dtypes
from contextlib import ExitStack

import concourse.bass as bass
import concourse.mybir as mybir
from concourse.bass_utils import run_bass_kernel_spmd

F32 = mybir.dt.float32
F16 = mybir.dt.float16
BF16 = mybir.dt.bfloat16
I32 = mybir.dt.int32
U16 = mybir.dt.uint16
AF = mybir.ActivationFunctionType
ALU = mybir.AluOpType
AXX = mybir.AxisListType.X
NBF = ml_dtypes.bfloat16
f32 = np.float32

D = 2048
S = 4096
OWN = 2048
GU = 11264
II = 5632
NCORES = 8
INVN = 1.0 / D


def _ktlist(qc):
    return list(range(0, 4 * qc + 4)) + list(range(16, 16 + 4 * qc + 4))


class Ctr:
    def __init__(self, sem):
        self.sem = sem
        self.n = 0


class Prog:
    def __init__(self, nc, stack):
        self.nc = nc
        self.stack = stack
        self.q = {e: [] for e in ("sync", "gpsimd", "tensor", "vector", "scalar")}
        self.ctr = {}
        for name in ("tS", "vS", "sS"):
            self.ctr[name] = Ctr(stack.enter_context(nc.semaphore(name)))
        self.guards = {}

    def _c(self, name):
        if name not in self.ctr:
            self.ctr[name] = Ctr(self.stack.enter_context(self.nc.semaphore(name)))
        return self.ctr[name]

    def opb(self, eng, fn, inc):
        v = self.op(eng, fn, inc=inc)
        self.wait(eng, inc, v)
        return v

    def op(self, eng, fn, inc=None, k=1):
        if inc is not None:
            c = self._c(inc)
            c.n += k
            sem = c.sem
            self.q[eng].append(lambda e, fn=fn, sem=sem, k=k: fn(e).then_inc(sem, k))
            return c.n
        self.q[eng].append(fn)
        return None

    def dma(self, eng, stream, outp, inp, **kw):
        return self.op(eng, lambda e, o=outp, i=inp, kw=kw: e.dma_start(out=o, in_=i, **kw),
                       inc=stream, k=16)

    def wait(self, eng, cname, v):
        if not v or v <= 0:
            return
        assert v <= self._c(cname).n, f"forward wait {cname} {v} > {self.ctr[cname].n}"
        sem = self.ctr[cname].sem
        self.q[eng].append(lambda e, sem=sem, v=v: e.wait_ge(sem, v))

    def waitall(self, eng, cname):
        self.wait(eng, cname, self._c(cname).n)

    def guard(self, eng, key):
        g = self.guards.get(key)
        if g is not None:
            self.wait(eng, g[0], g[1])

    def setg(self, key, cname, v):
        self.guards[key] = (cname, v)


def build_nc(limit=99):
    nc = bass.Bass("TRN2", target_bir_lowering=False)
    stack = ExitStack()

    def din(name, shape, dt):
        return nc.dram_tensor(name, list(shape), dt, kind="ExternalInput").ap()

    xin = din("xin", [S, D], F32)
    wqkv = din("wqkv", [D, 3 * D], F32)
    bqkv = din("bqkv", [1, 3 * D], F32)
    wout = din("wout", [D, D], F32)
    boutI = din("bout", [1, D], F32)
    pk_gu = din("pk_gu", [GU, D // 8], I32)
    pk_dn = din("pk_dn", [D, II // 8], I32)
    scT_gu = din("scT_gu", [GU, 16], F16)
    scT_dn = din("scT_dn", [D, 44], F16)
    ktab = din("ktab", [S, 1024], F32)
    qtab = din("qtab", [OWN, 1024], F32)
    umask = din("umask", [4, 128, 512], F32)
    bcolI = din("bcol", [128, 4 * 32], F32)
    onesr = din("onesr", [1, 128], F32)
    lmaskI = din("lmask", [128, 128], F32)

    out = nc.dram_tensor("out", [OWN, D], F32, kind="ExternalOutput").ap()

    d_ln1 = nc.dram_tensor("d_ln1", [S, D], BF16).ap()
    d_k = nc.dram_tensor("d_k", [S, D], BF16).ap()
    d_q = nc.dram_tensor("d_q", [OWN, D], BF16).ap()
    d_v = nc.dram_tensor("d_v", [S, D], BF16).ap()
    d_ctx = nc.dram_tensor("d_ctx", [OWN, D], BF16).ap()
    d_x2 = nc.dram_tensor("d_x2", [OWN, D], F32).ap()
    d_ln2 = nc.dram_tensor("d_ln2", [OWN, D], BF16).ap()
    d_wgu = nc.dram_tensor("d_wgu", [GU, D], BF16).ap()
    d_wdn = nc.dram_tensor("d_wdn", [D, II], BF16).ap()

    P = Prog(nc, stack)
    op, dma, wait, guard, setg = P.op, P.dma, P.wait, P.guard, P.setg
    opb, waitall = P.opb, P.waitall
    FINALS = {}

    class _Stop(Exception):
        pass

    def ckpt(n, pairs):
        FINALS["last"] = pairs
        if (n >= 40 and limit == n) or (n < 40 and limit <= n):
            raise _Stop()

    def sb(name, shape, dt):
        return stack.enter_context(nc.sbuf_tensor(name, list(shape), dt))

    AR1 = sb("AR1", [128, 32768], BF16)
    AR2 = sb("AR2", [128, 16384], BF16)
    AR3 = sb("AR3", [128, 16384], BF16)
    AR4 = sb("AR4", [128, 10240], F32)
    s_va = [sb(f"s_va{i}", [128, 32 * 129], BF16).ap() for i in range(2)]
    s_sil = sb("s_sil", [128, 512], F32)
    s_wd = [sb(f"s_wd{i}", [128, 512], BF16).ap() for i in range(4)]
    s_ctxs = [sb(f"s_ctxs{i}", [128, 512], BF16).ap() for i in range(2)]
    s_rc2 = sb("s_rc2", [128, 4], F32)
    s_cols = sb("s_cols", [128, 48], F32)
    s_rc = sb("s_rc", [128, 2], F32)
    c_bqg = sb("c_bqg", [1, 1024], BF16)
    c_bout = sb("c_bout", [1, D], BF16)
    c_lmask = sb("c_lmask", [128, 128], BF16)
    c_bcol = sb("c_bcol", [128, 128], F32)
    c_ones = sb("c_ones", [1, 128], BF16)
    c_scgu = sb("c_scgu", [128, 88 * 16], F32)
    c_scdn = sb("c_scdn", [128, 16 * 44], F32)

    A1, A2, A3, A4 = AR1.ap(), AR2.ap(), AR3.ap(), AR4.ap()

    psum = [stack.enter_context(nc.psum_tensor(f"ps{i}", [128, 1024], F32)).ap()
            for i in range(4)]

    def _phases():
        dma("gpsimd", "gC", c_ones[:], onesr)
        dma("gpsimd", "gC", c_bout[:], boutI)
        dma("gpsimd", "gC", c_lmask[:], lmaskI)
        dma("sync", "sC", c_bcol[:], bcolI)
        sg16 = A1.bitcast(F16)
        dma("sync", "sC", sg16[:, 0:88 * 16].rearrange("p (c g) -> p c g", g=16),
            scT_gu.rearrange("(c p) g -> p c g", p=128))
        v = dma("sync", "sC", sg16[:, 2048:2048 + 16 * 44].rearrange("p (c g) -> p c g", g=44),
                scT_dn.rearrange("(c p) g -> p c g", p=128))
        wait("vector", "sC", v)
        op("vector", lambda e: e.tensor_copy(c_scgu[:], sg16[:, 0:88 * 16]), inc="vS")
        scv = op("vector", lambda e: e.tensor_copy(c_scdn[:], sg16[:, 2048:2048 + 16 * 44]),
                 inc="vS")
        setg("AR1", "vS", scv)

        pko = [0, 704]
        mago, sgno, rawo, fixo, mg2o, wqo = 1408, 4224, 7040, 9856, 12672, 15488
        u16v = A1.bitcast(U16)
        i32v = A1.bitcast(I32)
        dq_units = [("gu", cc, 0) for cc in range(88)] + \
                   [("dn", cc, hf) for cc in range(16) for hf in range(2)]
        dq_state = {"i": 0, "spill": [], "first": True}

        def pump_dq():
            i = dq_state["i"]
            if i >= len(dq_units):
                return
            dq_state["i"] += 1
            which, cc, hf = dq_units[i]
            b = i % 2
            nu = 256 if which == "gu" else 352
            nn = nu * 8
            if dq_state["first"]:
                wait("gpsimd", "vS", scv)
                dq_state["first"] = False
            if dq_state.get("p3_resume"):
                wait("gpsimd", "tS", dq_state["p3_resume"])
                dq_state["p3_resume"] = None
            guard("gpsimd", f"dqpk{b}")
            if which == "gu":
                src = pk_gu[128 * cc:128 * (cc + 1), :]
            else:
                src = pk_dn[128 * cc:128 * (cc + 1), 352 * hf:352 * (hf + 1)]
            ldv = dma("gpsimd", f"pk{b}", i32v[:, pko[b] // 2:pko[b] // 2 + nu], src)
            wait("vector", f"pk{b}", ldv)
            guard("vector", "dqbuf")
            u = u16v[:, pko[b]:pko[b] + 2 * nu]
            for j in range(4):
                sh = 6 - 4 * j
                if sh >= 0:
                    op("vector", lambda e, j=j, u=u, sh=sh, nn=nn: e.tensor_scalar(
                        u16v[:, mago + j:mago + nn:4], u, sh, 0x1C0,
                        op0=ALU.logical_shift_left, op1=ALU.bitwise_and), inc="vS")
                else:
                    op("vector", lambda e, j=j, u=u, sh=sh, nn=nn: e.tensor_scalar(
                        u16v[:, mago + j:mago + nn:4], u, -sh, 0x1C0,
                        op0=ALU.logical_shift_right, op1=ALU.bitwise_and), inc="vS")
                pv8 = op("vector", lambda e, j=j, u=u, nn=nn: e.tensor_scalar(
                    u16v[:, sgno + j:sgno + nn:4], u, 8 << (4 * j), 12 - 4 * j,
                    op0=ALU.bitwise_and, op1=ALU.logical_shift_left), inc="vS")
            wait("vector", "vS", pv8)
            v = op("vector", lambda e, nn=nn: e.tensor_single_scalar(
                u16v[:, rawo:rawo + nn], u16v[:, mago:mago + nn], 0x3F00, ALU.add),
                inc="vS")
            wait("scalar", "vS", v)
            v = op("scalar", lambda e, nn=nn: e.activation(
                A1[:, fixo:fixo + nn], A1[:, rawo:rawo + nn], AF.Relu,
                bias=f32(1.0), scale=f32(-1.0)), inc="sS")
            wait("vector", "sS", v)
            vsub = op("vector", lambda e, nn=nn: e.tensor_tensor(
                A1[:, mg2o:mg2o + nn], A1[:, rawo:rawo + nn],
                A1[:, fixo:fixo + nn], ALU.subtract), inc="vS")
            wait("vector", "vS", vsub)
            guard("vector", f"dqwq{b}")
            wqb = wqo + 2816 * b
            vor2 = op("vector", lambda e, nn=nn, wqb=wqb: e.tensor_tensor(
                u16v[:, wqb:wqb + nn], u16v[:, mg2o:mg2o + nn],
                u16v[:, sgno:sgno + nn], ALU.bitwise_or), inc="vS")
            wait("vector", "vS", vor2)
            if which == "gu":
                sc, scb = c_scgu, 16 * cc
            else:
                sc, scb = c_scdn, 44 * cc + 22 * hf
            ngr = nn // 128
            for g in range(ngr):
                v = op("vector", lambda e, g=g, wqb=wqb, sc=sc, scb=scb: e.tensor_scalar_mul(
                    A1[:, wqb + 128 * g:wqb + 128 * (g + 1)],
                    A1[:, wqb + 128 * g:wqb + 128 * (g + 1)],
                    sc[:, scb + g:scb + g + 1]), inc="vS" if g == ngr - 1 else None)
            setg(f"dqpk{b}", "vS", v)
            setg("dqbuf", "vS", v)
            wait("gpsimd", "vS", v)
            if which == "gu":
                dstp = d_wgu[128 * cc:128 * (cc + 1), :]
            else:
                dstp = d_wdn[128 * cc:128 * (cc + 1), 2816 * hf:2816 * (hf + 1)]
            sv = dma("gpsimd", f"wq{b}", dstp, A1[:, wqb:wqb + nn])
            dq_state["spill"].append(sv)
            setg(f"dqwq{b}", f"wq{b}", sv)

        sx = [A4[:, 2048 * i:2048 * (i + 1)] for i in range(2)]
        sjunk = A4[:, 2048 * 2:2048 * 3]
        slo = [A2[:, 2048 * i:2048 * (i + 1)] for i in range(2)]

        def ln_stats(src, mcol, sdcol, sqbias=True, junk=None):
            pass

        p1_store = []
        for t in range(S // 128):
            b = t % 2
            guard("sync", f"p1x{b}")
            xl = dma("sync", f"x{b}", sx[b], xin[128 * t:128 * (t + 1), :])
            wait("vector", f"x{b}", xl)
            guard("vector", "p1cols")
            opb("vector", lambda e, b=b: e.reduce_sum(s_cols[:, 0:1], sx[b], axis=AXX),
                "vS")
            v = op("vector", lambda e: e.tensor_scalar_mul(
                s_cols[:, 1:2], s_cols[:, 0:1], -INVN), inc="vS")
            wait("scalar", "vS", v)
            opb("scalar", lambda e, b=b: e.activation(
                sjunk, sx[b], AF.Square, bias=s_cols[:, 1:2], accum_out=s_cols[:, 2:3]),
                "sS")
            v = op("scalar", lambda e: e.activation(
                s_cols[:, 3:4], s_cols[:, 2:3], AF.Sqrt,
                scale=f32(INVN), bias=f32(1e-5)), inc="sS")
            wait("vector", "sS", v)
            opb("vector", lambda e: e.reciprocal(s_cols[:, 4:5], s_cols[:, 3:4]), "vS")
            opb("vector", lambda e: e.tensor_tensor(
                s_cols[:, 5:6], s_cols[:, 1:2], s_cols[:, 4:5], ALU.mult), "vS")
            guard("vector", f"p1lo{b}")
            v = op("vector", lambda e, b=b: e.tensor_scalar(
                slo[b], sx[b], s_cols[:, 4:5], s_cols[:, 5:6],
                op0=ALU.mult, op1=ALU.add), inc="vS")
            setg(f"p1x{b}", "vS", v)
            setg("p1cols", "sS", P.ctr["sS"].n)
            wait("sync", "vS", v)
            p1_store.append(dma("sync", f"lo{b}", d_ln1[128 * t:128 * (t + 1), :], slo[b]))
            setg(f"p1lo{b}", f"lo{b}", p1_store[-1])
            pump_dq()
        ckpt(1, [("lo0", P.ctr["lo0"].n), ("lo1", P.ctr["lo1"].n)])

        lnq = A1[:, 0:16384]
        wg = A1[:, 16384:32768]
        qk = [A2[:, 1024 * i:1024 * (i + 1)] for i in range(2)]
        rts = [A4[:, 1024 * i:1024 * (i + 1)] for i in range(2)]
        rp = A4[:, 1024 * 2:1024 * 4]

        waitall("gpsimd", "wq0")
        waitall("gpsimd", "wq1")
        wait("gpsimd", "vS", P.ctr["vS"].n)
        waitall("sync", "wq0")
        waitall("sync", "wq1")
        wait("sync", "vS", P.ctr["vS"].n)
        groups = [("k", D, 4, ktab, d_k, 0), ("k", D + 1024, 4, ktab, d_k, 1024),
                  ("q", 0, 2, qtab, d_q, 0), ("q", 1024, 2, qtab, d_q, 1024),
                  ("v", 2 * D, 4, None, d_v, 0), ("v", 2 * D + 1024, 4, None, d_v, 1024)]
        for gname, colbase, nq, rtab, dst, dcol in groups:
            guard("gpsimd", "wg")
            guard("gpsimd", "bqg")
            for c in range(16):
                wgv = dma("gpsimd", "wg", wg[:, 1024 * c:1024 * (c + 1)],
                          wqkv[128 * c:128 * (c + 1), colbase:colbase + 1024])
            bqv = dma("gpsimd", "wg", c_bqg[:], bqkv[0:1, colbase:colbase + 1024])
            for qtr in range(nq):
                guard("sync", "lnq")
                if gname == "k" and colbase == D and qtr == 0:
                    waitall("sync", "lo0")
                    waitall("sync", "lo1")
                for c in range(16):
                    lnv = dma("sync", "slnq", lnq[:, 1024 * c:1024 * (c + 1)],
                              d_ln1[1024 * qtr:1024 * (qtr + 1), 128 * c:128 * (c + 1)],
                              transpose=True)
                for tt in range(8):
                    at = 8 * qtr + tt
                    bb = at % 2
                    if rtab is not None:
                        guard("sync", f"rt{bb}")
                        rtv = dma("sync", f"rt{bb}", rts[bb], rtab[128 * at:128 * (at + 1), :])
                    guard_qk_done = False
                    for nh2 in range(2):
                        pidx = (2 * at + nh2) % 4
                        ps = psum[pidx][:, 0:512]
                        guard("tensor", f"ps{pidx}")
                        if tt == 0 and nh2 == 0:
                            wait("tensor", "wg", bqv)
                            wait("tensor", "slnq", lnv)
                            if gname == "k" and colbase == D and qtr == 0 and tt == 0:
                                waitall("tensor", "gC")
                        for c in range(16):
                            op("tensor", lambda e, ps=ps, c=c, tt=tt, nh2=nh2: e.matmul(
                                ps, lnq[:, 1024 * c + 128 * tt:1024 * c + 128 * (tt + 1)],
                                wg[:, 1024 * c + 512 * nh2:1024 * c + 512 * (nh2 + 1)],
                                start=(c == 0), stop=False))
                        pe_v = op("tensor", lambda e, ps=ps, nh2=nh2: e.matmul(
                            ps, c_ones[:], c_bqg[0:1, 512 * nh2:512 * (nh2 + 1)],
                            start=False, stop=True), inc="tS")
                        if rtab is None:
                            wait("scalar", "tS", pe_v)
                            if not guard_qk_done:
                                guard("scalar", f"qk{bb}")
                            ev = op("scalar", lambda e, ps=ps, bb=bb, nh2=nh2: e.activation(
                                qk[bb][:, 512 * nh2:512 * (nh2 + 1)], ps, AF.Copy),
                                inc="sS")
                            setg(f"ps{pidx}", "sS", ev)
                            ev_c = "sS"
                        else:
                            wait("vector", "tS", pe_v)
                            if nh2 == 0:
                                wait("vector", f"rt{bb}", rtv)
                            if not guard_qk_done:
                                guard("vector", f"qk{bb}")
                            ps3 = ps.rearrange("p (h x d) -> p h x d", h=4, x=2)
                            sc3 = rts[bb][:, 512:1024].rearrange(
                                "p (h x d) -> p h x d", h=4, x=2)
                            rp3 = rp[:, 512:1024].rearrange(
                                "p (h x d) -> p h x d", h=4, x=2)
                            op("vector", lambda e, ps=ps, bb=bb: e.tensor_tensor(
                                rp[:, 0:512], ps, rts[bb][:, 0:512], ALU.mult), inc="vS")
                            op("vector", lambda e, ps3=ps3, sc3=sc3, rp3=rp3: e.tensor_tensor(
                                rp3[:, :, 0, :], ps3[:, :, 1, :], sc3[:, :, 0, :], ALU.mult),
                                inc="vS")
                            v3b = op("vector", lambda e, ps3=ps3, sc3=sc3, rp3=rp3: e.tensor_tensor(
                                rp3[:, :, 1, :], ps3[:, :, 0, :], sc3[:, :, 1, :], ALU.mult),
                                inc="vS")
                            wait("vector", "vS", v3b)
                            ev = op("vector", lambda e, bb=bb, nh2=nh2: e.tensor_tensor(
                                qk[bb][:, 512 * nh2:512 * (nh2 + 1)],
                                rp[:, 0:512], rp[:, 512:1024], ALU.add), inc="vS")
                            wait("vector", "vS", ev)
                            setg(f"ps{pidx}", "vS", ev)
                            ev_c = "vS"
                        guard_qk_done = True
                    if rtab is not None:
                        setg(f"rt{bb}", "vS", ev)
                    wait("sync", ev_c, ev)
                    stv = dma("sync", f"qs{bb}", dst[128 * at:128 * (at + 1), dcol:dcol + 1024], qk[bb])
                    setg(f"qk{bb}", f"qs{bb}", stv)
                setg("lnq", "tS", P.ctr["tS"].n)
            setg("wg", "tS", P.ctr["tS"].n)
            setg("bqg", "tS", P.ctr["tS"].n)
        p2_tS = P.ctr["tS"].n
        dq_state["p3_resume"] = p2_tS
        ckpt(2, [("qs0", P.ctr["qs0"].n), ("qs1", P.ctr["qs1"].n)])
        setg("AR1", "tS", p2_tS)

        kh = [A3[:, 4096 * i:4096 * (i + 1)] for i in range(2)]
        qh = [A3[:, 8192 + 2048 * i:8192 + 2048 * (i + 1)] for i in range(2)]
        um = A3[:, 12288:14336]
        pt = A2

        mz = None
        for i in range(2):
            mz = op("vector", lambda e, i=i: e.memset(
                s_va[i].rearrange("p (c v) -> p c v", c=32)[:, :, 128:129], 1.0),
                inc="vS")
        wait("gpsimd", "tS", p2_tS)
        umv = dma("gpsimd", "um", um.rearrange("p (a q) -> p a q", a=4),
                  umask.rearrange("a p q -> p a q"))
        waitall("scalar", "qs0")
        waitall("scalar", "qs1")
        waitall("scalar", "lo0")
        waitall("scalar", "lo1")

        for hh in range(16):
            b = hh % 2
            guard("sync", f"kh{b}")
            dma("sync", f"kq{b}", kh[b], d_k[:, 128 * hh:128 * (hh + 1)], transpose=True)
            kqv = dma("sync", f"kq{b}", qh[b], d_q[:, 128 * hh:128 * (hh + 1)], transpose=True)
            guard("gpsimd", f"kh{b}")
            vav = dma("gpsimd", f"va{b}", s_va[b].rearrange("p (c v) -> p c v", c=32)[:, :, 0:128],
                      d_v.rearrange("(c p) v -> p c v", p=128)[:, :, 128 * hh:128 * (hh + 1)])
            wait("tensor", f"kq{b}", kqv)
            wait("tensor", f"va{b}", vav)
            if hh == 0:
                wait("tensor", "vS", mz)
                wait("tensor", "um", umv)
            if hh == 0:
                wait("tensor", "vS", P.ctr["vS"].n)
                wait("tensor", "sS", P.ctr["sS"].n)
            for qc in range(4):
                ktl = _ktlist(qc)
                exp_last = None
                for idx, kt in enumerate(ktl):
                    pidx = idx % 4
                    ps = psum[pidx % 2][:, 512 * (pidx // 2):512 * (pidx // 2) + 512]
                    guard("tensor", f"sc{pidx}")
                    diag = 4 * qc <= kt < 4 * qc + 4
                    if diag:
                        op("tensor", lambda e, ps=ps, b=b, kt=kt, qc=qc: e.matmul(
                            ps, kh[b][:, 128 * kt:128 * (kt + 1)],
                            qh[b][:, 512 * qc:512 * (qc + 1)], start=True, stop=False))
                        so = kt - 4 * qc
                        pe_v = op("tensor", lambda e, ps=ps, so=so: e.matmul(
                            ps, c_lmask[:], um[:, 512 * so:512 * (so + 1)],
                            start=False, stop=True), inc="tS")
                    else:
                        pe_v = op("tensor", lambda e, ps=ps, b=b, kt=kt, qc=qc: e.matmul(
                            ps, kh[b][:, 128 * kt:128 * (kt + 1)],
                            qh[b][:, 512 * qc:512 * (qc + 1)], start=True, stop=True),
                            inc="tS")
                    wait("scalar", "tS", pe_v)
                    if idx == 0:
                        guard("scalar", "pt")
                    exp_last = op("scalar", lambda e, ps=ps, idx=idx, qc=qc: e.activation(
                        pt[:, 512 * idx:512 * (idx + 1)], ps, AF.Exp,
                        bias=c_bcol[:, 32 * qc + idx:32 * qc + idx + 1]), inc="sS")
                    setg(f"sc{pidx}", "sS", exp_last)
                wait("tensor", "sS", exp_last)
                pv_vs = []
                for qt in range(4):
                    pv = psum[2 + qt % 2][:, 512 * (qt // 2):512 * (qt // 2) + 129]
                    guard("tensor", f"pv{qt}")
                    pv_v = None
                    for idx, kt in enumerate(ktl):
                        last = (idx == len(ktl) - 1)
                        fn = (lambda e, pv=pv, idx=idx, qt=qt, kt=kt, b=b, last=last:
                              e.matmul(
                            pv, pt[:, 512 * idx + 128 * qt:512 * idx + 128 * (qt + 1)],
                            s_va[b][:, 129 * kt:129 * (kt + 1)],
                            start=(idx == 0), stop=last))
                        pv_v = op("tensor", fn, inc="tS" if last else None)
                    pv_vs.append(pv_v)
                cb = qc % 2
                guard("vector", f"cx{cb}")
                rc_v = None
                for qt in range(4):
                    pv = psum[2 + qt % 2][:, 512 * (qt // 2):512 * (qt // 2) + 129]
                    wait("vector", "tS", pv_vs[qt])
                    rc_v = op("vector", lambda e, pv=pv, qt=qt: e.reciprocal(
                        s_rc2[:, qt:qt + 1], pv[:, 128:129]),
                        inc="vS" if qt == 3 else None)
                wait("vector", "vS", rc_v)
                ts_v = None
                for qt in range(4):
                    pv = psum[2 + qt % 2][:, 512 * (qt // 2):512 * (qt // 2) + 129]
                    ts_v = op("vector", lambda e, pv=pv, qt=qt, cb=cb: e.tensor_scalar_mul(
                        s_ctxs[cb][:, 128 * qt:128 * (qt + 1)], pv[:, 0:128],
                        s_rc2[:, qt:qt + 1]), inc="vS")
                    setg(f"pv{qt}", "vS", ts_v)
                wait("vector", "vS", ts_v)
                wait("sync", "vS", ts_v)
                r0 = 512 * qc
                cxv = dma("sync", f"cx{cb}",
                          d_ctx[r0:r0 + 512, 128 * hh:128 * (hh + 1)].rearrange(
                              "(qt p) c -> p qt c", p=128),
                          s_ctxs[cb].rearrange("p (qt c) -> p qt c", qt=4))
                setg(f"cx{cb}", f"cx{cb}", cxv)
                setg("pt", "tS", P.ctr["tS"].n)
            setg(f"kh{b}", "tS", P.ctr["tS"].n)
            for _ in range(8):
                pump_dq()
        while dq_state["i"] < len(dq_units):
            pump_dq()
        p5_vS = P.ctr["vS"].n
        ckpt(3, [("wq0", P.ctr["wq0"].n), ("wq1", P.ctr["wq1"].n),
                 ("cx0", P.ctr["cx0"].n), ("cx1", P.ctr["cx1"].n)])

        cfm = A1[:, 0:16384]
        wo = A1[:, 16384:32768]
        aost = A2
        sx4 = [A4[:, 2048 * i:2048 * (i + 1)] for i in range(2)]
        sj4 = A4[:, 2048 * 2:2048 * 3]
        x2t = [A4[:, 2048 * (3 + i):2048 * (4 + i)] for i in range(2)]

        for th in range(2):
            guard("sync", "AR1cfm")
            if th == 0:
                waitall("sync", "wq0")
                waitall("sync", "wq1")
                wait("sync", "vS", p5_vS)
                waitall("sync", "cx0")
                waitall("sync", "cx1")
            for c in range(16):
                cfmv = dma("sync", "cfm", cfm[:, 1024 * c:1024 * (c + 1)],
                           d_ctx[1024 * th:1024 * (th + 1), 128 * c:128 * (c + 1)],
                           transpose=True)
            for oh in range(2):
                if th == 1 and oh == 0:
                    waitall("scalar", "xl20")
                    waitall("scalar", "xl21")
                    wait("scalar", "vS", P.ctr["vS"].n)
                guard("gpsimd", "wo")
                if th == 0 and oh == 0:
                    waitall("gpsimd", "wq0")
                    waitall("gpsimd", "wq1")
                    wait("gpsimd", "vS", p5_vS)
                for c in range(16):
                    wov = dma("gpsimd", "wo", wo[:, 1024 * c:1024 * (c + 1)],
                              wout[128 * c:128 * (c + 1), 1024 * oh:1024 * (oh + 1)])
                for t8 in range(8):
                    for oc2 in range(2):
                        pidx = (2 * t8 + oc2) % 4
                        ps = psum[pidx][:, 0:512]
                        guard("tensor", f"ps{pidx}")
                        if t8 == 0 and oc2 == 0:
                            if th == 0 and oh == 0:
                                wait("tensor", "vS", P.ctr["vS"].n)
                                wait("tensor", "sS", P.ctr["sS"].n)
                            wait("tensor", "cfm", cfmv)
                            wait("tensor", "wo", wov)
                        for c in range(16):
                            op("tensor", lambda e, ps=ps, c=c, t8=t8, oh=oh, oc2=oc2: e.matmul(
                                ps, cfm[:, 1024 * c + 128 * t8:1024 * c + 128 * (t8 + 1)],
                                wo[:, 1024 * c + 512 * oc2:1024 * c + 512 * (oc2 + 1)],
                                start=(c == 0), stop=False))
                        pe_v = op("tensor", lambda e, ps=ps, oh=oh, oc2=oc2: e.matmul(
                            ps, c_ones[:], c_bout[0:1, 1024 * oh + 512 * oc2:
                                                  1024 * oh + 512 * (oc2 + 1)],
                            start=False, stop=True), inc="tS")
                        wait("scalar", "tS", pe_v)
                        if th == 0 and oh == 0 and t8 == 0 and oc2 == 0:
                            guard("scalar", "pt")
                        ev = op("scalar", lambda e, ps=ps, t8=t8, oh=oh, oc2=oc2: e.activation(
                            aost[:, 2048 * t8 + 1024 * oh + 512 * oc2:
                                 2048 * t8 + 1024 * oh + 512 * (oc2 + 1)], ps, AF.Copy),
                            inc="sS")
                        setg(f"ps{pidx}", "sS", ev)
                setg("wo", "tS", P.ctr["tS"].n)
            aost_ready = ev
            wait("vector", "sS", aost_ready)
            for t8 in range(8):
                t = 8 * th + t8
                b2 = t8 % 2
                ao = aost[:, 2048 * t8:2048 * (t8 + 1)]
                guard("gpsimd", f"p4x{b2}")
                xl = dma("gpsimd", f"x4{b2}", sx4[b2], xin[128 * t:128 * (t + 1), :])
                guard("vector", "p4cols")
                opb("vector", lambda e, ao=ao: e.reduce_sum(s_cols[:, 0:1], ao, axis=AXX),
                    "vS")
                v = op("vector", lambda e: e.tensor_scalar_mul(
                    s_cols[:, 1:2], s_cols[:, 0:1], -INVN), inc="vS")
                wait("scalar", "vS", v)
                opb("scalar", lambda e, ao=ao: e.activation(
                    sj4, ao, AF.Square, bias=s_cols[:, 1:2], accum_out=s_cols[:, 2:3]),
                    "sS")
                v = op("scalar", lambda e: e.activation(
                    s_cols[:, 3:4], s_cols[:, 2:3], AF.Sqrt,
                    scale=f32(INVN), bias=f32(1e-5)), inc="sS")
                wait("vector", "sS", v)
                opb("vector", lambda e: e.reciprocal(s_cols[:, 4:5], s_cols[:, 3:4]), "vS")
                opb("vector", lambda e: e.tensor_tensor(
                    s_cols[:, 5:6], s_cols[:, 1:2], s_cols[:, 4:5], ALU.mult), "vS")
                v = op("vector", lambda e, ao=ao: e.tensor_scalar(
                    sj4, ao, s_cols[:, 4:5], s_cols[:, 5:6], op0=ALU.mult, op1=ALU.add),
                    inc="vS")
                wait("vector", "vS", v)
                wait("vector", f"x4{b2}", xl)
                guard("vector", f"p4x2{b2}")
                v = op("vector", lambda e, b2=b2: e.tensor_tensor(
                    x2t[b2], sx4[b2], sj4, ALU.add), inc="vS")
                wait("vector", "vS", v)
                v = opb("vector", lambda e, b2=b2: e.reduce_sum(
                    s_cols[:, 6:7], x2t[b2], axis=AXX), "vS")
                setg(f"p4x{b2}", "vS", v)
                v = op("vector", lambda e: e.tensor_scalar_mul(
                    s_cols[:, 7:8], s_cols[:, 6:7], -INVN), inc="vS")
                wait("scalar", "vS", v)
                opb("scalar", lambda e, b2=b2: e.activation(
                    sj4, x2t[b2], AF.Square, bias=s_cols[:, 7:8], accum_out=s_cols[:, 8:9]),
                    "sS")
                v = op("scalar", lambda e: e.activation(
                    s_cols[:, 9:10], s_cols[:, 8:9], AF.Sqrt,
                    scale=f32(INVN), bias=f32(1e-5)), inc="sS")
                wait("vector", "sS", v)
                opb("vector", lambda e: e.reciprocal(s_cols[:, 10:11], s_cols[:, 9:10]),
                    "vS")
                opb("vector", lambda e: e.tensor_tensor(
                    s_cols[:, 11:12], s_cols[:, 7:8], s_cols[:, 10:11], ALU.mult), "vS")
                v = op("vector", lambda e, ao=ao, b2=b2: e.tensor_scalar(
                    ao, x2t[b2], s_cols[:, 10:11], s_cols[:, 11:12],
                    op0=ALU.mult, op1=ALU.add), inc="vS")
                setg("p4cols", "vS", v)
                wait("gpsimd", "vS", v)
                x2v = dma("gpsimd", f"xl2{b2}", d_x2[128 * t:128 * (t + 1), :], x2t[b2])
                l2v = dma("gpsimd", f"xl2{b2}", d_ln2[128 * t:128 * (t + 1), :], ao)
                setg(f"p4x2{b2}", f"xl2{b2}", l2v)
            setg("AR1cfm", "tS", P.ctr["tS"].n)
        ckpt(4, [("xl20", P.ctr["xl20"].n), ("xl21", P.ctr["xl21"].n)])

        l2fm = A1[:, 0:8192]
        sh_ = A1[:, 8192:8192 + 44 * 512]
        gs = [A2[:, 8192 * i:8192 * (i + 1)] for i in range(2)]
        us = [A3[:, 8192 * i:8192 * (i + 1)] for i in range(2)]
        sy = A3[:, 0:8192]
        x6 = [A4[:, 2048 * i:2048 * (i + 1)] for i in range(2)]
        j6 = A4[:, 2048 * 2:2048 * 3]
        ot = [A4[:, 2048 * (3 + i):2048 * (4 + i)] for i in range(2)]
        out_st = []

        for T in range(4):
            guard("sync", "l2fm")
            if T == 0:
                waitall("sync", "xl20")
                waitall("sync", "xl21")
                wait("sync", "tS", P.ctr["tS"].n)
            for c in range(16):
                l2v = dma("sync", "l2fm", l2fm[:, 512 * c:512 * (c + 1)],
                          d_ln2[512 * T:512 * (T + 1), 128 * c:128 * (c + 1)],
                          transpose=True)
            for ib in range(11):
                sbb = ib % 2
                guard("sync", f"gs{sbb}")
                guard("scalar", f"gs{sbb}")
                if ib < 2:
                    guard("sync", "sy")
                    guard("scalar", "sy")
                gv2 = None
                _dual = False
                for c in range(16):
                    q1 = "sync" if (c % 2 == 0 or not _dual) else "scalar"
                    st1 = f"gsl{sbb}" if (c % 2 == 0 or not _dual) else f"gsg{sbb}"
                    gv = dma(q1, st1, gs[sbb][:, 512 * c:512 * (c + 1)],
                             d_wgu[512 * ib:512 * (ib + 1), 128 * c:128 * (c + 1)],
                             transpose=True)
                    if c % 2 == 0:
                        gv2 = dma(q1, st1, us[sbb][:, 512 * c:512 * (c + 1)],
                                  d_wgu[II + 512 * ib:II + 512 * (ib + 1),
                                        128 * c:128 * (c + 1)], transpose=True)
                    else:
                        gv = dma(q1, st1, us[sbb][:, 512 * c:512 * (c + 1)],
                                 d_wgu[II + 512 * ib:II + 512 * (ib + 1),
                                       128 * c:128 * (c + 1)], transpose=True)
                for i4 in range(4):
                    i = 4 * ib + i4
                    gp = psum[i4 % 2][:, 0:512]
                    up = psum[2 + i4 % 2][:, 0:512]
                    guard("tensor", f"ps{i4 % 2}")
                    guard("tensor", f"ps{2 + i4 % 2}")
                    if i4 == 0:
                        waitall("tensor", f"gsl{sbb}")
                        waitall("tensor", f"gsg{sbb}")
                        if ib == 0:
                            wait("tensor", "l2fm", l2v)
                            if T == 0:
                                waitall("tensor", "wq0")
                                waitall("tensor", "wq1")
                    for c in range(16):
                        op("tensor", lambda e, gp=gp, c=c, i4=i4, sbb=sbb: e.matmul(
                            gp, gs[sbb][:, 512 * c + 128 * i4:512 * c + 128 * (i4 + 1)],
                            l2fm[:, 512 * c:512 * (c + 1)], start=(c == 0), stop=(c == 15)))
                    pe_v = None
                    for c in range(16):
                        pe_v = op("tensor", lambda e, up=up, c=c, i4=i4, sbb=sbb: e.matmul(
                            up, us[sbb][:, 512 * c + 128 * i4:512 * c + 128 * (i4 + 1)],
                            l2fm[:, 512 * c:512 * (c + 1)], start=(c == 0), stop=(c == 15)),
                            inc="tS" if c == 15 else None)
                    wait("scalar", "tS", pe_v)
                    guard("scalar", "sil")
                    sv = op("scalar", lambda e, gp=gp: e.activation(s_sil[:], gp, AF.Silu),
                            inc="sS")
                    setg(f"ps{i4 % 2}", "sS", sv)
                    wait("vector", "sS", sv)
                    guard("vector", "shh")
                    v = op("vector", lambda e, up=up, i=i: e.tensor_tensor(
                        sh_[:, 512 * i:512 * (i + 1)], s_sil[:], up, ALU.mult), inc="vS")
                    setg(f"ps{2 + i4 % 2}", "vS", v)
                    setg("sil", "vS", v)
                setg(f"gs{sbb}", "tS", P.ctr["tS"].n)
                if T == 0 and ib == 0:
                    ckpt(41, [("vS", v)])
            gate_tS = P.ctr["tS"].n
            if T == 0:
                ckpt(42, [("vS", v)])
            stg = [x6[1], j6]
            for fq in range(4):
                for qr in range(4):
                    sl = qr % 2
                    slab = A2[:, 5632 * sl:5632 * sl + 5632]
                    guard("sync", f"dsl{sl}")
                    guard("scalar", f"dsl{sl}")
                    if fq == 0 and qr < 2:
                        guard("sync", "sy")
                        guard("scalar", "sy")
                        guard("sync", "gs0")
                        guard("sync", "gs1")
                        guard("scalar", "gs0")
                        guard("scalar", "gs1")
                    g1 = g2 = 0
                    for ii in range(11):
                        i = 11 * qr + ii
                        _dual = False
                        q1 = "sync" if (ii % 2 == 0 or not _dual) else "scalar"
                        st1 = f"dwl{sl}" if (ii % 2 == 0 or not _dual) else f"dwg{sl}"
                        gv = dma(q1, st1, slab[:, 512 * ii:512 * (ii + 1)],
                                 d_wdn[512 * fq:512 * (fq + 1), 128 * i:128 * (i + 1)],
                                 transpose=True)
                        if ii % 2 == 0:
                            g1 = gv
                        else:
                            g2 = gv
                    waitall("tensor", f"dwl{sl}")
                    waitall("tensor", f"dwg{sl}")
                    for sub in range(4):
                        guard("tensor", f"ps{sub}")
                    for ii in range(11):
                        i = 11 * qr + ii
                        for sub in range(4):
                            wd_v = op("tensor", lambda e, sub=sub, i=i, ii=ii, slab=slab: e.matmul(
                                psum[sub][:, 0:512],
                                sh_[:, 512 * i + 128 * sub:512 * i + 128 * (sub + 1)],
                                slab[:, 512 * ii:512 * (ii + 1)],
                                start=(ii == 0), stop=(ii == 10)),
                                inc="tS" if (ii == 10 and sub == 3) else None)
                    setg(f"dsl{sl}", "tS", wd_v)
                    wait("vector", "tS", wd_v)
                    for sub in range(4):
                        ps = psum[sub][:, 0:512]
                        if qr == 3:
                            rv = opb("vector", lambda e, ps=ps, fq=fq, sub=sub: e.reduce_sum(
                                s_cols[:, 16 + 4 * fq + sub:17 + 4 * fq + sub], ps,
                                axis=AXX), "vS")
                        if qr == 0:
                            v = op("vector", lambda e, ps=ps, sub=sub: e.tensor_copy(
                                stg[0][:, 512 * sub:512 * (sub + 1)], ps), inc="vS")
                        elif qr < 3:
                            v = op("vector", lambda e, ps=ps, sub=sub, qr=qr: e.tensor_tensor(
                                stg[qr % 2][:, 512 * sub:512 * (sub + 1)],
                                stg[1 - qr % 2][:, 512 * sub:512 * (sub + 1)],
                                ps, ALU.add), inc="vS")
                        else:
                            v = op("vector", lambda e, ps=ps, sub=sub, fq=fq: e.tensor_tensor(
                                sy[:, 2048 * sub + 512 * fq:2048 * sub + 512 * (fq + 1)],
                                stg[0][:, 512 * sub:512 * (sub + 1)],
                                ps, ALU.add), inc="vS")
                        wait("vector", "vS", v)
                        setg(f"ps{sub}", "vS", v)
                    if T == 0 and fq == 0 and qr == 0:
                        pass
                for sub in range(4):
                    ysl = sy[:, 2048 * sub + 512 * fq:2048 * sub + 512 * (fq + 1)]
                    opb("vector", lambda e, ysl=ysl, fq=fq, sub=sub: e.reduce_sum(
                        s_cols[:, 16 + 4 * fq + sub:17 + 4 * fq + sub], ysl,
                        axis=AXX), "vS")
                    guard("scalar", "sil")
                    wait("scalar", "vS", P.ctr["vS"].n)
                    sqv = op("scalar", lambda e, ysl=ysl, fq=fq, sub=sub: e.activation(
                        s_sil[:], ysl, AF.Square,
                        accum_out=s_cols[:, 32 + 4 * fq + sub:33 + 4 * fq + sub]),
                        inc="sS")
                    setg("sil", "sS", sqv)
                if T == 0 and fq == 0:
                    ckpt(43, [("vS", P.ctr["vS"].n)])
            setg("gs0", "tS", P.ctr["tS"].n)
            setg("gs1", "tS", P.ctr["tS"].n)
            for sub in range(4):
                t = 4 * T + sub
                b6 = 0
                guard("gpsimd", "p6x0")
                xl = dma("gpsimd", "x60", x6[0], d_x2[128 * t:128 * (t + 1), :])
                opb("vector", lambda e, sub=sub: e.tensor_tensor(
                    s_cols[:, 1:2], s_cols[:, 16 + sub:17 + sub],
                    s_cols[:, 20 + sub:21 + sub], ALU.add), "vS")
                opb("vector", lambda e, sub=sub: e.tensor_tensor(
                    s_cols[:, 2:3], s_cols[:, 24 + sub:25 + sub],
                    s_cols[:, 28 + sub:29 + sub], ALU.add), "vS")
                opb("vector", lambda e: e.tensor_tensor(
                    s_cols[:, 3:4], s_cols[:, 1:2], s_cols[:, 2:3], ALU.add), "vS")
                wait("vector", "sS", P.ctr["sS"].n)
                opb("vector", lambda e, sub=sub: e.tensor_tensor(
                    s_cols[:, 1:2], s_cols[:, 32 + sub:33 + sub],
                    s_cols[:, 36 + sub:37 + sub], ALU.add), "vS")
                opb("vector", lambda e, sub=sub: e.tensor_tensor(
                    s_cols[:, 2:3], s_cols[:, 40 + sub:41 + sub],
                    s_cols[:, 44 + sub:45 + sub], ALU.add), "vS")
                opb("vector", lambda e: e.tensor_tensor(
                    s_cols[:, 5:6], s_cols[:, 1:2], s_cols[:, 2:3], ALU.add), "vS")
                opb("vector", lambda e: e.tensor_scalar_mul(
                    s_cols[:, 8:9], s_cols[:, 3:4], INVN), "vS")
                opb("vector", lambda e: e.tensor_scalar_mul(
                    s_cols[:, 9:10], s_cols[:, 5:6], INVN), "vS")
                opb("vector", lambda e: e.tensor_tensor(
                    s_cols[:, 10:11], s_cols[:, 8:9], s_cols[:, 8:9], ALU.mult), "vS")
                opb("vector", lambda e: e.tensor_tensor(
                    s_cols[:, 11:12],
                    s_cols[:, 9:10], s_cols[:, 10:11], ALU.subtract), "vS")
                opb("vector", lambda e: e.tensor_scalar(
                    s_cols[:, 12:13], s_cols[:, 9:10], 1e-5, 1e-11,
                    op0=ALU.mult, op1=ALU.add), "vS")
                v = op("vector", lambda e: e.tensor_tensor(
                    s_cols[:, 13:14], s_cols[:, 11:12], s_cols[:, 12:13], ALU.add),
                    inc="vS")
                wait("scalar", "vS", v)
                v = op("scalar", lambda e: e.activation(
                    s_cols[:, 14:15], s_cols[:, 13:14], AF.Sqrt), inc="sS")
                wait("vector", "sS", v)
                opb("vector", lambda e: e.reciprocal(s_cols[:, 15:16], s_cols[:, 14:15]),
                    "vS")
                opb("vector", lambda e: e.tensor_tensor(
                    s_rc[:, 1:2], s_cols[:, 8:9], s_cols[:, 15:16], ALU.mult), "vS")
                opb("vector", lambda e: e.tensor_scalar_mul(
                    s_cols[:, 12:13], s_rc[:, 1:2], -1.0), "vS")
                wait("vector", "x60", xl)
                ob = sub % 2
                guard("vector", f"p6o{ob}")
                v = op("vector", lambda e, sub=sub: e.tensor_scalar(
                    j6, sy[:, 2048 * sub:2048 * (sub + 1)],
                    s_cols[:, 15:16], s_cols[:, 12:13], op0=ALU.mult, op1=ALU.add),
                    inc="vS")
                wait("vector", "vS", v)
                v = op("vector", lambda e, ob=ob: e.tensor_tensor(
                    ot[ob], x6[0], j6, ALU.add), inc="vS")
                setg("p6x0", "vS", v)
                wait("gpsimd", "vS", v)
                out_st.append(dma("gpsimd", f"ob{ob}", out[128 * t:128 * (t + 1), :], ot[ob]))
                setg(f"p6o{ob}", f"ob{ob}", out_st[-1])
            setg("l2fm", "tS", P.ctr["tS"].n)
            setg("sy", "vS", P.ctr["vS"].n)
            ckpt(5 + T, [("ob0", P.ctr["ob0"].n), ("ob1", P.ctr["ob1"].n)])
        FINALS["full"] = [("ob0", P.ctr["ob0"].n), ("ob1", P.ctr["ob1"].n)]


    try:
        _phases()
    except _Stop:
        pass

    fpairs = FINALS.get("full", FINALS.get("last"))
    if os.environ.get("MK_DUMP"):
        dmp = {"ln1": (d_ln1, [S, D], BF16), "k": (d_k, [S, D], BF16),
               "q": (d_q, [OWN, D], BF16), "v": (d_v, [S, D], BF16),
               "ctx": (d_ctx, [OWN, D], BF16), "x2": (d_x2, [OWN, D], F32),
               "ln2": (d_ln2, [OWN, D], BF16), "wgu": (d_wgu, [GU, D], BF16),
               "wdn": (d_wdn, [D, II], BF16)}
        for cn, vv in fpairs:
            wait("sync", cn, vv)
        for nm in os.environ["MK_DUMP"].split(","):
            ap_, shp, dt = dmp[nm]
            o = nc.dram_tensor("dump_" + nm, shp, dt, kind="ExternalOutput").ap()
            dma("sync", "dmp", o, ap_)
        fpairs = [("dmp", P.ctr["dmp"].n)]
    _fsems = [(P.ctr[cn].sem, vv) for cn, vv in fpairs]
    with nc.Block() as block:
        @block.sync
        def _(e):
            for fn in P.q["sync"]:
                fn(e)
            for sem_, vv_ in _fsems:
                e.wait_ge(sem_, vv_)

        @block.gpsimd
        def _(e):
            for fn in P.q["gpsimd"]:
                fn(e)

        @block.tensor
        def _(e):
            for fn in P.q["tensor"]:
                fn(e)

        @block.vector
        def _(e):
            for fn in P.q["vector"]:
                fn(e)

        @block.scalar
        def _(e):
            for fn in P.q["scalar"]:
                fn(e)

    stack.close()
    return nc


def _host_consts(parity):
    hd = 128
    inv_freq = 1.0 / (10000.0 ** (np.arange(0, hd, 2, dtype=np.float64) / hd))
    t = np.arange(S, dtype=np.float64)
    fr = np.outer(t, inv_freq)
    cos = np.cos(fr).astype(np.float32)
    sin = np.sin(fr).astype(np.float32)
    sc = np.float32(1.0 / np.sqrt(hd))
    ccat = np.concatenate([cos, cos], 1)
    scat = np.concatenate([-sin, sin], 1)
    ccat4 = np.tile(ccat, (1, 4))
    scat4 = np.tile(scat, (1, 4))
    own = np.concatenate([np.arange(1024 * qc + 512 * parity,
                                     1024 * qc + 512 * parity + 512)
                          for qc in range(4)])
    other = np.concatenate([np.arange(1024 * j + 512 * (1 - parity),
                                      1024 * j + 512 * (1 - parity) + 512)
                           for j in range(4)])
    perm = np.concatenate([own, other])
    ktab = np.concatenate([ccat4[perm] * sc, scat4[perm] * sc], 1)
    qtab = np.concatenate([ccat4[own], scat4[own]], 1)
    um = np.zeros((4, 128, 512), np.float32)
    kp = np.arange(128)[:, None]
    qj = np.arange(512)[None, :]
    for s_ in range(4):
        um[s_] = (s_ * 128 + kp > qj).astype(np.float32)
    bcol = np.full((4 * 32, 128), -10030.0, np.float32)
    for qc in range(4):
        for idx, kt in enumerate(_ktlist(qc)):
            if parity == 0 and kt >= 16 + 4 * qc:
                bcol[32 * qc + idx] = -10030.0
            else:
                bcol[32 * qc + idx] = -30.0
    bcol = np.ascontiguousarray(bcol.T)
    return dict(ktab=np.ascontiguousarray(ktab), qtab=np.ascontiguousarray(qtab),
                umask=um, bcol=bcol, perm=perm)


def kernel(L0_attn_x, L0_attn_Wqkv, L0_attn_bqkv, L0_attn_Wout, L0_attn_bout,
           L0_ffn_gate_up_packed, L0_ffn_gate_up_scales,
           L0_ffn_down_packed, L0_ffn_down_scales):
    x = np.ascontiguousarray(np.asarray(L0_attn_x, dtype=np.float32))
    wqkv = np.ascontiguousarray(np.asarray(L0_attn_Wqkv, dtype=np.float32))
    bqkv = np.ascontiguousarray(np.asarray(L0_attn_bqkv, dtype=np.float32)[None, :])
    wout = np.ascontiguousarray(np.asarray(L0_attn_Wout, dtype=np.float32))
    bout = np.ascontiguousarray(np.asarray(L0_attn_bout, dtype=np.float32)[None, :])
    pk_gu = np.ascontiguousarray(np.asarray(L0_ffn_gate_up_packed, dtype=np.int32))
    pk_dn = np.ascontiguousarray(np.asarray(L0_ffn_down_packed, dtype=np.int32))
    scT_gu = np.ascontiguousarray(np.asarray(L0_ffn_gate_up_scales).T)
    scT_dn = np.ascontiguousarray(np.asarray(L0_ffn_down_scales).T)
    onesr = np.ones((1, 128), np.float32)
    lmask = (-10000.0 * np.eye(128)).astype(np.float32)

    consts = [_host_consts(0), _host_consts(1)]
    nc = build_nc()
    in_maps = []
    for c in range(NCORES):
        B, par = c // 2, c % 2
        cst = consts[par]
        xp = np.ascontiguousarray(x[B][cst["perm"]])
        in_maps.append(dict(
            xin=xp, wqkv=wqkv, bqkv=bqkv, wout=wout, bout=bout,
            pk_gu=pk_gu, pk_dn=pk_dn, scT_gu=scT_gu, scT_dn=scT_dn,
            ktab=cst["ktab"], qtab=cst["qtab"],
            umask=np.ascontiguousarray(cst["umask"]),
            bcol=np.ascontiguousarray(cst["bcol"]),
            onesr=onesr, lmask=lmask,
        ))
    res = run_bass_kernel_spmd(nc, in_maps, core_ids=list(range(NCORES)))
    outa = np.empty((4, S, D), np.float32)
    for c in range(NCORES):
        B, par = c // 2, c % 2
        outa[B][consts[par]["perm"][:OWN]] = res.results[c]["out"]
    return outa


